# version 08
# baseline (speedup 1.0000x reference)
import numpy as np
import ml_dtypes
import concourse.bass as bass
import concourse.mybir as mybir
from concourse.bass_core_rust import IndirectOffsetOnAxis
from contextlib import ExitStack

F32 = mybir.dt.float32
BF16 = mybir.dt.bfloat16
I32 = mybir.dt.int32
U32 = mybir.dt.uint32
AF = mybir.ActivationFunctionType
ALU = mybir.AluOpType
AX = mybir.AxisListType

S = 8192
DM = 2048
H = 16
TPC = 1024
NT = S // 128
NTO = TPC // 128
NC = DM // 128
J = 64
KP = 8 * J
HID = 5632
NJC = HID // 128
SCALE = float(1.0 / np.sqrt(128.0))
EPS = 1e-5
NEG = -1e30
NCORES = 8

ENGS = ("sync", "tensor", "vector", "scalar", "gpsimd")

R1 = 0
R2 = 65536
R3 = 110592
R4 = 143360
AREND = 176128
ARCOLS = AREND // 4


class Bld:
    def __init__(self, sems):
        self.sems = sems
        self.ops = {e: [] for e in ENGS}
        self.v = {k: 0 for k in sems}

    def _add(self, eng, fn, waits, inc, by):
        self.ops[eng].append((fn, tuple(waits), inc, by))
        if inc:
            self.v[inc] += by
            return self.v[inc]

    def t(self, fn, waits=()):
        return self._add("tensor", fn, waits, "T", 1)

    def ve(self, fn, waits=()):
        w = list(waits)
        if self.v["V"] > 0:
            w.append(("V", self.v["V"]))
        return self._add("vector", fn, w, "V", 1)

    def a(self, fn, waits=()):
        w = list(waits)
        if self.v["A"] > 0:
            w.append(("A", self.v["A"]))
        return self._add("scalar", fn, w, "A", 1)

    def g(self, fn, waits=()):
        w = list(waits)
        if self.v["G"] > 0:
            w.append(("G", self.v["G"]))
        return self._add("gpsimd", fn, w, "G", 1)

    def dma(self, q, out, in_, sem, waits=()):
        def fn(e, out=out, in_=in_):
            return e.dma_start(out=out, in_=in_)
        return self._add(q, fn, waits, sem, 16)

    def dmat(self, q, out, in_, sem, waits=()):
        def fn(e, out=out, in_=in_):
            return e.dma_start(out=out, in_=in_, transpose=True)
        return self._add(q, fn, waits, sem, 16)

    def gather(self, out, table, idx_ap, sem, waits=()):
        def fn(e, out=out, table=table, idx_ap=idx_ap):
            return e.indirect_dma_start(out=out, out_offset=None, in_=table,
                                        in_offset=IndirectOffsetOnAxis(idx_ap, 0))
        return self._add("gpsimd", fn, waits, sem, 16)

    def bar(self):
        snap = [(k, v) for k, v in self.v.items() if v > 0]
        for e in ENGS:
            self.ops[e].append((None, tuple(snap), None, 0))

    def emit(self, block):
        sems = self.sems
        for eng_name in ENGS:
            ops = self.ops[eng_name]

            def mk(ops=ops):
                def body(e):
                    for fn, waits, inc, by in ops:
                        for (sn, val) in waits:
                            if val and val > 0:
                                e.wait_ge(sems[sn], val)
                        if fn is None:
                            continue
                        inst = fn(e)
                        if inc:
                            inst.then_inc(sems[inc], by)
                return body
            getattr(block, eng_name)(mk())


def build(debug=0):
    nc = bass.Bass("TRN2", target_bir_lowering=False)
    ctx = ExitStack()
    E = ctx.enter_context
    ap = lambda t: t.ap()

    x_in = ap(nc.dram_tensor("x", [S, DM], F32, kind="ExternalInput"))
    wq_t = ap(nc.dram_tensor("wq_t", [NC, NC, 128, 128], F32, kind="ExternalInput"))
    wk_t = ap(nc.dram_tensor("wk_t", [NC, 128, DM], F32, kind="ExternalInput"))
    wv_t = ap(nc.dram_tensor("wv_t", [NC, 128, DM], F32, kind="ExternalInput"))
    wo_t = ap(nc.dram_tensor("wo_t", [NC, NC, 128, 128], F32, kind="ExternalInput"))
    bq_c = ap(nc.dram_tensor("bq_c", [128, NC], F32, kind="ExternalInput"))
    bk_c = ap(nc.dram_tensor("bk_c", [128, NC], F32, kind="ExternalInput"))
    bv_c = ap(nc.dram_tensor("bv_c", [128, NC], F32, kind="ExternalInput"))
    bo_c = ap(nc.dram_tensor("bo_c", [128, NC], F32, kind="ExternalInput"))
    bk_r = ap(nc.dram_tensor("bk_r", [1, DM], F32, kind="ExternalInput"))
    bq_r = ap(nc.dram_tensor("bq_r", [1, DM], F32, kind="ExternalInput"))
    wg_b = ap(nc.dram_tensor("wg_b", [NJC, 128, DM], I32, kind="ExternalInput"))
    wu_b = ap(nc.dram_tensor("wu_b", [NJC, 128, DM], I32, kind="ExternalInput"))
    wd_b = ap(nc.dram_tensor("wd_b", [NJC, 128, DM], I32, kind="ExternalInput"))
    sg_c = ap(nc.dram_tensor("sg_c", [128, NJC], F32, kind="ExternalInput"))
    su_c = ap(nc.dram_tensor("su_c", [128, NJC], F32, kind="ExternalInput"))
    sd_c = ap(nc.dram_tensor("sd_c", [128, NC], F32, kind="ExternalInput"))
    ones_col = ap(nc.dram_tensor("ones_col", [128, 1], BF16, kind="ExternalInput"))
    ident_in = ap(nc.dram_tensor("ident_in", [128, 128], BF16, kind="ExternalInput"))
    identf_in = ap(nc.dram_tensor("identf_in", [128, 128], F32, kind="ExternalInput"))
    e16_in = ap(nc.dram_tensor("e16_in", [16, 128], F32, kind="ExternalInput"))
    b8_in = ap(nc.dram_tensor("b8_in", [128, 128], F32, kind="ExternalInput"))
    l8_in = ap(nc.dram_tensor("l8_in", [128, 128], F32, kind="ExternalInput"))
    sbase_in = ap(nc.dram_tensor("sbase_in", [128, 1], F32, kind="ExternalInput"))
    zpr_in = ap(nc.dram_tensor("zpr_in", [1, 128], BF16, kind="ExternalInput"))
    onesr_in = ap(nc.dram_tensor("onesr_in", [1, 128], BF16, kind="ExternalInput"))
    ident16_in = ap(nc.dram_tensor("ident16_in", [16, 16], BF16, kind="ExternalInput"))
    zr512_in = ap(nc.dram_tensor("zr512_in", [1, 512], BF16, kind="ExternalInput"))

    out_d = ap(nc.dram_tensor("out", [TPC, DM], F32, kind="ExternalOutput"))

    xnsp = ap(nc.dram_tensor("xnsp", [S, DM], BF16, kind="Internal"))
    wkb_d = ap(nc.dram_tensor("wkb_d", [DM, DM], BF16, kind="Internal"))
    wvb_d = ap(nc.dram_tensor("wvb_d", [DM, DM], BF16, kind="Internal"))
    gdram = ap(nc.dram_tensor("gdram", [S, DM], BF16, kind="Internal"))
    wtt_d = ap(nc.dram_tensor("wtt_d", [16, DM], BF16, kind="Internal"))
    ch_d = ap(nc.dram_tensor("ch_d", [16, 1], F32, kind="Internal"))
    impb_d = ap(nc.dram_tensor("impb_d", [16, S], F32, kind="Internal"))
    candv_d = ap(nc.dram_tensor("candv_d", [16, 8 * J], F32, kind="Internal"))
    mbias_d = ap(nc.dram_tensor("mbias_d", [H, KP], BF16, kind="Internal"))
    wqb_d = ap(nc.dram_tensor("wqb_d", [DM, DM], BF16, kind="Internal"))
    ksel_d = ap(nc.dram_tensor("ksel_d", [H, 4, 128, 128], BF16, kind="Internal"))
    vsf_d = ap(nc.dram_tensor("vsf_d", [H, 4, 128, 128], BF16, kind="Internal"))
    attout_d = ap(nc.dram_tensor("attout_d", [DM, TPC], F32, kind="Internal"))
    x2_d = ap(nc.dram_tensor("x2_d", [TPC, DM], F32, kind="Internal"))
    xn2_d = ap(nc.dram_tensor("xn2_d", [TPC, DM], BF16, kind="Internal"))

    dbg = {}
    if debug:
        def dd(nm, shape, dt=F32):
            dbg[nm] = ap(nc.dram_tensor(nm, shape, dt, kind="ExternalOutput"))
        dd("d_mean", [128, NT]); dd("d_rstd", [128, NT]); dd("d_xbar", [1, DM])
        dd("d_qbar", [1, DM]); dd("d_wtt", [16, DM], BF16); dd("d_ch", [1, 16])
        dd("d_imp", [16, S]); dd("d_tau", [128, 1]); dd("d_valid", [128, J])
        dd("d_s1v", [128, J]); dd("d_s1g", [128, J], U32)
        dd("d_dacc", [128, NC * TPC])
        dd("d_attout", [DM, TPC]); dd("d_x2", [TPC, DM])
        dd("d_xn2", [TPC, DM], BF16); dd("d_ksel", [H, 4, 128, 128], BF16)
        dd("d_vsf", [H, 4, 128, 128], BF16); dd("d_qfm", [128, NC * TPC], BF16)

    sems = {}
    semnames = (["T", "V", "A", "G", "CST", "CST2", "CH", "WT0", "WT1", "CAND", "MB",
                 "KS", "XOF", "XN2F", "WOS", "IML", "WTLD", "DBG2", "QR0", "QR1",
                 "D0", "D1", "D2", "D3", "SP0", "SP1", "SP2", "SP3",
                 "W0", "W1", "WS0", "WS1",
                 "TK0", "TK1", "TK2", "TK3", "IM0", "IM1", "KV0", "KV1",
                 "G0", "G1", "G2", "G3", "S0", "S1", "S2", "S3",
                 "Q0", "Q1", "KH0", "KH1", "VH0", "VH1", "MH0", "MH1",
                 "WO0", "WO1", "AC0", "AC1", "XR0", "XR1", "DBG",
                 "N20", "N21", "X20", "X21", "FG", "FU", "FD",
                 "XF0", "XF1", "O0", "O1", "FG0", "FG1", "FU0", "FU1", "PV0", "PV1"])
    for sn in semnames:
        sems[sn] = E(nc.semaphore(sn))
    b = Bld(sems)

    AR = E(nc.sbuf_tensor("AR", [128, ARCOLS], F32))

    def vf(off, p, n):
        assert off % 4 == 0
        return AR[0:p, off // 4: off // 4 + n]

    def vb(off, p, n):
        assert off % 4 == 0 and n % 2 == 0
        return AR[0:p, off // 4: off // 4 + n // 2].bitcast(BF16)

    def vi(off, p, n):
        return AR[0:p, off // 4: off // 4 + n].bitcast(I32)

    sb = lambda name, shape, dt: E(nc.sbuf_tensor(name, shape, dt))
    stat6 = [sb(f"stat6_{i}", [128, 24], F32) for i in range(4)]
    mvt = [sb(f"mv{i}", [128, 2], F32) for i in range(4)]
    scr1 = vb(R1 + 32768, 128, 2048)
    mean_sb = sb("mean_sb", [128, NT], F32)
    rstd_sb = sb("rstd_sb", [128, NT], F32)
    negms_sb = sb("negms_sb", [128, NT], F32)
    std_sb = [sb(f"std_sb{i}", [128, 1], F32) for i in range(4)]
    ones_sb = sb("ones_sb", [128, 1], BF16)
    ident_sb = sb("ident_sb", [128, 128], BF16)
    identf_sb = sb("identf_sb", [128, 128], F32)
    e16_sb = sb("e16_sb", [16, 128], F32)
    b8_sb = sb("b8_sb", [128, 128], F32)
    l8_sb = sb("l8_sb", [128, 128], F32)
    sbase_sb = sb("sbase_sb", [128, 1], F32)
    zpr_sb = sb("zpr_sb", [1, 128], BF16)
    onesr_sb = sb("onesr_sb", [1, 128], BF16)
    ident16_sb = sb("ident16_sb", [16, 16], BF16)
    zr_sb = sb("zr_sb", [1, 512], BF16)
    bkc_sb = sb("bkc_sb", [128, NC], F32)
    bvc_sb = sb("bvc_sb", [128, NC], F32)
    bqc_sb = sb("bqc_sb", [128, NC], F32)
    boc_sb = sb("boc_sb", [128, NC], F32)
    sg_sb = sb("sg_sb", [128, NJC], F32)
    su_sb = sb("su_sb", [128, NJC], F32)
    sd_sb = sb("sd_sb", [128, NC], F32)
    xbcol = sb("xbcol", [128, NC], BF16)
    qcol = sb("qcol", [128, NC], BF16)
    wtil = sb("wtil", [128, NC * 16], BF16)
    wtt16 = sb("wtt16", [16, DM], BF16)
    chrow = sb("chrow", [1, 16], F32)
    ch16 = sb("ch16", [16, 1], F32)
    impg_s = [sb(f"impg_s{i}", [16, 512], F32) for i in range(2)]
    imp_hs = sb("imp_hs", [128, 1024], F32)
    s1v = sb("s1v", [128, J], F32)
    s1i = sb("s1i", [128, J], U32)
    s1g = sb("s1g", [128, J], U32)
    m8t = sb("m8t", [128, 8], F32)
    cand = sb("cand", [16, 8 * J], F32)
    m8b = sb("m8b", [16, 8], F32)
    tau16 = sb("tau16", [16, 1], F32)
    tau128 = sb("tau128", [128, 1], F32)
    strict = sb("strict", [128, J], F32)
    tiem = sb("tiem", [128, J], F32)
    scnt = sb("scnt", [128, 1], F32)
    tcnt = sb("tcnt", [128, 1], F32)
    c1h = sb("c1h", [128, 1], F32)
    quota = sb("quota", [128, 1], F32)
    tieoff = sb("tieoff", [128, 1], F32)
    tiescan = sb("tiescan", [128, J], F32)
    zcol = sb("zcol", [128, J], F32)
    tierank = sb("tierank", [128, J], F32)
    tieok = sb("tieok", [128, J], F32)
    valid = sb("valid", [128, J], F32)
    mb128 = sb("mb128", [128, J], BF16)
    smax = sb("smax", [128, 1], F32)
    ssum = sb("ssum", [128, 1], F32)
    srec = sb("srec", [128, 1], F32)
    xsum_hi = sb("xsum_hi", [1, TPC], BF16)
    csts = sb("csts", [128, 3], F32)
    xsum_lo = sb("xsum_lo", [1, TPC], BF16)

    xt = [vf(R1 + i * 8192, 128, 2048) for i in range(4)]
    xnt = [vb(R4 + i * 4096, 128, 2048) for i in range(4)]
    wrow = [vb(R2 + i * 4096, 128, 2048) for i in range(2)]
    wkt_row = [vb(R2 + 8192 + i * 4096, 128, 2048) for i in range(4)]
    qrow = [vb(R2 + 24576 + i * 4096, 128, 2048) for i in range(2)]
    xbar_sb = vf(R1 + 0, 1, 2048)
    xbar16 = vb(R1 + 8192, 1, 2048)
    qbar_sb = vf(R1 + 16384, 1, 2048)
    qbar16 = vb(R1 + 24576, 1, 2048)
    prodr = vf(R1 + 32768, 1, 2048)
    bkr_row = vf(R1 + 40960, 1, 2048)
    bqr_row = vf(R1 + 49152, 1, 2048)
    wttrow = [vb(R1 + 57344, 1, 2048), vb(R1 + 61440, 1, 2048)]
    fmx = [vb(R3 + i * 16384, 128, 8192) for i in range(2)]
    wkv = [vb(R2 + i * 4096, 128, 2048) for i in range(2)]
    gt = [vb(R2 + 8192 + i * 4096, 128, 2048) for i in range(4)]
    fmc = [vb(R2 + 24576 + i * 4096, 128, 2048) for i in range(2)]
    kstg = vb(R2 + 32768, 128, 2048)
    xnofm = vb(R3 + 0, 128, 16384)
    wqcol = [vb(R2 + i * 4096, 128, 2048) for i in range(2)]
    qfm = vb(R4 + 0, 128, 16384)
    outfm = vb(R3 + 0, 128, 16384)
    kselb = [vb(R2 + 8192 + i * 1024, 128, 512) for i in range(2)]
    vkmb = [vb(R2 + 16384 + i * 1024, 128, 512) for i in range(2)]
    mbrow = [vb(R2 + 24576 + i * 1024, 1, 512) for i in range(2)]
    wsm = vb(R2 + 28672, 128, 512)
    wTt = vb(R2 + 30720, 128, 512)
    ssub = vf(R2 + 32768, 128, 512)
    sexp = vb(R2 + 34816, 128, 512)
    wocol = [vb(R2 + i * 4096, 128, 2048) for i in range(2)]
    attf = vf(R2 + 8192, 128, 1024)
    attcols = [vf(R2 + 12288 + i * 8192, 128, 2048) for i in range(2)]
    x2t = [vf(R1 + i * 8192, 128, 2048) for i in range(2)]
    xn2t = [vb(R4 + i * 4096, 128, 2048) for i in range(2)]
    dacc = vf(R1 + 0, 128, 16384)
    xn2fm = vb(R4 + 0, 128, 16384)
    wdbb = [vb(R2 + i * 4096, 128, 2048) for i in range(8)]
    wgbb = [vb(R2 + (8 + i) * 4096, 128, 2048) for i in range(2)]
    wubb = [vb(R2 + 10 * 4096, 128, 2048), vb(R3 + 28672, 128, 2048)]
    hpr = [vb(R3 + i * 2048, 128, 1024) for i in range(8)]
    gsil = vb(R3 + 16384, 128, 1024)
    dscs = vb(R3 + 18432, 128, 1024)
    wdi = vi(R3 + 20480, 128, 2048)
    xsum_f = vf(R3 + 0, 1, 1024)
    xsum_t = vf(R3 + 4096, 1, 1024)
    x2fin = [vf(R4 + i * 8192, 128, 2048) for i in range(2)]

    pbk = [E(nc.psum_tensor(f"pbk{i}", [128, 512], F32)) for i in range(8)]

    def pbf(i):
        return pbk[i][:]

    def pbb(i, c0=0, n=512):
        assert c0 % 2 == 0 and n % 2 == 0
        return pbk[i][:, c0 // 2: (c0 + n) // 2].bitcast(BF16)

    for dst, src in [(ones_sb[:], ones_col), (ident_sb[:], ident_in),
                     (identf_sb[:], identf_in), (e16_sb[:], e16_in),
                     (b8_sb[:], b8_in), (l8_sb[:], l8_in),
                     (sbase_sb[:], sbase_in), (zpr_sb[:], zpr_in),
                     (onesr_sb[:], onesr_in), (ident16_sb[:], ident16_in),
                     (zr_sb[:], zr512_in),
                     (bkc_sb[:], bk_c), (bvc_sb[:], bv_c),
                     (bqc_sb[:], bq_c), (boc_sb[:], bo_c),
                     (sg_sb[:], sg_c), (su_sb[:], su_c), (sd_sb[:], sd_c)]:
        b.dma("sync", dst, src, "CST")
    LDC = b.v["CST"]
    vz = b.ve(lambda e: e.memset(zcol[:], 0.0))
    vz = b.ve(lambda e: e.memset(csts[:, 0:1], EPS))
    vz = b.ve(lambda e: e.memset(csts[:, 1:2], 1.0 / S))
    VCST = b.ve(lambda e: e.memset(csts[:, 2:3], SCALE))

    sgk = [0] * (3 * NC)
    for which, (srcs, wdst) in enumerate(
            [([wq_t[dc].rearrange("m p q -> p m q") for dc in range(NC)], wqb_d),
             ([wk_t[dc] for dc in range(NC)], wkb_d),
             ([wv_t[dc] for dc in range(NC)], wvb_d)]):
        for dc in range(NC):
            idx = which * NC + dc
            i = idx % 2
            w = [] if idx < 2 else [(f"WS{i}", sgk[idx - 2])]
            lgv = b.dma("gpsimd", wrow[i], srcs[dc], f"W{i}", waits=w)
            sgk[idx] = b._add("gpsimd",
                              (lambda e, i=i, dc=dc, wdst=wdst: e.dma_start(
                                  out=wdst[dc * 128:(dc + 1) * 128, :], in_=wrow[i])),
                              [(f"W{i}", lgv)], f"WS{i}", 16)
    WQS = (sgk[NC - 2], sgk[NC - 1])
    WKS = (sgk[2 * NC - 2], sgk[2 * NC - 1])
    WSTOT = (b.v["WS0"], b.v["WS1"])

    a_xn = [0] * NT
    sg_sp = [0] * NT
    t_xb = [0] * NT

    def emit_spill(k):
        i = k % 4
        sg_sp[k] = b._add("scalar" if k % 2 == 0 else "sync",
                          (lambda e, i=i, k=k: e.dma_start(
                              out=xnsp[k * 128:(k + 1) * 128, :], in_=xnt[i])),
                          [("V", a_xn[k])], f"SP{i}", 16)

    for k in range(NT):
        i = k % 4
        w = [("V", a_xn[k - 4])] if k >= 4 else []
        ldv = b.dma("sync" if k % 2 == 0 else "scalar",
                    xt[i], x_in[k * 128:(k + 1) * 128, :], f"D{i}", waits=w)
        if k >= 4:
            emit_spill(k - 4)
        if k % 3 != 2:
            asum = b.a(lambda e, i=i: e.activation(scr1, xt[i], AF.Copy,
                                                   bias=0.0, scale=1.0,
                                                   accum_out=mvt[i][:, 0:1]),
                       waits=[(f"D{i}", ldv)])
            asq = b.a(lambda e, i=i: e.activation(scr1, xt[i], AF.Square,
                                                  bias=0.0, scale=1.0,
                                                  accum_out=mvt[i][:, 1:2]))
            vm = b.ve(lambda e, i=i, k=k: e.tensor_scalar(
                mean_sb[:, k:k + 1], mvt[i][:, 0:1], 1.0 / DM, None,
                ALU.mult, ALU.bypass), waits=[("A", asum)])
            vv_ = b.ve(lambda e, i=i, k=k: e.scalar_tensor_tensor(
                std_sb[i][:], mean_sb[:, k:k + 1], -1.0, mean_sb[:, k:k + 1],
                ALU.mult, ALU.mult))
            vv_ = b.ve(lambda e, i=i: e.scalar_tensor_tensor(
                std_sb[i][:], mvt[i][:, 1:2], 1.0 / DM, std_sb[i][:],
                ALU.mult, ALU.add), waits=[("A", asq)])
            av = b.a(lambda e, i=i: e.activation(std_sb[i][:], std_sb[i][:], AF.Sqrt,
                                                 bias=csts[:, 0:1], scale=1.0),
                     waits=[("V", vv_), ("V", VCST)])
        else:
            for q in range(4):
                b.ve(lambda e, i=i, q=q: e.bn_stats(
                    stat6[i][:, q * 6:(q + 1) * 6], xt[i][:, q * 512:(q + 1) * 512]),
                    waits=[(f"D{i}", ldv)] if q == 0 else ())
            b.ve(lambda e, i=i: e.bn_aggr(mvt[i][:], stat6[i][:]))
            vm = b.ve(lambda e, i=i, k=k: e.tensor_copy(mean_sb[:, k:k + 1],
                                                        mvt[i][:, 0:1]))
            av = b.a(lambda e, i=i: e.activation(std_sb[i][:], mvt[i][:, 1:2], AF.Sqrt,
                                                 bias=csts[:, 0:1], scale=1.0),
                     waits=[("V", vm), ("V", VCST)])
        b.ve(lambda e, i=i, k=k: e.reciprocal(rstd_sb[:, k:k + 1], std_sb[i][:]),
             waits=[("A", av)])
        vn = b.ve(lambda e, i=i, k=k: e.scalar_tensor_tensor(
            negms_sb[:, k:k + 1], mean_sb[:, k:k + 1], -1.0, rstd_sb[:, k:k + 1],
            ALU.mult, ALU.mult))
        wn = []
        if k >= 4:
            wn += [(f"SP{i}", sg_sp[k - 4]), ("T", t_xb[k - 4])]
        axn = b.ve(lambda e, i=i, k=k: e.tensor_scalar(
            xnt[i], xt[i], rstd_sb[:, k:k + 1], negms_sb[:, k:k + 1],
            ALU.mult, ALU.add), waits=wn)
        a_xn[k] = axn
        for q in range(4):
            t_xb[k] = b.t(lambda e, i=i, q=q, k=k: e.matmul(
                pbk[q][0:1, :], ones_sb[:], xnt[i][:, q * 512:(q + 1) * 512],
                start=(k == 0), stop=(k == NT - 1)),
                waits=([("V", axn)] + ([("CST", LDC)] if k == 0 else [])) if q == 0 else ())
    for k in range(NT - 4, NT):
        emit_spill(k)
    T_P1 = b.v["T"]

    av = 0
    for q in range(4):
        av = b.ve(lambda e, q=q: e.tensor_scalar(
            xbar_sb[:, q * 512:(q + 1) * 512], pbk[q][0:1, :], 1.0 / S, None,
            ALU.mult, ALU.bypass), waits=[("T", T_P1)] if q == 0 else ())
    A_XB = av
    vx16 = b.ve(lambda e: e.tensor_copy(xbar16, xbar_sb))
    t_tr = 0
    for c in range(NC):
        t_tr = b.t(lambda e, c=c: e.matmul(pbk[4][:, c:c + 1],
                                           xbar16[:, c * 128:(c + 1) * 128],
                                           onesr_sb[:, 0:1], start=True, stop=True),
                   waits=[("V", vx16)] if c == 0 else ())
    vxb = b.ve(lambda e: e.tensor_copy(xbcol[:], pbk[4][:, 0:NC]),
               waits=[("T", t_tr)])

    if debug:
        b.dma("sync", dbg["d_mean"], mean_sb[:], "DBG", waits=[("V", b.v["V"])])
        b.dma("sync", dbg["d_rstd"], rstd_sb[:], "DBG", waits=[("V", b.v["V"])])
        b.dma("sync", dbg["d_xbar"], xbar_sb, "DBG", waits=[("V", av)])

    ldbq = b.dma("sync", bqr_row, bq_r, "CST2")
    ldbk = b.dma("sync", bkr_row, bk_r, "CST2")
    lg_q = [0] * NC
    t_q = [0] * NC
    for dc in range(NC):
        i = dc % 2
        w = [("T", t_q[dc - 2])] if dc >= 2 else [("WS0", WQS[0]), ("WS1", WQS[1])]
        lg_q[dc] = b.dma("sync", qrow[i], wqb_d[dc * 128:(dc + 1) * 128, :],
                         f"QR{i}", waits=w)
        for q in range(4):
            t_q[dc] = b.t(lambda e, i=i, q=q, dc=dc: e.matmul(
                pbk[q][0:1, :], xbcol[:, dc:dc + 1], qrow[i][:, q * 512:(q + 1) * 512],
                start=(dc == 0), stop=(dc == NC - 1)),
                waits=[(f"QR{i}", lg_q[dc]), ("V", vxb)] if q == 0 else ())
    aq = 0
    for q in range(4):
        aq = b.a(lambda e, q=q: e.activation(
            qbar_sb[:, q * 512:(q + 1) * 512], pbk[q][0:1, :], AF.Copy,
            bias=0.0, scale=1.0), waits=[("T", t_q[NC - 1])] if q == 0 else ())
    vqb = b.ve(lambda e: e.tensor_tensor(qbar_sb, qbar_sb, bqr_row, ALU.add),
               waits=[("A", aq), ("CST2", 32)])
    vq16 = b.ve(lambda e: e.tensor_copy(qbar16, qbar_sb))
    t_tr = 0
    for c in range(NC):
        t_tr = b.t(lambda e, c=c: e.matmul(pbk[4][:, 32 + c:33 + c],
                                           qbar16[:, c * 128:(c + 1) * 128],
                                           onesr_sb[:, 0:1], start=True, stop=True),
                   waits=[("V", vq16), ("V", vxb)] if c == 0 else ())
    vqc = b.ve(lambda e: e.tensor_copy(qcol[:], pbk[4][:, 32:32 + NC]),
               waits=[("T", t_tr)])
    vpr = b.ve(lambda e: e.tensor_tensor(prodr, qbar_sb, bkr_row, ALU.mult))
    vch = b.ve(lambda e: e.tensor_reduce(
        chrow[:].rearrange("a (h q) -> a h q", q=1),
        prodr.rearrange("a (h q) -> a h q", q=128), AX.X, ALU.add))
    chst = b.dma("sync", ch_d, chrow[:], "CH", waits=[("V", vch)])
    chld = b.dma("sync", ch16[:], ch_d, "CH", waits=[("CH", chst)])

    if debug:
        b.dma("sync", dbg["d_qbar"], qbar_sb, "DBG", waits=[("V", vqb)])
        b.dma("sync", dbg["d_ch"], chrow[:], "DBG", waits=[("V", vch)])


    a_wt = [0] * NC
    so_wt = [0] * NC
    t_wt = [0] * NC
    for jc in range(NC):
        i = jc % 4
        w = [("WS0", WKS[0]), ("WS1", WKS[1])]
        if jc >= 4:
            w.append(("T", t_wt[jc - 4]))
        lav = b.dmat("scalar",
                     wkt_row[i], wkb_d[:, jc * 128:(jc + 1) * 128],
                     f"TK{i}", waits=w)
        for q in range(4):
            wmm = [(f"TK{i}", lav), ("V", vqc)] if q == 0 else ()
            if jc > 1 and q == 0:
                wmm = list(wmm) + [(f"WT{jc % 2}", so_wt[jc - 2])]
            t_wt[jc] = b.t(lambda e, i=i, q=q, jc=jc: e.matmul(
                pbk[q][0:1, :], qcol[:, jc:jc + 1], wkt_row[i][:, q * 512:(q + 1) * 512],
                start=True, stop=True), waits=wmm)
        for q in range(4):
            a_wt[jc] = b.a(lambda e, q=q, jc=jc: e.activation(
                wttrow[jc % 2][:, q * 512:(q + 1) * 512], pbk[q][0:1, :],
                AF.Copy, bias=0.0, scale=1.0),
                waits=[("T", t_wt[jc])] if q == 0 else ())
        so_wt[jc] = b._add("sync",
                           (lambda e, jc=jc: e.dma_start(
                               out=wtt_d[jc:jc + 1, :], in_=wttrow[jc % 2])),
                           [("A", a_wt[jc])], f"WT{jc % 2}", 16)
    wtld = b.dma("sync", wtt16[:], wtt_d, "WTLD",
                 waits=[("WT0", so_wt[NC - 2]), ("WT1", so_wt[NC - 1])])
    t_tr = 0
    for dc in range(NC):
        t_tr = b.t(lambda e, dc=dc: e.matmul(
            pbk[5][:, dc * 16:(dc + 1) * 16],
            wtt16[:, dc * 128:(dc + 1) * 128], ident16_sb[:],
            start=True, stop=True),
            waits=[("WTLD", wtld)] if dc == 0 else ())
    vwt = b.ve(lambda e: e.tensor_copy(wtil[:], pbk[5][:, 0:NC * 16]),
               waits=[("T", t_tr)])
    if debug:
        b.dma("sync", dbg["d_wtt"], wtt16[:], "DBG", waits=[("WTLD", wtld)])

    b.bar()

    t_imp = [0] * NC
    SPTOT = tuple(b.v[f"SP{j}"] for j in range(4))
    for dc in range(NC):
        i = dc % 2
        w = [(f"SP{j}", SPTOT[j]) for j in range(4)]
        if dc >= 2:
            w.append(("T", t_imp[dc - 2]))
        else:
            w.append(("T", t_wt[NC - 1]))
        lav = b.dmat("scalar",
                     fmx[i], xnsp[:, dc * 128:(dc + 1) * 128],
                     f"TK{i}", waits=w)
        for mg in range(16):
            bank, pos = mg % 8, 32 * (mg // 8)
            t_imp[dc] = b.t(lambda e, i=i, dc=dc, mg=mg, bank=bank, pos=pos: e.matmul(
                pbk[bank][pos:pos + 16, :], wtil[:, dc * 16:(dc + 1) * 16],
                fmx[i][:, mg * 512:(mg + 1) * 512],
                start=(dc == 0), stop=(dc == NC - 1),
                tile_position=(0, pos), skip_group_check=True),
                waits=[(f"TK{i}", lav)] if mg == 0 else ())
    so_imp = [0, 0]
    a_imp = 0
    for mg in range(16):
        bank, pos = mg % 8, 32 * (mg // 8)
        i = mg % 2
        w = [("T", t_imp[NC - 1]), ("CH", chld)] if mg == 0 else []
        if mg >= 2:
            w.append((f"IM{mg % 2}", so_imp[mg % 2]))
        a_imp = b.ve(lambda e, mg=mg, bank=bank, pos=pos, i=i: e.tensor_scalar(
            impg_s[i][:], pbk[bank][pos:pos + 16, :], ch16[:], None,
            ALU.add, ALU.bypass), waits=w)
        so_imp[mg % 2] = b._add("sync",
                                (lambda e, mg=mg, i=i: e.dma_start(
                                    out=impb_d[:, mg * 512:(mg + 1) * 512],
                                    in_=impg_s[i][:])),
                                [("V", a_imp)], f"IM{mg % 2}", 16)
    A_IMP = a_imp
    IMTOT = (b.v["IM0"], b.v["IM1"])
    if debug:
        b.dma("sync", dbg["d_imp"], impb_d, "DBG",
              waits=[("IM0", IMTOT[0]), ("IM1", IMTOT[1])])
    impld = b.dma("sync", imp_hs[:],
                  impb_d.rearrange("h (s f) -> h s f", s=8), "IML",
                  waits=[("IM0", IMTOT[0]), ("IM1", IMTOT[1])])

    vv = 0
    for r in range(8):
        vv = b.ve(lambda e: e.max(m8t[:], imp_hs[:]),
                  waits=[("IML", impld)] if r == 0 else ())
        vv = b.ve(lambda e, r=r: e.max_index(s1i[:, 8 * r:8 * r + 8], m8t[:], imp_hs[:]))
        vv = b.ve(lambda e, r=r: e.tensor_copy(s1v[:, 8 * r:8 * r + 8], m8t[:]))
        if r < 7:
            vv = b.ve(lambda e: e.match_replace(imp_hs[:], m8t[:], imp_hs[:], NEG))
    vs1g = b.ve(lambda e: e.tensor_scalar(s1g[:], s1i[:], sbase_sb[:], None,
                                          ALU.add, ALU.bypass))
    cvst = b.dma("sync", candv_d.rearrange("h (s j) -> h s j", s=8), s1v[:],
                 "CAND", waits=[("V", b.v["V"])])
    cvld = b.dma("sync", cand[:], candv_d, "CAND", waits=[("CAND", cvst)])
    for r in range(32):
        vv = b.ve(lambda e: e.max(m8b[:], cand[:]),
                  waits=[("CAND", cvld)] if r == 0 else ())
        if r < 31:
            vv = b.ve(lambda e: e.match_replace(cand[:], m8b[:], cand[:], NEG))
    vtau = b.ve(lambda e: e.tensor_copy(tau16[:], m8b[:, 7:8]))
    ttau = b.t(lambda e: e.matmul(pbk[0][:, 0:1], e16_sb[:], tau16[:],
                                  start=True, stop=True),
               waits=[("V", vtau)])
    vtb = b.ve(lambda e: e.tensor_copy(tau128[:], pbk[0][:, 0:1]), waits=[("T", ttau)])
    vst = b.ve(lambda e: e.tensor_scalar(strict[:], s1v[:], tau128[:], None,
                                         ALU.is_gt, ALU.add, accum_out=scnt[:]))
    vti = b.ve(lambda e: e.tensor_scalar(tiem[:], s1v[:], tau128[:], None,
                                         ALU.is_equal, ALU.add, accum_out=tcnt[:]))
    tc1 = b.t(lambda e: e.matmul(pbk[1][:, 0:1], b8_sb[:], scnt[:],
                                 start=True, stop=True), waits=[("V", vst)])
    tto = b.t(lambda e: e.matmul(pbk[2][:, 0:1], l8_sb[:], tcnt[:],
                                 start=True, stop=True), waits=[("V", vti)])
    vc1 = b.ve(lambda e: e.tensor_copy(c1h[:], pbk[1][:, 0:1]), waits=[("T", tc1)])
    vto = b.ve(lambda e: e.tensor_copy(tieoff[:], pbk[2][:, 0:1]), waits=[("T", tto)])
    vqo = b.ve(lambda e: e.tensor_scalar(quota[:], c1h[:], -1.0, 256.0,
                                         ALU.mult, ALU.add))
    vsc = b.ve(lambda e: e.tensor_tensor_scan(tiescan[:], tiem[:], zcol[:], 0.0,
                                              ALU.add, ALU.add))
    vtr = b.ve(lambda e: e.tensor_tensor(tierank[:], tiescan[:], tiem[:], ALU.subtract))
    vtr = b.ve(lambda e: e.tensor_scalar(tierank[:], tierank[:], tieoff[:], None,
                                         ALU.add, ALU.bypass))
    vtk = b.ve(lambda e: e.tensor_scalar(tieok[:], tierank[:], quota[:], None,
                                         ALU.is_lt, ALU.bypass))
    vva = b.ve(lambda e: e.tensor_tensor(valid[:], tiem[:], tieok[:], ALU.mult))
    vva = b.ve(lambda e: e.tensor_tensor(valid[:], valid[:], strict[:], ALU.add))
    vmb = b.ve(lambda e: e.tensor_scalar(mb128[:], valid[:], -1.0, 30000.0,
                                         ALU.add, ALU.mult))
    mbst = 0
    for qq in range(4):
        mbst = b.dma("sync",
                     mbias_d[:, qq * 128:(qq + 1) * 128].rearrange(
                         "h (s j) -> h s j", s=8),
                     mb128[:, qq * 16:(qq + 1) * 16],
                     "MB", waits=[("V", vmb)] if qq == 0 else ())
    if debug:
        b.dma("sync", dbg["d_tau"], tau128[:], "DBG", waits=[("V", vtb)])
        b.dma("sync", dbg["d_valid"], valid[:], "DBG", waits=[("V", vva)])
        b.dma("sync", dbg["d_s1v"], s1v[:], "DBG", waits=[("V", b.v["V"])])
        b.dma("sync", dbg["d_s1g"], s1g[:], "DBG", waits=[("V", vs1g)])

    so_g = [0] * J
    for j in range(J):
        i = j % 4
        w = [("V", vs1g)] + [(f"SP{j}", SPTOT[j]) for j in range(4)]
        if j >= 4:
            w.append((f"S{i}", so_g[j - 4]))
        lgv = b.gather(gt[i], xnsp, s1g[:, j:j + 1], f"G{i}", waits=w)
        so_g[j] = b._add("sync",
                         (lambda e, j=j, i=i: e.dma_start(
                             out=gdram[j * 128:(j + 1) * 128, :], in_=gt[i])),
                         [(f"G{i}", lgv)], f"S{i}", 16)
    STOT = [b.v[f"S{i}"] for i in range(4)]
    W_GD = [(f"S{i}", STOT[i]) for i in range(4)]

    la_xo = [0] * NC
    for dc in range(NC):
        la_xo[dc] = b.dmat("scalar", vb(R3 + dc * 2048, 128, 1024),
                           xnsp[0:TPC, dc * 128:(dc + 1) * 128], "XOF",
                           waits=[("T", t_imp[NC - 1])] +
                                 [(f"SP{j}", SPTOT[j]) for j in range(4)]
                           if dc == 0 else ())
    LA_XO = b.v["XOF"]
    lg_wq = [0] * NC
    t_qp = [0] * NC
    a_qp = [0] * NC
    for m in range(NC):
        i = m % 2
        w = [("T", t_qp[m - 2])] if m >= 2 else [("T", t_imp[NC - 1])]
        lg_wq[m] = b.dma("sync", wqcol[i].rearrange("p (c q) -> p c q", q=128),
                         wqb_d[:, m * 128:(m + 1) * 128].rearrange(
                             "(c p) q -> p c q", p=128), f"Q{i}", waits=w)
        for th in range(2):
            for dc in range(NC):
                t_qp[m] = b.t(lambda e, i=i, m=m, th=th, dc=dc: e.matmul(
                    pbk[4 + th][:, :], wqcol[i][:, dc * 128:(dc + 1) * 128],
                    vb(R3 + dc * 2048 + th * 1024, 128, 512),
                    start=(dc == 0), stop=(dc == NC - 1)),
                    waits=([(f"Q{i}", lg_wq[m]), ("XOF", LA_XO)] +
                           ([("V", a_qp[m - 1])] if m >= 1 else [("V", A_IMP)]))
                    if (th == 0 and dc == 0) else ())
        for th in range(2):
            a_qp[m] = b.ve(lambda e, m=m, th=th: e.tensor_scalar(
                qfm[:, m * 1024 + th * 512: m * 1024 + (th + 1) * 512],
                pbk[4 + th][:, :], bqc_sb[:, m:m + 1], None, ALU.add, ALU.bypass),
                waits=[("T", t_qp[m])] if th == 0 else ())
    A_QP = a_qp[NC - 1]


    def hview(i, h):
        v3 = fmc[i].rearrange("p (j q) -> p j q", q=128)
        v3 = v3[:, :, 8 * h:8 * h + 8]
        return v3.rearrange("p j s -> p s j")

    t_ks = 0
    so_ks = 0
    a_ev = 0
    tks_par = [0, 0]
    wkv2 = [vb(R2 + 36864 + i * 4096, 128, 2048) for i in range(2)]
    for qq in range(4):
        for bk_ in range(8):
            wz = []
            if bk_ == 0:
                wz = [("CST", LDC)]
                if qq > 0:
                    wz.append(("V", a_ev))
                else:
                    wz.append(("T", t_imp[NC - 1]))
                    wz.append(("V", vto))
            if bk_ == 4 and qq == 0:
                wz = [("V", A_QP)]
            t_ks = b.t(lambda e, bk_=bk_: e.matmul(
                pbk[bk_][:, :], zr_sb[:, 0:128], zr_sb[:],
                start=True, stop=True), waits=wz)
        la_f = [0] * NC
        ld_w = [0] * NC
        ld_w2 = [0] * NC
        for dc in range(NC):
            i = dc % 2
            w = list(W_GD)
            if qq > 0 or dc >= 2:
                w.append(("T", tks_par[i]))
            else:
                w.append(("T", t_qp[NC - 1]))
            la_f[dc] = b.dmat("scalar",
                              fmc[i],
                              gdram[qq * 2048:(qq + 1) * 2048,
                                    dc * 128:(dc + 1) * 128],
                              f"TK{i}", waits=w)
            ld_w[dc] = b.dma("sync", wkv[i],
                             wkb_d[dc * 128:(dc + 1) * 128, :],
                             f"KV{i}", waits=[w[-1]])
            ld_w2[dc] = b.dma("gpsimd", wkv2[i],
                              wvb_d[dc * 128:(dc + 1) * 128, :],
                              f"PV{i}", waits=[w[-1]])
            for h in range(H):
                wmm = [(f"TK{i}", la_f[dc]), (f"KV{i}", ld_w[dc]),
                       (f"PV{i}", ld_w2[dc])] if h == 0 else ()
                if h == 0 and dc == 0 and qq > 0:
                    wmm = list(wmm) + [("KS", so_ks)]
                t_ks = b.t(lambda e, i=i, dc=dc, h=h: e.matmul(
                    pbk[h // 4][:, (h % 4) * 128:(h % 4 + 1) * 128],
                    wkv[i][:, h * 128:(h + 1) * 128], hview(i, h),
                    start=False, stop=False,
                    skip_group_check=True), waits=wmm)
                t_ks = b.t(lambda e, i=i, dc=dc, h=h: e.matmul(
                    pbk[4 + h // 4][:, (h % 4) * 128:(h % 4 + 1) * 128],
                    wkv2[i][:, h * 128:(h + 1) * 128], hview(i, h),
                    start=False, stop=False,
                    skip_group_check=True))
            tks_par[i] = t_ks
        for which, (dstd, bias_sb, boff) in enumerate(
                [(ksel_d, bkc_sb, 0), (vsf_d, bvc_sb, 4)]):
            for h in range(H):
                wev = []
                if h == 0:
                    if which == 0:
                        wev.append(("T", t_ks))
                    if so_ks:
                        wev.append(("KS", so_ks))
                a_ev = b.ve(lambda e, h=h, bias_sb=bias_sb, boff=boff: e.tensor_scalar(
                    kstg[:, h * 128:(h + 1) * 128],
                    pbk[boff + h // 4][:, (h % 4) * 128:(h % 4 + 1) * 128],
                    bias_sb[:, h:h + 1], None, ALU.add, ALU.bypass), waits=wev)
            so_ks = b._add("sync",
                           (lambda e, qq=qq, dstd=dstd: e.dma_start(
                               out=dstd[:, qq].rearrange("h p q -> p h q"),
                               in_=kstg.rearrange("p (h q) -> p h q", q=128))),
                           [("V", a_ev)], "KS", 16)
    SO_KS = so_ks

    b.bar()


    last_sc_read = [0, 0]
    last_av_read = [0, 0]
    last_scT = [0, 0]
    last_avT = [0, 0]
    ld_k = [0, 0]
    ld_v = [0, 0]
    ld_m = [0, 0]
    v_wT = 0
    MBTOT = b.v["MB"]

    def em_loads(h):
        hb = h % 2
        wk_ = [("KS", SO_KS)]
        if last_scT[hb]:
            wk_.append(("T", last_scT[hb]))
        ld_k[hb] = b.dma("sync", kselb[hb].rearrange("p (c q) -> p c q", q=128),
                         ksel_d[h].rearrange("qq p q -> p qq q"),
                         f"KH{hb}", waits=wk_)
        wv_ = [("KS", SO_KS)]
        if last_avT[hb]:
            wv_.append(("T", last_avT[hb]))
        ld_v[hb] = b.dmat("scalar", vkmb[hb],
                          vsf_d[h].rearrange("qq p q -> (qq p) q"),
                          f"VH{hb}", waits=wv_)
        ld_m[hb] = b.dma("gpsimd", mbrow[hb], mbias_d[h:h + 1, :],
                         f"MH{hb}", waits=[("MB", MBTOT)] + (
                             [("T", last_scT[hb])] if last_scT[hb] else []))

    def em_scores(it):
        h, tt = it // NTO, it % NTO
        hb = h % 2
        sbank = 4 + it % 2
        wsc = [("V", A_QP), (f"KH{hb}", ld_k[hb]), (f"MH{hb}", ld_m[hb])]
        if last_sc_read[it % 2]:
            wsc.append(("V", last_sc_read[it % 2]))
        tsc = b.t(lambda e, h=h, tt=tt, sbank=sbank, hb=hb: e.matmul(
            pbk[sbank][:, :], qfm[:, h * 1024 + tt * 128: h * 1024 + (tt + 1) * 128],
            kselb[hb][:, :], start=True, stop=False), waits=wsc)
        tsc = b.t(lambda e, hb=hb, sbank=sbank: e.matmul(
            pbk[sbank][:, :], onesr_sb[:], mbrow[hb], start=False, stop=True))
        last_scT[hb] = tsc
        return tsc

    def em_rest(it, tsc):
        nonlocal_v = None
        h, tt = it // NTO, it % NTO
        hb = h % 2
        sbank = 4 + it % 2
        abank = 6 + it % 2
        vmx = b.ve(lambda e, sbank=sbank: e.reduce_max(smax[:], pbk[sbank][:, :], AX.X),
                   waits=[("T", tsc)])
        vsb = b.ve(lambda e, sbank=sbank: e.tensor_scalar(
            ssub, pbk[sbank][:, :], smax[:], None, ALU.subtract, ALU.bypass))
        last_sc_read[it % 2] = vsb
        aex = b.a(lambda e: e.activation(sexp, ssub, AF.Exp, bias=0.0,
                                         scale=csts[:, 2:3], accum_out=ssum[:]),
                  waits=[("V", vsb)])
        vrc = b.ve(lambda e: e.reciprocal(srec[:], ssum[:]), waits=[("A", aex)])
        vws = b.ve(lambda e: e.tensor_scalar(wsm, sexp, srec[:], None,
                                             ALU.mult, ALU.bypass))
        global_v_wT = v_wT
        t_wt4 = 0
        for kc in range(4):
            wv_ = [("V", vws)] if kc == 0 else ()
            if kc == 0 and global_v_wT:
                wv_ = list(wv_) + [("V", global_v_wT)]
            t_wt4 = b.t(lambda e, kc=kc: e.transpose(
                pbb(3, kc * 128, 128), wsm[:, kc * 128:(kc + 1) * 128],
                ident_sb[:]), waits=wv_)
        nv_wT = b.ve(lambda e: e.tensor_copy(wTt, pbb(3, 0, 512)),
                     waits=[("T", t_wt4)])
        wav = [("V", nv_wT), (f"VH{hb}", ld_v[hb])]
        if last_av_read[it % 2]:
            wav.append(("A", last_av_read[it % 2]))
        tav = 0
        for kc in range(4):
            tav = b.t(lambda e, hb=hb, kc=kc, abank=abank: e.matmul(
                pbk[abank][:, 0:128], vkmb[hb][:, kc * 128:(kc + 1) * 128],
                wTt[:, kc * 128:(kc + 1) * 128],
                start=(kc == 0), stop=(kc == 3)),
                waits=wav if kc == 0 else ())
        last_avT[hb] = tav
        aov = b.a(lambda e, h=h, tt=tt, abank=abank: e.activation(
            outfm[:, h * 1024 + tt * 128: h * 1024 + (tt + 1) * 128],
            pbk[abank][:, 0:128], AF.Copy, bias=0.0, scale=1.0),
            waits=[("T", tav)])
        last_av_read[it % 2] = aov
        return nv_wT

    NIT = H * NTO
    pend = None
    for it in range(NIT + 1):
        if it < NIT:
            if it % NTO == 0:
                em_loads(it // NTO)
            tsc = em_scores(it)
        if pend is not None:
            v_wT = em_rest(it - 1, pend)
        pend = tsc if it < NIT else None
        A_ATT = b.v["A"]

    b.bar()

    lg_wo = [0] * NC
    t_wo = [0] * NC
    v_wo = [0] * NC
    so_wo = [0] * NC
    for m in range(NC):
        i = m % 2
        w = [("T", t_wo[m - 2])] if m >= 2 else []
        lg_wo[m] = b.dma("gpsimd", wocol[i].rearrange("p (c q) -> p c q", q=128),
                         wo_t[:, m].rearrange("c p q -> p c q"), f"WO{i}", waits=w)
        for th in range(2):
            for dc in range(NC):
                t_wo[m] = b.t(lambda e, i=i, th=th, dc=dc: e.matmul(
                    pbk[4 + th][:, :], wocol[i][:, dc * 128:(dc + 1) * 128],
                    outfm[:, dc * 1024 + th * 512: dc * 1024 + (th + 1) * 512],
                    start=(dc == 0), stop=(dc == NC - 1)),
                    waits=([(f"WO{i}", lg_wo[m]), ("A", A_ATT)] +
                           ([("V", v_wo[m - 1])] if m >= 1 else []))
                    if (th == 0 and dc == 0) else ())
        for th in range(2):
            wv_ = [("T", t_wo[m])] if th == 0 else []
            if th == 0 and m >= 1:
                wv_.append(("WOS", so_wo[m - 1]))
            v_wo[m] = b.ve(lambda e, th=th: e.tensor_copy(
                attf[:, th * 512:(th + 1) * 512], pbk[4 + th][:, :]), waits=wv_)
        a_bo = b.ve(lambda e, m=m: e.tensor_scalar(
            attf, attf, boc_sb[:, m:m + 1], None, ALU.add, ALU.bypass))
        so_wo[m] = b._add("sync",
                          (lambda e, m=m: e.dma_start(
                              out=attout_d[m * 128:(m + 1) * 128, :], in_=attf)),
                          [("V", a_bo)], "WOS", 16)
    SO_WO = so_wo[NC - 1]

    b.bar()

    DBG_QFM = 0
    if debug:
        b.dma("sync", dbg["d_attout"], attout_d, "DBG", waits=[("WOS", SO_WO)])
        b.dma("sync", dbg["d_ksel"], ksel_d, "DBG", waits=[("KS", SO_KS)])
        b.dma("sync", dbg["d_vsf"], vsf_d, "DBG", waits=[("KS", SO_KS)])
        DBG_QFM = b.dma("sync", dbg["d_qfm"], qfm[:], "DBG", waits=[("V", A_QP)])

    ld_ac = [0, 0]
    sg_x2 = [0] * NTO
    sg_n2 = [0] * NTO
    a_n2 = [0] * NTO
    last_at = [0, 0]
    vadd_bk = [0, 0]
    for kt in range(NTO):
        i = kt % 2
        w = [("WOS", SO_WO)]
        if kt >= 2:
            w.append(("T", last_at[i]))
        ld_ac[i] = b.dma("sync", attcols[i].rearrange("p (m q) -> p m q", q=128),
                         attout_d[:, kt * 128:(kt + 1) * 128].rearrange(
                             "(m p) q -> p m q", p=128), f"AC{i}", waits=w)
        ldx = b.dma("sync", x2t[i], x_in[kt * 128:(kt + 1) * 128, :], f"XR{i}",
                    waits=[(f"X2{i}", sg_x2[kt - 2])] if kt >= 2 else
                          [("WOS", SO_WO)])
        vadd = 0
        for m in range(NC):
            bkx = 2 + m % 2
            t_at = b.t(lambda e, i=i, m=m, bkx=bkx: e.transpose(
                pbk[bkx][:, 0:128], attcols[i][:, m * 128:(m + 1) * 128], identf_sb[:]),
                waits=[(f"AC{i}", ld_ac[i]),
                       ("V", vadd_bk[m % 2] if vadd_bk[m % 2] else b.v["V"])])
            last_at[i] = t_at
            vadd = b.ve(lambda e, i=i, m=m, bkx=bkx: e.tensor_tensor(
                x2t[i][:, m * 128:(m + 1) * 128], x2t[i][:, m * 128:(m + 1) * 128],
                pbk[bkx][:, 0:128], ALU.add),
                waits=[("T", t_at), (f"XR{i}", ldx)])
            vadd_bk[m % 2] = vadd
        for q in range(4):
            b.ve(lambda e, i=i, q=q: e.bn_stats(stat6[i][:, q * 6:(q + 1) * 6],
                                                x2t[i][:, q * 512:(q + 1) * 512]))
        b.ve(lambda e, i=i: e.bn_aggr(mvt[i][:], stat6[i][:]))
        vm = b.ve(lambda e, i=i: e.tensor_copy(mean_sb[:, 0:1], mvt[i][:, 0:1]))
        av2 = b.a(lambda e, i=i: e.activation(std_sb[i][:], mvt[i][:, 1:2], AF.Sqrt,
                                              bias=csts[:, 0:1], scale=1.0),
                  waits=[("V", vm), ("A", b.v["A"])])
        vr2 = b.ve(lambda e, i=i: e.reciprocal(rstd_sb[:, 0:1], std_sb[i][:]),
                   waits=[("A", av2)])
        vn2 = b.ve(lambda e, i=i: e.scalar_tensor_tensor(
            negms_sb[:, 0:1], mvt[i][:, 0:1], -1.0, rstd_sb[:, 0:1],
            ALU.mult, ALU.mult))
        wn = []
        if kt >= 2:
            wn.append((f"N2{i}", sg_n2[kt - 2]))
        if debug and kt < 2:
            wn.append(("DBG", DBG_QFM))
        a_n2[kt] = b.ve(lambda e, i=i: e.tensor_scalar(
            xn2t[i], x2t[i], rstd_sb[:, 0:1], negms_sb[:, 0:1], ALU.mult, ALU.add),
            waits=wn)
        sg_n2[kt] = b._add("sync",
                           (lambda e, i=i, kt=kt: e.dma_start(
                               out=xn2_d[kt * 128:(kt + 1) * 128, :], in_=xn2t[i])),
                           [("V", a_n2[kt])], f"N2{i}", 16)
        sg_x2[kt] = b._add("scalar",
                           (lambda e, i=i, kt=kt: e.dma_start(
                               out=x2_d[kt * 128:(kt + 1) * 128, :], in_=x2t[i])),
                           [("V", a_n2[kt])], f"X2{i}", 16)
    N2TOT = (b.v["N20"], b.v["N21"])
    X2TOT = (b.v["X20"], b.v["X21"])

    b.bar()

    if debug:
        b.dma("gpsimd", dbg["d_x2"], x2_d, "DBG2",
              waits=[("X20", X2TOT[0]), ("X21", X2TOT[1])])
        b.dma("gpsimd", dbg["d_xn2"], xn2_d, "DBG2",
              waits=[("N20", N2TOT[0]), ("N21", N2TOT[1])])

    la_n2 = [0] * NC
    for dc in range(NC):
        la_n2[dc] = b.dmat("scalar", vb(R4 + dc * 2048, 128, 1024),
                           xn2_d[:, dc * 128:(dc + 1) * 128], "XN2F",
                           waits=[("N20", N2TOT[0]), ("N21", N2TOT[1])])
    LA_N2 = b.v["XN2F"]
    t_xs = 0
    for th in range(2):
        for dc in range(NC):
            t_xs = b.t(lambda e, th=th, dc=dc: e.matmul(
                pbk[th][0:1, :], ones_sb[:],
                vb(R4 + dc * 2048 + th * 1024, 128, 512),
                start=(dc == 0), stop=(dc == NC - 1)),
                waits=[("XN2F", LA_N2)] if (th == 0 and dc == 0) else ())
    a_xs = 0
    for th in range(2):
        a_xs = b.a(lambda e, th=th: e.activation(
            xsum_f[:, th * 512:(th + 1) * 512], pbk[th][0:1, :], AF.Copy,
            bias=0.0, scale=1.0), waits=[("T", t_xs)] if th == 0 else ())
    vhi = b.ve(lambda e: e.tensor_copy(xsum_hi[:], xsum_f), waits=[("A", a_xs)])
    vt_ = b.ve(lambda e: e.tensor_tensor(xsum_t, xsum_f, xsum_hi[:], ALU.subtract))
    vlo = b.ve(lambda e: e.tensor_copy(xsum_lo[:], xsum_t))
    V_XS = vlo

    groups = [list(range(g * 8, min(NJC, g * 8 + 8))) for g in range(6)]
    lg_g = 0
    lg_u = 0
    last_gd = 0
    g_d = [0] * 9
    a_sil = 0
    v_hp = [0] * 8
    last_vhp = 0
    t_dn = 0
    v_fl = 0
    t_gu = [0] * NJC
    for gi, grp in enumerate(groups):
        for jj, jc in enumerate(grp):
            jb = jc % 2
            wg_w = [("T", t_gu[jc - 2])] if jc >= 2 else []
            lg_g = b.dma("gpsimd", wgbb[jb], wg_b[jc], f"FG{jb}", waits=wg_w)
            lg_u = b.dma("gpsimd", wubb[jb], wu_b[jc], f"FU{jb}", waits=wg_w)
            ld_w = [("G", last_gd)] if last_gd else []
            ldd = b.dma("sync", wdi, wd_b[jc], "FD", waits=ld_w)
            g_d[jj] = b.g(lambda e, jj=jj, jc=jc: e.tensor_scalar(
                wdbb[jj], wdi, -128.0, su_sb[:, jc:jc + 1], ALU.add, ALU.mult),
                waits=[("FD", ldd)] + ([("T", t_dn)] if gi > 0 else []))
            last_gd = g_d[jj]
            for th in range(2):
                wmm = [(f"FG{jb}", lg_g), ("XN2F", LA_N2), ("V", V_XS)]
                if a_sil:
                    wmm.append(("A", a_sil))
                for dc in range(NC):
                    t_gu[jc] = b.t(lambda e, th=th, dc=dc, jb=jb: e.matmul(
                        pbk[2 + th][:, :], wgbb[jb][:, dc * 128:(dc + 1) * 128],
                        vb(R4 + dc * 2048 + th * 1024, 128, 512),
                        start=(dc == 0), stop=False),
                        waits=wmm if dc == 0 else ())
                t_gu[jc] = b.t(lambda e, th=th: e.matmul(
                    pbk[2 + th][:, :], zpr_sb[:], xsum_hi[:, th * 512:(th + 1) * 512],
                    start=False, stop=False))
                t_gu[jc] = b.t(lambda e, th=th: e.matmul(
                    pbk[2 + th][:, :], zpr_sb[:], xsum_lo[:, th * 512:(th + 1) * 512],
                    start=False, stop=True))
            for th in range(2):
                wsil = []
                if th == 0:
                    wsil = [("T", t_gu[jc])]
                    if last_vhp:
                        wsil.append(("V", last_vhp))
                a_sil = b.a(lambda e, th=th, jc=jc: e.activation(
                    gsil[:, th * 512:(th + 1) * 512], pbk[2 + th][:, :], AF.Silu,
                    bias=0.0, scale=sg_sb[:, jc:jc + 1]), waits=wsil)
            for th in range(2):
                wmm = ([(f"FU{jb}", lg_u)] + ([("V", last_vhp)] if last_vhp else [])) if th == 0 else ()
                for dc in range(NC):
                    t_gu[jc] = b.t(lambda e, th=th, dc=dc, jb=jb: e.matmul(
                        pbk[4 + th][:, :], wubb[jb][:, dc * 128:(dc + 1) * 128],
                        vb(R4 + dc * 2048 + th * 1024, 128, 512),
                        start=(dc == 0), stop=False),
                        waits=wmm if dc == 0 else ())
                t_gu[jc] = b.t(lambda e, th=th: e.matmul(
                    pbk[4 + th][:, :], zpr_sb[:], xsum_hi[:, th * 512:(th + 1) * 512],
                    start=False, stop=False))
                t_gu[jc] = b.t(lambda e, th=th: e.matmul(
                    pbk[4 + th][:, :], zpr_sb[:], xsum_lo[:, th * 512:(th + 1) * 512],
                    start=False, stop=True))
            for th in range(2):
                v_hp[jj] = b.ve(lambda e, jj=jj, th=th: e.tensor_tensor(
                    hpr[jj][:, th * 512:(th + 1) * 512], pbk[4 + th][:, :],
                    gsil[:, th * 512:(th + 1) * 512], ALU.mult),
                    waits=[("T", t_gu[jc]), ("A", a_sil)] if th == 0 else ())
            last_vhp = v_hp[jj]
        for m in range(NC):
            for jj in range(len(grp)):
                wmm = []
                if jj == 0:
                    wmm = [("V", v_hp[len(grp) - 1]), ("G", g_d[len(grp) - 1])]
                    if v_fl:
                        wmm.append(("V", v_fl))
                for th in range(2):
                    t_dn = b.t(lambda e, m=m, th=th, jj=jj,
                               ng=len(grp): e.matmul(
                        pbk[6 + th][:, :], wdbb[jj][:, m * 128:(m + 1) * 128],
                        hpr[jj][:, th * 512:(th + 1) * 512],
                        start=(jj == 0), stop=(jj == ng - 1)),
                        waits=wmm if th == 0 else ())
                    wmm = ()
            for th in range(2):
                if gi == 0:
                    v_fl = b.ve(lambda e, m=m, th=th: e.tensor_scalar(
                        dacc[:, m * 1024 + th * 512: m * 1024 + (th + 1) * 512],
                        pbk[6 + th][:, :], sd_sb[:, m:m + 1], None,
                        ALU.mult, ALU.bypass), waits=[("T", t_dn)] if th == 0 else ())
                else:
                    v_fl = b.ve(lambda e, m=m, th=th: e.scalar_tensor_tensor(
                        dacc[:, m * 1024 + th * 512: m * 1024 + (th + 1) * 512],
                        pbk[6 + th][:, :], sd_sb[:, m:m + 1],
                        dacc[:, m * 1024 + th * 512: m * 1024 + (th + 1) * 512],
                        ALU.mult, ALU.add), waits=[("T", t_dn)] if th == 0 else ())
    V_FFN = b.v["V"]
    if debug:
        b.dma("sync", dbg["d_dacc"], dacc, "DBG", waits=[("V", V_FFN)])

    b.bar()

    so_out = [0] * NTO
    ldx2 = [0, 0]
    vadd_bk = [0, 0]
    for kt in range(NTO):
        i = kt % 2
        w = [("V", V_FFN), ("X20", X2TOT[0]), ("X21", X2TOT[1])]
        if kt >= 2:
            w.append((f"O{i}", so_out[kt - 2]))
        ldx2[i] = b.dma("sync", x2fin[i], x2_d[kt * 128:(kt + 1) * 128, :],
                        f"XF{i}", waits=w)
        vadd = 0
        for m in range(NC):
            bkx = 2 + m % 2
            wt = [("V", vadd_bk[m % 2])] if vadd_bk[m % 2] else []
            if kt == 0 and m < 2:
                wt.append(("V", V_FFN))
            t_ds = b.t(lambda e, m=m, kt=kt, bkx=bkx: e.transpose(
                pbk[bkx][:, 0:128],
                dacc[:, m * 1024 + kt * 128: m * 1024 + (kt + 1) * 128],
                identf_sb[:]), waits=wt)
            t_ds_bk = t_ds
            vadd = b.ve(lambda e, i=i, m=m, bkx=bkx: e.tensor_tensor(
                x2fin[i][:, m * 128:(m + 1) * 128],
                x2fin[i][:, m * 128:(m + 1) * 128],
                pbk[bkx][:, 0:128], ALU.add), waits=[("T", t_ds), (f"XF{i}", ldx2[i])])
            vadd_bk[m % 2] = vadd
        so_out[kt] = b._add("sync",
                            (lambda e, i=i, kt=kt: e.dma_start(
                                out=out_d[kt * 128:(kt + 1) * 128, :], in_=x2fin[i])),
                            [("V", vadd)], f"O{i}", 16)
    fin = [("O0", b.v["O0"]), ("O1", b.v["O1"])]
    if debug:
        fin.append(("DBG", b.v["DBG"]))
        fin.append(("DBG2", b.v["DBG2"]))
    b.ops["sync"].append((None, tuple(fin), None, 0))

    with nc.Block() as block:
        b.emit(block)
    ctx.close()
    return nc, dbg


def prep_inputs(x, Wq, bq, Wk, bk, Wv, bv, Wo, bo, wq_gate, s_gate, wq_up, s_up,
                wq_down, s_down, core):
    xr = np.concatenate([x[core * TPC:], x[:core * TPC]], axis=0)
    xr = np.ascontiguousarray(xr)

    def tile4(W):
        return np.ascontiguousarray(
            W.reshape(NC, 128, NC, 128).transpose(0, 2, 1, 3))

    def rowch(W):
        return np.ascontiguousarray(W.reshape(NC, 128, DM))

    def colsplit(v):
        return np.ascontiguousarray(v.reshape(NC, 128).T)

    wgb = np.ascontiguousarray(
        wq_gate.reshape(NJC, 128, NC, 128).transpose(0, 3, 2, 1)).reshape(NJC, 128, DM)
    wub = np.ascontiguousarray(
        wq_up.reshape(NJC, 128, NC, 128).transpose(0, 3, 2, 1)).reshape(NJC, 128, DM)
    wdb = np.ascontiguousarray(
        wq_down.reshape(DM, NJC, 128).transpose(1, 2, 0)).reshape(NJC, 128, DM)

    e16 = np.zeros((16, 128), np.float32)
    for jj in range(16):
        e16[jj, jj * 8:(jj + 1) * 8] = 1.0
    b8 = np.zeros((128, 128), np.float32)
    l8 = np.zeros((128, 128), np.float32)
    for p in range(128):
        h0 = p // 8
        b8[h0 * 8:(h0 + 1) * 8, p] = 1.0
        l8[h0 * 8: p, p] = 1.0
    sbase = (np.arange(128) % 8 * 1024).astype(np.float32).reshape(128, 1)

    return dict(
        x=xr,
        wq_t=tile4(Wq), wk_t=rowch(Wk), wv_t=rowch(Wv), wo_t=tile4(Wo),
        bq_c=colsplit(bq), bk_c=colsplit(bk), bv_c=colsplit(bv), bo_c=colsplit(bo),
        bk_r=np.ascontiguousarray(bk.reshape(1, DM)),
        bq_r=np.ascontiguousarray(bq.reshape(1, DM)),
        wg_b=wgb, wu_b=wub, wd_b=wdb,
        sg_c=np.ascontiguousarray(s_gate.reshape(NJC, 128).T),
        su_c=np.ascontiguousarray(s_up.reshape(NJC, 128).T),
        sd_c=np.ascontiguousarray(s_down.reshape(NC, 128).T),
        ones_col=np.ones((128, 1), ml_dtypes.bfloat16),
        ident_in=np.eye(128, dtype=ml_dtypes.bfloat16),
        identf_in=np.eye(128, dtype=np.float32),
        e16_in=e16, b8_in=b8, l8_in=l8, sbase_in=sbase,
        zpr_in=np.full((1, 128), -128.0, ml_dtypes.bfloat16),
        onesr_in=np.ones((1, 128), ml_dtypes.bfloat16),
        ident16_in=np.eye(16, dtype=ml_dtypes.bfloat16),
        zr512_in=np.zeros((1, 512), ml_dtypes.bfloat16),
    )


def kernel(L0_attn_hidden_states, L0_attn_Wq, L0_attn_bq, L0_attn_Wk, L0_attn_bk,
           L0_attn_Wv, L0_attn_bv, L0_attn_Wo, L0_attn_bo, L0_ffn_wq_gate,
           L0_ffn_s_gate, L0_ffn_wq_up, L0_ffn_s_up, L0_ffn_wq_down, L0_ffn_s_down):
    from concourse.bass_utils import run_bass_kernel_spmd
    x = np.ascontiguousarray(np.asarray(L0_attn_hidden_states, np.float32).reshape(S, DM))
    args = (x, np.asarray(L0_attn_Wq, np.float32), np.asarray(L0_attn_bq, np.float32),
            np.asarray(L0_attn_Wk, np.float32), np.asarray(L0_attn_bk, np.float32),
            np.asarray(L0_attn_Wv, np.float32), np.asarray(L0_attn_bv, np.float32),
            np.asarray(L0_attn_Wo, np.float32), np.asarray(L0_attn_bo, np.float32),
            np.asarray(L0_ffn_wq_gate, np.int32), np.asarray(L0_ffn_s_gate, np.float32),
            np.asarray(L0_ffn_wq_up, np.int32), np.asarray(L0_ffn_s_up, np.float32),
            np.asarray(L0_ffn_wq_down, np.int32), np.asarray(L0_ffn_s_down, np.float32))
    nc, _ = build(debug=0)
    in_maps = [prep_inputs(*args, core=c) for c in range(NCORES)]
    res = run_bass_kernel_spmd(nc, in_maps, core_ids=list(range(NCORES)))
    out = np.concatenate([np.asarray(res.results[c]["out"]) for c in range(NCORES)],
                         axis=0)
    return out.reshape(1, S, DM)


if __name__ == "__main__":
    nc, _ = build(debug=1)
    n = sum(len(v) for v in nc.m.functions[0].basicblocks[0].instructions) if False else 0
    print("build ok")


# version 09
# speedup vs baseline: 1.0136x; 1.0136x over previous
import numpy as np
import ml_dtypes
import concourse.bass as bass
import concourse.mybir as mybir
from concourse.bass_core_rust import IndirectOffsetOnAxis
from contextlib import ExitStack

F32 = mybir.dt.float32
BF16 = mybir.dt.bfloat16
I32 = mybir.dt.int32
U32 = mybir.dt.uint32
AF = mybir.ActivationFunctionType
ALU = mybir.AluOpType
AX = mybir.AxisListType

S = 8192
DM = 2048
H = 16
TPC = 1024
NT = S // 128
NTO = TPC // 128
NC = DM // 128
J = 64
KP = 8 * J
HID = 5632
NJC = HID // 128
SCALE = float(1.0 / np.sqrt(128.0))
EPS = 1e-5
NEG = -1e30
NCORES = 8

ENGS = ("sync", "tensor", "vector", "scalar", "gpsimd")

R1 = 0
R2 = 65536
R3 = 110592
R4 = 143360
AREND = 176128
ARCOLS = AREND // 4


class Bld:
    def __init__(self, sems):
        self.sems = sems
        self.ops = {e: [] for e in ENGS}
        self.v = {k: 0 for k in sems}

    def _add(self, eng, fn, waits, inc, by):
        self.ops[eng].append((fn, tuple(waits), inc, by))
        if inc:
            self.v[inc] += by
            return self.v[inc]

    def t(self, fn, waits=()):
        return self._add("tensor", fn, waits, "T", 1)

    def ve(self, fn, waits=()):
        w = list(waits)
        if self.v["V"] > 0:
            w.append(("V", self.v["V"]))
        return self._add("vector", fn, w, "V", 1)

    def a(self, fn, waits=()):
        w = list(waits)
        if self.v["A"] > 0:
            w.append(("A", self.v["A"]))
        return self._add("scalar", fn, w, "A", 1)

    def g(self, fn, waits=()):
        w = list(waits)
        if self.v["G"] > 0:
            w.append(("G", self.v["G"]))
        return self._add("gpsimd", fn, w, "G", 1)

    def dma(self, q, out, in_, sem, waits=()):
        def fn(e, out=out, in_=in_):
            return e.dma_start(out=out, in_=in_)
        return self._add(q, fn, waits, sem, 16)

    def dmat(self, q, out, in_, sem, waits=()):
        def fn(e, out=out, in_=in_):
            return e.dma_start(out=out, in_=in_, transpose=True)
        return self._add(q, fn, waits, sem, 16)

    def gather(self, out, table, idx_ap, sem, waits=()):
        def fn(e, out=out, table=table, idx_ap=idx_ap):
            return e.indirect_dma_start(out=out, out_offset=None, in_=table,
                                        in_offset=IndirectOffsetOnAxis(idx_ap, 0))
        return self._add("gpsimd", fn, waits, sem, 16)

    def bar(self):
        snap = [(k, v) for k, v in self.v.items() if v > 0]
        for e in ENGS:
            self.ops[e].append((None, tuple(snap), None, 0))

    def emit(self, block):
        sems = self.sems
        for eng_name in ENGS:
            ops = self.ops[eng_name]

            def mk(ops=ops):
                def body(e):
                    for fn, waits, inc, by in ops:
                        for (sn, val) in waits:
                            if val and val > 0:
                                e.wait_ge(sems[sn], val)
                        if fn is None:
                            continue
                        inst = fn(e)
                        if inc:
                            inst.then_inc(sems[inc], by)
                return body
            getattr(block, eng_name)(mk())


def build(debug=0):
    nc = bass.Bass("TRN2", target_bir_lowering=False)
    ctx = ExitStack()
    E = ctx.enter_context
    ap = lambda t: t.ap()

    x_in = ap(nc.dram_tensor("x", [S, DM], F32, kind="ExternalInput"))
    wq_t = ap(nc.dram_tensor("wq_t", [NC, NC, 128, 128], F32, kind="ExternalInput"))
    wk_t = ap(nc.dram_tensor("wk_t", [NC, 128, DM], F32, kind="ExternalInput"))
    wv_t = ap(nc.dram_tensor("wv_t", [NC, 128, DM], F32, kind="ExternalInput"))
    wo_t = ap(nc.dram_tensor("wo_t", [NC, NC, 128, 128], F32, kind="ExternalInput"))
    bq_c = ap(nc.dram_tensor("bq_c", [128, NC], F32, kind="ExternalInput"))
    bk_c = ap(nc.dram_tensor("bk_c", [128, NC], F32, kind="ExternalInput"))
    bv_c = ap(nc.dram_tensor("bv_c", [128, NC], F32, kind="ExternalInput"))
    bo_c = ap(nc.dram_tensor("bo_c", [128, NC], F32, kind="ExternalInput"))
    bk_r = ap(nc.dram_tensor("bk_r", [1, DM], F32, kind="ExternalInput"))
    bq_r = ap(nc.dram_tensor("bq_r", [1, DM], F32, kind="ExternalInput"))
    wg_b = ap(nc.dram_tensor("wg_b", [NJC, 128, DM], I32, kind="ExternalInput"))
    wu_b = ap(nc.dram_tensor("wu_b", [NJC, 128, DM], I32, kind="ExternalInput"))
    wd_b = ap(nc.dram_tensor("wd_b", [NJC, 128, DM], I32, kind="ExternalInput"))
    sg_c = ap(nc.dram_tensor("sg_c", [128, NJC], F32, kind="ExternalInput"))
    su_c = ap(nc.dram_tensor("su_c", [128, NJC], F32, kind="ExternalInput"))
    sd_c = ap(nc.dram_tensor("sd_c", [128, NC], F32, kind="ExternalInput"))
    ones_col = ap(nc.dram_tensor("ones_col", [128, 1], BF16, kind="ExternalInput"))
    ident_in = ap(nc.dram_tensor("ident_in", [128, 128], BF16, kind="ExternalInput"))
    identf_in = ap(nc.dram_tensor("identf_in", [128, 128], F32, kind="ExternalInput"))
    e16_in = ap(nc.dram_tensor("e16_in", [16, 128], F32, kind="ExternalInput"))
    b8_in = ap(nc.dram_tensor("b8_in", [128, 128], F32, kind="ExternalInput"))
    l8_in = ap(nc.dram_tensor("l8_in", [128, 128], F32, kind="ExternalInput"))
    sbase_in = ap(nc.dram_tensor("sbase_in", [128, 1], F32, kind="ExternalInput"))
    zpr_in = ap(nc.dram_tensor("zpr_in", [1, 128], BF16, kind="ExternalInput"))
    onesr_in = ap(nc.dram_tensor("onesr_in", [1, 128], BF16, kind="ExternalInput"))
    ident16_in = ap(nc.dram_tensor("ident16_in", [16, 16], BF16, kind="ExternalInput"))
    zr512_in = ap(nc.dram_tensor("zr512_in", [1, 512], BF16, kind="ExternalInput"))

    out_d = ap(nc.dram_tensor("out", [TPC, DM], F32, kind="ExternalOutput"))

    xnsp = ap(nc.dram_tensor("xnsp", [S, DM], BF16, kind="Internal"))
    wkb_d = ap(nc.dram_tensor("wkb_d", [DM, DM], BF16, kind="Internal"))
    wvb_d = ap(nc.dram_tensor("wvb_d", [DM, DM], BF16, kind="Internal"))
    gdram = ap(nc.dram_tensor("gdram", [S, DM], BF16, kind="Internal"))
    wtt_d = ap(nc.dram_tensor("wtt_d", [16, DM], BF16, kind="Internal"))
    ch_d = ap(nc.dram_tensor("ch_d", [16, 1], F32, kind="Internal"))
    impb_d = ap(nc.dram_tensor("impb_d", [16, S], F32, kind="Internal"))
    candv_d = ap(nc.dram_tensor("candv_d", [16, 8 * J], F32, kind="Internal"))
    mbias_d = ap(nc.dram_tensor("mbias_d", [H, KP], BF16, kind="Internal"))
    wqb_d = ap(nc.dram_tensor("wqb_d", [DM, DM], BF16, kind="Internal"))
    ksel_d = ap(nc.dram_tensor("ksel_d", [H, 4, 128, 128], BF16, kind="Internal"))
    vsf_d = ap(nc.dram_tensor("vsf_d", [H, 4, 128, 128], BF16, kind="Internal"))
    attout_d = ap(nc.dram_tensor("attout_d", [DM, TPC], F32, kind="Internal"))
    x2_d = ap(nc.dram_tensor("x2_d", [TPC, DM], F32, kind="Internal"))
    xn2_d = ap(nc.dram_tensor("xn2_d", [TPC, DM], BF16, kind="Internal"))

    dbg = {}
    if debug:
        def dd(nm, shape, dt=F32):
            dbg[nm] = ap(nc.dram_tensor(nm, shape, dt, kind="ExternalOutput"))
        dd("d_mean", [128, NT]); dd("d_rstd", [128, NT]); dd("d_xbar", [1, DM])
        dd("d_qbar", [1, DM]); dd("d_wtt", [16, DM], BF16); dd("d_ch", [1, 16])
        dd("d_imp", [16, S]); dd("d_tau", [128, 1]); dd("d_valid", [128, J])
        dd("d_s1v", [128, J]); dd("d_s1g", [128, J], U32)
        dd("d_dacc", [128, NC * TPC])
        dd("d_attout", [DM, TPC]); dd("d_x2", [TPC, DM])
        dd("d_xn2", [TPC, DM], BF16); dd("d_ksel", [H, 4, 128, 128], BF16)
        dd("d_vsf", [H, 4, 128, 128], BF16); dd("d_qfm", [128, NC * TPC], BF16)

    sems = {}
    semnames = (["T", "V", "A", "G", "CST", "CST2", "CH", "WT0", "WT1", "CAND", "MB",
                 "KS", "XOF", "XN2F", "WOS", "IML", "WTLD", "DBG2", "QR0", "QR1",
                 "D0", "D1", "D2", "D3", "SP0", "SP1", "SP2", "SP3",
                 "W0", "W1", "WS0", "WS1",
                 "TK0", "TK1", "TK2", "TK3", "IM0", "IM1", "KV0", "KV1",
                 "G0", "G1", "G2", "G3", "S0", "S1", "S2", "S3",
                 "Q0", "Q1", "KH0", "KH1", "VH0", "VH1", "MH0", "MH1",
                 "WO0", "WO1", "AC0", "AC1", "XR0", "XR1", "DBG",
                 "N20", "N21", "X20", "X21", "FG", "FU", "FD",
                 "XF0", "XF1", "O0", "O1", "FG0", "FG1", "FU0", "FU1", "PV0", "PV1"])
    for sn in semnames:
        sems[sn] = E(nc.semaphore(sn))
    b = Bld(sems)

    AR = E(nc.sbuf_tensor("AR", [128, ARCOLS], F32))

    def vf(off, p, n):
        assert off % 4 == 0
        return AR[0:p, off // 4: off // 4 + n]

    def vb(off, p, n):
        assert off % 4 == 0 and n % 2 == 0
        return AR[0:p, off // 4: off // 4 + n // 2].bitcast(BF16)

    def vi(off, p, n):
        return AR[0:p, off // 4: off // 4 + n].bitcast(I32)

    sb = lambda name, shape, dt: E(nc.sbuf_tensor(name, shape, dt))
    stat6 = [sb(f"stat6_{i}", [128, 24], F32) for i in range(4)]
    mvt = [sb(f"mv{i}", [128, 2], F32) for i in range(4)]
    scr1 = vb(R1 + 57344, 128, 2048)
    mean_sb = sb("mean_sb", [128, NT], F32)
    rstd_sb = sb("rstd_sb", [128, NT], F32)
    negms_sb = sb("negms_sb", [128, NT], F32)
    std_sb = [sb(f"std_sb{i}", [128, 1], F32) for i in range(4)]
    ones_sb = sb("ones_sb", [128, 1], BF16)
    ident_sb = sb("ident_sb", [128, 128], BF16)
    identf_sb = sb("identf_sb", [128, 128], F32)
    e16_sb = sb("e16_sb", [16, 128], F32)
    b8_sb = sb("b8_sb", [128, 128], F32)
    l8_sb = sb("l8_sb", [128, 128], F32)
    sbase_sb = sb("sbase_sb", [128, 1], F32)
    zpr_sb = sb("zpr_sb", [1, 128], BF16)
    onesr_sb = sb("onesr_sb", [1, 128], BF16)
    ident16_sb = sb("ident16_sb", [16, 16], BF16)
    zr_sb = sb("zr_sb", [1, 512], BF16)
    bkc_sb = sb("bkc_sb", [128, NC], F32)
    bvc_sb = sb("bvc_sb", [128, NC], F32)
    bqc_sb = sb("bqc_sb", [128, NC], F32)
    boc_sb = sb("boc_sb", [128, NC], F32)
    sg_sb = sb("sg_sb", [128, NJC], F32)
    su_sb = sb("su_sb", [128, NJC], F32)
    sd_sb = sb("sd_sb", [128, NC], F32)
    xbcol = sb("xbcol", [128, NC], BF16)
    qcol = sb("qcol", [128, NC], BF16)
    wtil = sb("wtil", [128, NC * 16], BF16)
    wtt16 = sb("wtt16", [16, DM], BF16)
    chrow = sb("chrow", [1, 16], F32)
    ch16 = sb("ch16", [16, 1], F32)
    impg_s = [sb(f"impg_s{i}", [16, 512], F32) for i in range(2)]
    imp_hs = sb("imp_hs", [128, 1024], F32)
    s1v = sb("s1v", [128, J], F32)
    s1i = sb("s1i", [128, J], U32)
    s1g = sb("s1g", [128, J], U32)
    m8t = sb("m8t", [128, 8], F32)
    cand = sb("cand", [16, 8 * J], F32)
    m8b = sb("m8b", [16, 8], F32)
    tau16 = sb("tau16", [16, 1], F32)
    tau128 = sb("tau128", [128, 1], F32)
    strict = sb("strict", [128, J], F32)
    tiem = sb("tiem", [128, J], F32)
    scnt = sb("scnt", [128, 1], F32)
    tcnt = sb("tcnt", [128, 1], F32)
    c1h = sb("c1h", [128, 1], F32)
    quota = sb("quota", [128, 1], F32)
    tieoff = sb("tieoff", [128, 1], F32)
    tiescan = sb("tiescan", [128, J], F32)
    zcol = sb("zcol", [128, J], F32)
    tierank = sb("tierank", [128, J], F32)
    tieok = sb("tieok", [128, J], F32)
    valid = sb("valid", [128, J], F32)
    mb128 = sb("mb128", [128, J], BF16)
    smax = sb("smax", [128, 1], F32)
    ssum = sb("ssum", [128, 1], F32)
    srec = sb("srec", [128, 1], F32)
    xsum_hi = sb("xsum_hi", [1, TPC], BF16)
    csts = sb("csts", [128, 3], F32)
    xsum_lo = sb("xsum_lo", [1, TPC], BF16)

    xt = [vf(R1 + i * 8192, 128, 2048) for i in range(4)]
    xnt = [vb(R4 + i * 4096, 128, 2048) for i in range(4)]
    wrow = [vb(R2 + i * 4096, 128, 2048) for i in range(2)]
    wkt_row = [vb(R2 + 8192 + i * 4096, 128, 2048) for i in range(4)]
    qrow = [vb(R2 + 24576 + i * 4096, 128, 2048) for i in range(2)]
    xbar_sb = vf(R1 + 0, 1, 2048)
    xbar16 = vb(R1 + 8192, 1, 2048)
    qbar_sb = vf(R1 + 16384, 1, 2048)
    qbar16 = vb(R1 + 24576, 1, 2048)
    prodr = vf(R1 + 32768, 1, 2048)
    bkr_row = vf(R1 + 40960, 1, 2048)
    bqr_row = vf(R1 + 49152, 1, 2048)
    bq16row = vb(R1 + 28672, 1, 2048)
    ones512 = vb(R2 + 32768, 1, 512)
    wttrow = [vb(R1 + 57344, 1, 2048), vb(R1 + 61440, 1, 2048)]
    fmx = [vb(R3 + i * 16384, 128, 8192) for i in range(2)]
    wkv = [vb(R2 + i * 4096, 128, 2048) for i in range(2)]
    gt = [vb(R2 + 8192 + i * 4096, 128, 2048) for i in range(4)]
    fmc = [vb(R2 + 24576 + i * 4096, 128, 2048) for i in range(2)]
    kstg = vb(R2 + 32768, 128, 2048)
    xnofm = vb(R3 + 0, 128, 16384)
    wqcol = [vb(R2 + i * 4096, 128, 2048) for i in range(2)]
    qfm = vb(R4 + 0, 128, 16384)
    outfm = vb(R3 + 0, 128, 16384)
    kselb = [vb(R2 + 8192 + i * 1024, 128, 512) for i in range(2)]
    vkmb = [vb(R2 + 16384 + i * 1024, 128, 512) for i in range(2)]
    mbrow = [vb(R2 + 24576 + i * 1024, 1, 512) for i in range(2)]
    wsm = vb(R2 + 28672, 128, 512)
    wTt = vb(R2 + 30720, 128, 512)
    ssub = vf(R2 + 32768, 128, 512)
    sexp = vb(R2 + 34816, 128, 512)
    wocol = [vb(R2 + i * 4096, 128, 2048) for i in range(2)]
    attf = vf(R2 + 8192, 128, 1024)
    attcols = [vf(R2 + 12288 + i * 8192, 128, 2048) for i in range(2)]
    x2t = [vf(R1 + i * 8192, 128, 2048) for i in range(2)]
    xn2t = [vb(R4 + i * 4096, 128, 2048) for i in range(2)]
    dacc = vf(R1 + 0, 128, 16384)
    xn2fm = vb(R4 + 0, 128, 16384)
    wdbb = [vb(R2 + i * 4096, 128, 2048) for i in range(8)]
    wgbb = [vb(R2 + (8 + i) * 4096, 128, 2048) for i in range(2)]
    wubb = [vb(R2 + 10 * 4096, 128, 2048), vb(R3 + 28672, 128, 2048)]
    hpr = [vb(R3 + i * 2048, 128, 1024) for i in range(8)]
    gsil = vb(R3 + 16384, 128, 1024)
    dscs = vb(R3 + 18432, 128, 1024)
    wdi = vi(R3 + 20480, 128, 2048)
    xsum_f = vf(R3 + 0, 1, 1024)
    xsum_t = vf(R3 + 4096, 1, 1024)
    x2fin = [vf(R4 + i * 8192, 128, 2048) for i in range(2)]

    pbk = [E(nc.psum_tensor(f"pbk{i}", [128, 512], F32)) for i in range(8)]

    def pbf(i):
        return pbk[i][:]

    def pbb(i, c0=0, n=512):
        assert c0 % 2 == 0 and n % 2 == 0
        return pbk[i][:, c0 // 2: (c0 + n) // 2].bitcast(BF16)

    for dst, src in [(ones_sb[:], ones_col), (ident_sb[:], ident_in),
                     (identf_sb[:], identf_in), (e16_sb[:], e16_in),
                     (b8_sb[:], b8_in), (l8_sb[:], l8_in),
                     (sbase_sb[:], sbase_in), (zpr_sb[:], zpr_in),
                     (onesr_sb[:], onesr_in), (ident16_sb[:], ident16_in),
                     (zr_sb[:], zr512_in),
                     (bkc_sb[:], bk_c), (bvc_sb[:], bv_c),
                     (bqc_sb[:], bq_c), (boc_sb[:], bo_c),
                     (sg_sb[:], sg_c), (su_sb[:], su_c), (sd_sb[:], sd_c)]:
        b.dma("sync", dst, src, "CST")
    LDC = b.v["CST"]
    vz = b.ve(lambda e: e.memset(zcol[:], 0.0))
    for q in range(4):
        b.ve(lambda e, q=q: e.tensor_copy(ones512[0:1, q * 128:(q + 1) * 128],
                                          onesr_sb[:]), waits=[("CST", LDC)] if q == 0 else ())
    vz = b.ve(lambda e: e.memset(csts[:, 0:1], EPS))
    vz = b.ve(lambda e: e.memset(csts[:, 1:2], 1.0 / S))
    VCST = b.ve(lambda e: e.memset(csts[:, 2:3], SCALE))

    sgk = [0] * (3 * NC)
    for which, (srcs, wdst) in enumerate(
            [([wq_t[dc].rearrange("m p q -> p m q") for dc in range(NC)], wqb_d),
             ([wk_t[dc] for dc in range(NC)], wkb_d),
             ([wv_t[dc] for dc in range(NC)], wvb_d)]):
        for dc in range(NC):
            idx = which * NC + dc
            i = idx % 2
            w = [] if idx < 2 else [(f"WS{i}", sgk[idx - 2])]
            lgv = b.dma("gpsimd", wrow[i], srcs[dc], f"W{i}", waits=w)
            sgk[idx] = b._add("gpsimd",
                              (lambda e, i=i, dc=dc, wdst=wdst: e.dma_start(
                                  out=wdst[dc * 128:(dc + 1) * 128, :], in_=wrow[i])),
                              [(f"W{i}", lgv)], f"WS{i}", 16)
    WQS = (sgk[NC - 2], sgk[NC - 1])
    WKS = (sgk[2 * NC - 2], sgk[2 * NC - 1])
    WSTOT = (b.v["WS0"], b.v["WS1"])

    a_xn = [0] * NT
    sg_sp = [0] * NT
    t_xb = [0] * NT

    def emit_spill(k):
        i = k % 4
        sg_sp[k] = b._add("scalar" if k % 2 == 0 else "sync",
                          (lambda e, i=i, k=k: e.dma_start(
                              out=xnsp[k * 128:(k + 1) * 128, :], in_=xnt[i])),
                          [("V", a_xn[k])], f"SP{i}", 16)

    for k in range(NT):
        i = k % 4
        w = [("V", a_xn[k - 4])] if k >= 4 else []
        ldv = b.dma("sync" if k % 2 == 0 else "scalar",
                    xt[i], x_in[k * 128:(k + 1) * 128, :], f"D{i}", waits=w)
        if k >= 4:
            emit_spill(k - 4)
        if k % 3 != 2:
            asum = b.a(lambda e, i=i: e.activation(scr1, xt[i], AF.Copy,
                                                   bias=0.0, scale=1.0,
                                                   accum_out=mvt[i][:, 0:1]),
                       waits=[(f"D{i}", ldv)])
            asq = b.a(lambda e, i=i: e.activation(scr1, xt[i], AF.Square,
                                                  bias=0.0, scale=1.0,
                                                  accum_out=mvt[i][:, 1:2]))
            vm = b.ve(lambda e, i=i, k=k: e.tensor_scalar(
                mean_sb[:, k:k + 1], mvt[i][:, 0:1], 1.0 / DM, None,
                ALU.mult, ALU.bypass), waits=[("A", asum)])
            vv_ = b.ve(lambda e, i=i, k=k: e.scalar_tensor_tensor(
                std_sb[i][:], mean_sb[:, k:k + 1], -1.0, mean_sb[:, k:k + 1],
                ALU.mult, ALU.mult))
            vv_ = b.ve(lambda e, i=i: e.scalar_tensor_tensor(
                std_sb[i][:], mvt[i][:, 1:2], 1.0 / DM, std_sb[i][:],
                ALU.mult, ALU.add), waits=[("A", asq)])
            av = b.a(lambda e, i=i: e.activation(std_sb[i][:], std_sb[i][:], AF.Sqrt,
                                                 bias=csts[:, 0:1], scale=1.0),
                     waits=[("V", vv_), ("V", VCST)])
        else:
            for q in range(4):
                b.ve(lambda e, i=i, q=q: e.bn_stats(
                    stat6[i][:, q * 6:(q + 1) * 6], xt[i][:, q * 512:(q + 1) * 512]),
                    waits=[(f"D{i}", ldv)] if q == 0 else ())
            b.ve(lambda e, i=i: e.bn_aggr(mvt[i][:], stat6[i][:]))
            vm = b.ve(lambda e, i=i, k=k: e.tensor_copy(mean_sb[:, k:k + 1],
                                                        mvt[i][:, 0:1]))
            av = b.a(lambda e, i=i: e.activation(std_sb[i][:], mvt[i][:, 1:2], AF.Sqrt,
                                                 bias=csts[:, 0:1], scale=1.0),
                     waits=[("V", vm), ("V", VCST)])
        b.ve(lambda e, i=i, k=k: e.reciprocal(rstd_sb[:, k:k + 1], std_sb[i][:]),
             waits=[("A", av)])
        vn = b.ve(lambda e, i=i, k=k: e.scalar_tensor_tensor(
            negms_sb[:, k:k + 1], mean_sb[:, k:k + 1], -1.0, rstd_sb[:, k:k + 1],
            ALU.mult, ALU.mult))
        wn = []
        if k >= 4:
            wn += [(f"SP{i}", sg_sp[k - 4]), ("T", t_xb[k - 4])]
        axn = b.ve(lambda e, i=i, k=k: e.tensor_scalar(
            xnt[i], xt[i], rstd_sb[:, k:k + 1], negms_sb[:, k:k + 1],
            ALU.mult, ALU.add), waits=wn)
        a_xn[k] = axn
        for q in range(4):
            t_xb[k] = b.t(lambda e, i=i, q=q, k=k: e.matmul(
                pbk[q][0:1, :], ones_sb[:], xnt[i][:, q * 512:(q + 1) * 512],
                start=(k == 0), stop=(k == NT - 1)),
                waits=([("V", axn)] + ([("CST", LDC)] if k == 0 else [])) if q == 0 else ())
    for k in range(NT - 4, NT):
        emit_spill(k)
    T_P1 = b.v["T"]

    av = 0
    for q in range(4):
        av = b.ve(lambda e, q=q: e.tensor_scalar(
            xbar_sb[:, q * 512:(q + 1) * 512], pbk[q][0:1, :], 1.0 / S, None,
            ALU.mult, ALU.bypass), waits=[("T", T_P1)] if q == 0 else ())
    A_XB = av
    vx16 = b.ve(lambda e: e.tensor_copy(xbar16, xbar_sb))
    t_tr = 0
    for c in range(NC):
        t_tr = b.t(lambda e, c=c: e.matmul(pbk[4][:, c:c + 1],
                                           xbar16[:, c * 128:(c + 1) * 128],
                                           onesr_sb[:, 0:1], start=True, stop=True),
                   waits=[("V", vx16)] if c == 0 else ())
    vxb = b.ve(lambda e: e.tensor_copy(xbcol[:], pbk[4][:, 0:NC]),
               waits=[("T", t_tr)])

    if debug:
        b.dma("sync", dbg["d_mean"], mean_sb[:], "DBG", waits=[("V", b.v["V"])])
        b.dma("sync", dbg["d_rstd"], rstd_sb[:], "DBG", waits=[("V", b.v["V"])])
        b.dma("sync", dbg["d_xbar"], xbar_sb, "DBG", waits=[("V", av)])

    ldbq = b.dma("sync", bqr_row, bq_r, "CST2")
    ldbk = b.dma("sync", bkr_row, bk_r, "CST2")
    VBQ16 = b.ve(lambda e: e.tensor_copy(bq16row, bqr_row), waits=[("CST2", 32)])
    lg_q = [0] * NC
    t_q = [0] * NC
    for dc in range(NC):
        i = dc % 2
        w = [("T", t_q[dc - 2])] if dc >= 2 else [("WS0", WQS[0]), ("WS1", WQS[1])]
        lg_q[dc] = b.dma("sync", qrow[i], wqb_d[dc * 128:(dc + 1) * 128, :],
                         f"QR{i}", waits=w)
        for q in range(4):
            t_q[dc] = b.t(lambda e, i=i, q=q, dc=dc: e.matmul(
                pbk[q][0:1, :], xbcol[:, dc:dc + 1], qrow[i][:, q * 512:(q + 1) * 512],
                start=(dc == 0), stop=(dc == NC - 1)),
                waits=[(f"QR{i}", lg_q[dc]), ("V", vxb)] if q == 0 else ())
    aq = 0
    for q in range(4):
        aq = b.a(lambda e, q=q: e.activation(
            qbar_sb[:, q * 512:(q + 1) * 512], pbk[q][0:1, :], AF.Copy,
            bias=0.0, scale=1.0), waits=[("T", t_q[NC - 1])] if q == 0 else ())
    vqb = b.ve(lambda e: e.tensor_tensor(qbar_sb, qbar_sb, bqr_row, ALU.add),
               waits=[("A", aq), ("CST2", 32)])
    vq16 = b.ve(lambda e: e.tensor_copy(qbar16, qbar_sb))
    t_tr = 0
    for c in range(NC):
        t_tr = b.t(lambda e, c=c: e.matmul(pbk[4][:, 32 + c:33 + c],
                                           qbar16[:, c * 128:(c + 1) * 128],
                                           onesr_sb[:, 0:1], start=True, stop=True),
                   waits=[("V", vq16), ("V", vxb)] if c == 0 else ())
    vqc = b.ve(lambda e: e.tensor_copy(qcol[:], pbk[4][:, 32:32 + NC]),
               waits=[("T", t_tr)])
    vpr = b.ve(lambda e: e.tensor_tensor(prodr, qbar_sb, bkr_row, ALU.mult))
    vch = b.ve(lambda e: e.tensor_reduce(
        chrow[:].rearrange("a (h q) -> a h q", q=1),
        prodr.rearrange("a (h q) -> a h q", q=128), AX.X, ALU.add))
    chst = b.dma("sync", ch_d, chrow[:], "CH", waits=[("V", vch)])
    chld = b.dma("sync", ch16[:], ch_d, "CH", waits=[("CH", chst)])

    if debug:
        b.dma("sync", dbg["d_qbar"], qbar_sb, "DBG", waits=[("V", vqb)])
        b.dma("sync", dbg["d_ch"], chrow[:], "DBG", waits=[("V", vch)])


    a_wt = [0] * NC
    so_wt = [0] * NC
    t_wt = [0] * NC
    for jc in range(NC):
        i = jc % 4
        w = [("WS0", WKS[0]), ("WS1", WKS[1])]
        if jc >= 4:
            w.append(("T", t_wt[jc - 4]))
        lav = b.dmat("scalar",
                     wkt_row[i], wkb_d[:, jc * 128:(jc + 1) * 128],
                     f"TK{i}", waits=w)
        for q in range(4):
            wmm = [(f"TK{i}", lav), ("V", vqc)] if q == 0 else ()
            if jc > 1 and q == 0:
                wmm = list(wmm) + [(f"WT{jc % 2}", so_wt[jc - 2])]
            t_wt[jc] = b.t(lambda e, i=i, q=q, jc=jc: e.matmul(
                pbk[q][0:1, :], qcol[:, jc:jc + 1], wkt_row[i][:, q * 512:(q + 1) * 512],
                start=True, stop=True), waits=wmm)
        for q in range(4):
            a_wt[jc] = b.a(lambda e, q=q, jc=jc: e.activation(
                wttrow[jc % 2][:, q * 512:(q + 1) * 512], pbk[q][0:1, :],
                AF.Copy, bias=0.0, scale=1.0),
                waits=[("T", t_wt[jc])] if q == 0 else ())
        so_wt[jc] = b._add("sync",
                           (lambda e, jc=jc: e.dma_start(
                               out=wtt_d[jc:jc + 1, :], in_=wttrow[jc % 2])),
                           [("A", a_wt[jc])], f"WT{jc % 2}", 16)
    wtld = b.dma("sync", wtt16[:], wtt_d, "WTLD",
                 waits=[("WT0", so_wt[NC - 2]), ("WT1", so_wt[NC - 1])])
    t_tr = 0
    for dc in range(NC):
        t_tr = b.t(lambda e, dc=dc: e.matmul(
            pbk[5][:, dc * 16:(dc + 1) * 16],
            wtt16[:, dc * 128:(dc + 1) * 128], ident16_sb[:],
            start=True, stop=True),
            waits=[("WTLD", wtld)] if dc == 0 else ())
    vwt = b.ve(lambda e: e.tensor_copy(wtil[:], pbk[5][:, 0:NC * 16]),
               waits=[("T", t_tr)])
    if debug:
        b.dma("sync", dbg["d_wtt"], wtt16[:], "DBG", waits=[("WTLD", wtld)])

    b.bar()

    t_imp = [0] * NC
    SPTOT = tuple(b.v[f"SP{j}"] for j in range(4))
    for dc in range(NC):
        i = dc % 2
        w = [(f"SP{j}", SPTOT[j]) for j in range(4)]
        if dc >= 2:
            w.append(("T", t_imp[dc - 2]))
        else:
            w.append(("T", t_wt[NC - 1]))
        lav = b.dmat("scalar",
                     fmx[i], xnsp[:, dc * 128:(dc + 1) * 128],
                     f"TK{i}", waits=w)
        for mg in range(16):
            bank, pos = mg % 8, 32 * (mg // 8)
            t_imp[dc] = b.t(lambda e, i=i, dc=dc, mg=mg, bank=bank, pos=pos: e.matmul(
                pbk[bank][pos:pos + 16, :], wtil[:, dc * 16:(dc + 1) * 16],
                fmx[i][:, mg * 512:(mg + 1) * 512],
                start=(dc == 0), stop=(dc == NC - 1),
                tile_position=(0, pos), skip_group_check=True),
                waits=[(f"TK{i}", lav)] if mg == 0 else ())
    so_imp = [0, 0]
    a_imp = 0
    for mg in range(16):
        bank, pos = mg % 8, 32 * (mg // 8)
        i = mg % 2
        w = [("T", t_imp[NC - 1]), ("CH", chld)] if mg == 0 else []
        if mg >= 2:
            w.append((f"IM{mg % 2}", so_imp[mg % 2]))
        a_imp = b.ve(lambda e, mg=mg, bank=bank, pos=pos, i=i: e.tensor_scalar(
            impg_s[i][:], pbk[bank][pos:pos + 16, :], ch16[:], None,
            ALU.add, ALU.bypass), waits=w)
        so_imp[mg % 2] = b._add("sync",
                                (lambda e, mg=mg, i=i: e.dma_start(
                                    out=impb_d[:, mg * 512:(mg + 1) * 512],
                                    in_=impg_s[i][:])),
                                [("V", a_imp)], f"IM{mg % 2}", 16)
    A_IMP = a_imp
    IMTOT = (b.v["IM0"], b.v["IM1"])
    if debug:
        b.dma("sync", dbg["d_imp"], impb_d, "DBG",
              waits=[("IM0", IMTOT[0]), ("IM1", IMTOT[1])])
    impld = b.dma("sync", imp_hs[:],
                  impb_d.rearrange("h (s f) -> h s f", s=8), "IML",
                  waits=[("IM0", IMTOT[0]), ("IM1", IMTOT[1])])

    PRED_VS1G = b.v["V"] + 32
    so_g = [0] * J
    for j in range(J):
        i = j % 4
        w = [("V", PRED_VS1G)] + [(f"SP{j}", SPTOT[j]) for j in range(4)]
        if j >= 4:
            w.append((f"S{i}", so_g[j - 4]))
        lgv = b.gather(gt[i], xnsp, s1g[:, j:j + 1], f"G{i}", waits=w)
        so_g[j] = b._add("sync",
                         (lambda e, j=j, i=i: e.dma_start(
                             out=gdram[j * 128:(j + 1) * 128, :], in_=gt[i])),
                         [(f"G{i}", lgv)], f"S{i}", 16)
    STOT = [b.v[f"S{i}"] for i in range(4)]
    W_GD = [(f"S{i}", STOT[i]) for i in range(4)]

    la_xo = [0] * NC
    for dc in range(NC):
        la_xo[dc] = b.dmat("scalar", vb(R3 + dc * 2048, 128, 1024),
                           xnsp[0:TPC, dc * 128:(dc + 1) * 128], "XOF",
                           waits=[("T", t_imp[NC - 1])] +
                                 [(f"SP{j}", SPTOT[j]) for j in range(4)]
                           if dc == 0 else ())
    LA_XO = b.v["XOF"]
    lg_wq = [0] * NC
    t_qp = [0] * NC
    a_qp = [0] * NC
    for m in range(NC):
        i = m % 2
        w = [("T", t_qp[m - 2])] if m >= 2 else [("T", t_imp[NC - 1])]
        lg_wq[m] = b.dma("gpsimd", wqcol[i].rearrange("p (c q) -> p c q", q=128),
                         wqb_d[:, m * 128:(m + 1) * 128].rearrange(
                             "(c p) q -> p c q", p=128), f"Q{i}", waits=w)
        for th in range(2):
            for dc in range(NC):
                t_qp[m] = b.t(lambda e, i=i, m=m, th=th, dc=dc: e.matmul(
                    pbk[4 + th][:, :], wqcol[i][:, dc * 128:(dc + 1) * 128],
                    vb(R3 + dc * 2048 + th * 1024, 128, 512),
                    start=(dc == 0), stop=False),
                    waits=([(f"Q{i}", lg_wq[m]), ("XOF", LA_XO)] +
                           ([("A", a_qp[m - 1])] if m >= 1 else [("V", A_IMP)]))
                    if (th == 0 and dc == 0) else ())
            t_qp[m] = b.t(lambda e, m=m, th=th: e.matmul(
                pbk[4 + th][:, :], bq16row[0:1, m * 128:(m + 1) * 128],
                ones512[0:1, :], start=False, stop=True),
                waits=[("V", VBQ16)] if (m == 0 and th == 0) else ())
        for th in range(2):
            a_qp[m] = b.a(lambda e, m=m, th=th: e.activation(
                qfm[:, m * 1024 + th * 512: m * 1024 + (th + 1) * 512],
                pbk[4 + th][:, :], AF.Copy, bias=0.0, scale=1.0),
                waits=[("T", t_qp[m])] if th == 0 else ())
    A_QP = a_qp[NC - 1]

    vv = 0
    for r in range(8):
        vv = b.ve(lambda e: e.max(m8t[:], imp_hs[:]),
                  waits=[("IML", impld)] if r == 0 else ())
        vv = b.ve(lambda e, r=r: e.max_index(s1i[:, 8 * r:8 * r + 8], m8t[:], imp_hs[:]))
        vv = b.ve(lambda e, r=r: e.tensor_copy(s1v[:, 8 * r:8 * r + 8], m8t[:]))
        if r < 7:
            vv = b.ve(lambda e: e.match_replace(imp_hs[:], m8t[:], imp_hs[:], NEG))
    vs1g = b.ve(lambda e: e.tensor_scalar(s1g[:], s1i[:], sbase_sb[:], None,
                                          ALU.add, ALU.bypass))
    assert vs1g == PRED_VS1G, (vs1g, PRED_VS1G)
    cvst = b.dma("scalar", candv_d.rearrange("h (s j) -> h s j", s=8), s1v[:],
                 "CAND", waits=[("V", b.v["V"])])
    cvld = b.dma("scalar", cand[:], candv_d, "CAND", waits=[("CAND", cvst)])
    for r in range(32):
        vv = b.ve(lambda e: e.max(m8b[:], cand[:]),
                  waits=[("CAND", cvld)] if r == 0 else ())
        if r < 31:
            vv = b.ve(lambda e: e.match_replace(cand[:], m8b[:], cand[:], NEG))
    vtau = b.ve(lambda e: e.tensor_copy(tau16[:], m8b[:, 7:8]))
    ttau = b.t(lambda e: e.matmul(pbk[0][:, 0:1], e16_sb[:], tau16[:],
                                  start=True, stop=True),
               waits=[("V", vtau)])
    vtb = b.ve(lambda e: e.tensor_copy(tau128[:], pbk[0][:, 0:1]), waits=[("T", ttau)])
    vst = b.ve(lambda e: e.tensor_scalar(strict[:], s1v[:], tau128[:], None,
                                         ALU.is_gt, ALU.add, accum_out=scnt[:]))
    vti = b.ve(lambda e: e.tensor_scalar(tiem[:], s1v[:], tau128[:], None,
                                         ALU.is_equal, ALU.add, accum_out=tcnt[:]))
    tc1 = b.t(lambda e: e.matmul(pbk[1][:, 0:1], b8_sb[:], scnt[:],
                                 start=True, stop=True), waits=[("V", vst)])
    tto = b.t(lambda e: e.matmul(pbk[2][:, 0:1], l8_sb[:], tcnt[:],
                                 start=True, stop=True), waits=[("V", vti)])
    vc1 = b.ve(lambda e: e.tensor_copy(c1h[:], pbk[1][:, 0:1]), waits=[("T", tc1)])
    vto = b.ve(lambda e: e.tensor_copy(tieoff[:], pbk[2][:, 0:1]), waits=[("T", tto)])
    vqo = b.ve(lambda e: e.tensor_scalar(quota[:], c1h[:], -1.0, 256.0,
                                         ALU.mult, ALU.add))
    vsc = b.ve(lambda e: e.tensor_tensor_scan(tiescan[:], tiem[:], zcol[:], 0.0,
                                              ALU.add, ALU.add))
    vtr = b.ve(lambda e: e.tensor_tensor(tierank[:], tiescan[:], tiem[:], ALU.subtract))
    vtr = b.ve(lambda e: e.tensor_scalar(tierank[:], tierank[:], tieoff[:], None,
                                         ALU.add, ALU.bypass))
    vtk = b.ve(lambda e: e.tensor_scalar(tieok[:], tierank[:], quota[:], None,
                                         ALU.is_lt, ALU.bypass))
    vva = b.ve(lambda e: e.tensor_tensor(valid[:], tiem[:], tieok[:], ALU.mult))
    vva = b.ve(lambda e: e.tensor_tensor(valid[:], valid[:], strict[:], ALU.add))
    vmb = b.ve(lambda e: e.tensor_scalar(mb128[:], valid[:], -1.0, 30000.0,
                                         ALU.add, ALU.mult))
    mbst = 0
    for qq in range(4):
        mbst = b.dma("scalar",
                     mbias_d[:, qq * 128:(qq + 1) * 128].rearrange(
                         "h (s j) -> h s j", s=8),
                     mb128[:, qq * 16:(qq + 1) * 16],
                     "MB", waits=[("V", vmb)] if qq == 0 else ())
    if debug:
        b.dma("sync", dbg["d_tau"], tau128[:], "DBG", waits=[("V", vtb)])
        b.dma("sync", dbg["d_valid"], valid[:], "DBG", waits=[("V", vva)])
        b.dma("sync", dbg["d_s1v"], s1v[:], "DBG", waits=[("V", b.v["V"])])
        b.dma("sync", dbg["d_s1g"], s1g[:], "DBG", waits=[("V", vs1g)])


    def hview(i, h):
        v3 = fmc[i].rearrange("p (j q) -> p j q", q=128)
        v3 = v3[:, :, 8 * h:8 * h + 8]
        return v3.rearrange("p j s -> p s j")

    t_ks = 0
    so_ks = 0
    a_ev = 0
    tks_par = [0, 0]
    wkv2 = [vb(R2 + 36864 + i * 4096, 128, 2048) for i in range(2)]
    for qq in range(4):
        for bk_ in range(8):
            wz = []
            if bk_ == 0:
                wz = [("CST", LDC)]
                if qq > 0:
                    wz.append(("V", a_ev))
                else:
                    wz.append(("T", t_imp[NC - 1]))
                    wz.append(("V", vto))
            if bk_ == 4 and qq == 0:
                wz = [("A", A_QP)]
            t_ks = b.t(lambda e, bk_=bk_: e.matmul(
                pbk[bk_][:, :], zr_sb[:, 0:128], zr_sb[:],
                start=True, stop=True), waits=wz)
        la_f = [0] * NC
        ld_w = [0] * NC
        ld_w2 = [0] * NC
        for dc in range(NC):
            i = dc % 2
            w = list(W_GD)
            if qq > 0 or dc >= 2:
                w.append(("T", tks_par[i]))
            else:
                w.append(("T", t_qp[NC - 1]))
            la_f[dc] = b.dmat("scalar",
                              fmc[i],
                              gdram[qq * 2048:(qq + 1) * 2048,
                                    dc * 128:(dc + 1) * 128],
                              f"TK{i}", waits=w)
            ld_w[dc] = b.dma("sync", wkv[i],
                             wkb_d[dc * 128:(dc + 1) * 128, :],
                             f"KV{i}", waits=[w[-1]])
            ld_w2[dc] = b.dma("gpsimd", wkv2[i],
                              wvb_d[dc * 128:(dc + 1) * 128, :],
                              f"PV{i}", waits=[w[-1]])
            for h in range(H):
                wmm = [(f"TK{i}", la_f[dc]), (f"KV{i}", ld_w[dc]),
                       (f"PV{i}", ld_w2[dc])] if h == 0 else ()
                if h == 0 and dc == 0 and qq > 0:
                    wmm = list(wmm) + [("KS", so_ks)]
                t_ks = b.t(lambda e, i=i, dc=dc, h=h: e.matmul(
                    pbk[h // 4][:, (h % 4) * 128:(h % 4 + 1) * 128],
                    wkv[i][:, h * 128:(h + 1) * 128], hview(i, h),
                    start=False, stop=False,
                    skip_group_check=True), waits=wmm)
                t_ks = b.t(lambda e, i=i, dc=dc, h=h: e.matmul(
                    pbk[4 + h // 4][:, (h % 4) * 128:(h % 4 + 1) * 128],
                    wkv2[i][:, h * 128:(h + 1) * 128], hview(i, h),
                    start=False, stop=False,
                    skip_group_check=True))
            tks_par[i] = t_ks
        for which, (dstd, bias_sb, boff) in enumerate(
                [(ksel_d, bkc_sb, 0), (vsf_d, bvc_sb, 4)]):
            for h in range(H):
                wev = []
                if h == 0:
                    if which == 0:
                        wev.append(("T", t_ks))
                    if so_ks:
                        wev.append(("KS", so_ks))
                a_ev = b.ve(lambda e, h=h, bias_sb=bias_sb, boff=boff: e.tensor_scalar(
                    kstg[:, h * 128:(h + 1) * 128],
                    pbk[boff + h // 4][:, (h % 4) * 128:(h % 4 + 1) * 128],
                    bias_sb[:, h:h + 1], None, ALU.add, ALU.bypass), waits=wev)
            so_ks = b._add("sync",
                           (lambda e, qq=qq, dstd=dstd: e.dma_start(
                               out=dstd[:, qq].rearrange("h p q -> p h q"),
                               in_=kstg.rearrange("p (h q) -> p h q", q=128))),
                           [("V", a_ev)], "KS", 16)
    SO_KS = so_ks

    b.bar()


    last_sc_read = [0, 0]
    last_av_read = [0, 0]
    last_scT = [0, 0]
    last_avT = [0, 0]
    ld_k = [0, 0]
    ld_v = [0, 0]
    ld_m = [0, 0]
    v_wT = 0
    MBTOT = b.v["MB"]

    def em_loads(h):
        hb = h % 2
        wk_ = [("KS", SO_KS)]
        if last_scT[hb]:
            wk_.append(("T", last_scT[hb]))
        ld_k[hb] = b.dma("sync", kselb[hb].rearrange("p (c q) -> p c q", q=128),
                         ksel_d[h].rearrange("qq p q -> p qq q"),
                         f"KH{hb}", waits=wk_)
        wv_ = [("KS", SO_KS)]
        if last_avT[hb]:
            wv_.append(("T", last_avT[hb]))
        ld_v[hb] = b.dmat("scalar", vkmb[hb],
                          vsf_d[h].rearrange("qq p q -> (qq p) q"),
                          f"VH{hb}", waits=wv_)
        ld_m[hb] = b.dma("gpsimd", mbrow[hb], mbias_d[h:h + 1, :],
                         f"MH{hb}", waits=[("MB", MBTOT)] + (
                             [("T", last_scT[hb])] if last_scT[hb] else []))

    def em_scores(it):
        h, tt = it // NTO, it % NTO
        hb = h % 2
        sbank = 4 + it % 2
        wsc = [("A", A_QP), (f"KH{hb}", ld_k[hb]), (f"MH{hb}", ld_m[hb])]
        if last_sc_read[it % 2]:
            wsc.append(("V", last_sc_read[it % 2]))
        tsc = b.t(lambda e, h=h, tt=tt, sbank=sbank, hb=hb: e.matmul(
            pbk[sbank][:, :], qfm[:, h * 1024 + tt * 128: h * 1024 + (tt + 1) * 128],
            kselb[hb][:, :], start=True, stop=False), waits=wsc)
        tsc = b.t(lambda e, hb=hb, sbank=sbank: e.matmul(
            pbk[sbank][:, :], onesr_sb[:], mbrow[hb], start=False, stop=True))
        last_scT[hb] = tsc
        return tsc

    def em_rest(it, tsc):
        nonlocal_v = None
        h, tt = it // NTO, it % NTO
        hb = h % 2
        sbank = 4 + it % 2
        abank = 6 + it % 2
        vmx = b.ve(lambda e, sbank=sbank: e.reduce_max(smax[:], pbk[sbank][:, :], AX.X),
                   waits=[("T", tsc)])
        vsb = b.ve(lambda e, sbank=sbank: e.tensor_scalar(
            ssub, pbk[sbank][:, :], smax[:], None, ALU.subtract, ALU.bypass))
        last_sc_read[it % 2] = vsb
        aex = b.a(lambda e: e.activation(sexp, ssub, AF.Exp, bias=0.0,
                                         scale=csts[:, 2:3], accum_out=ssum[:]),
                  waits=[("V", vsb)])
        vrc = b.ve(lambda e: e.reciprocal(srec[:], ssum[:]), waits=[("A", aex)])
        vws = b.ve(lambda e: e.tensor_scalar(wsm, sexp, srec[:], None,
                                             ALU.mult, ALU.bypass))
        global_v_wT = v_wT
        t_wt4 = 0
        for kc in range(4):
            wv_ = [("V", vws)] if kc == 0 else ()
            if kc == 0 and global_v_wT:
                wv_ = list(wv_) + [("V", global_v_wT)]
            t_wt4 = b.t(lambda e, kc=kc: e.transpose(
                pbb(3, kc * 128, 128), wsm[:, kc * 128:(kc + 1) * 128],
                ident_sb[:]), waits=wv_)
        nv_wT = b.ve(lambda e: e.tensor_copy(wTt, pbb(3, 0, 512)),
                     waits=[("T", t_wt4)])
        wav = [("V", nv_wT), (f"VH{hb}", ld_v[hb])]
        if last_av_read[it % 2]:
            wav.append(("A", last_av_read[it % 2]))
        tav = 0
        for kc in range(4):
            tav = b.t(lambda e, hb=hb, kc=kc, abank=abank: e.matmul(
                pbk[abank][:, 0:128], vkmb[hb][:, kc * 128:(kc + 1) * 128],
                wTt[:, kc * 128:(kc + 1) * 128],
                start=(kc == 0), stop=(kc == 3)),
                waits=wav if kc == 0 else ())
        last_avT[hb] = tav
        aov = b.a(lambda e, h=h, tt=tt, abank=abank: e.activation(
            outfm[:, h * 1024 + tt * 128: h * 1024 + (tt + 1) * 128],
            pbk[abank][:, 0:128], AF.Copy, bias=0.0, scale=1.0),
            waits=[("T", tav)])
        last_av_read[it % 2] = aov
        return nv_wT

    NIT = H * NTO
    pend = None
    for it in range(NIT + 1):
        if it < NIT:
            if it % NTO == 0:
                em_loads(it // NTO)
            tsc = em_scores(it)
        if pend is not None:
            v_wT = em_rest(it - 1, pend)
        pend = tsc if it < NIT else None
        A_ATT = b.v["A"]

    b.bar()

    lg_wo = [0] * NC
    t_wo = [0] * NC
    v_wo = [0] * NC
    so_wo = [0] * NC
    for m in range(NC):
        i = m % 2
        w = [("T", t_wo[m - 2])] if m >= 2 else []
        lg_wo[m] = b.dma("gpsimd", wocol[i].rearrange("p (c q) -> p c q", q=128),
                         wo_t[:, m].rearrange("c p q -> p c q"), f"WO{i}", waits=w)
        for th in range(2):
            for dc in range(NC):
                t_wo[m] = b.t(lambda e, i=i, th=th, dc=dc: e.matmul(
                    pbk[4 + th][:, :], wocol[i][:, dc * 128:(dc + 1) * 128],
                    outfm[:, dc * 1024 + th * 512: dc * 1024 + (th + 1) * 512],
                    start=(dc == 0), stop=(dc == NC - 1)),
                    waits=([(f"WO{i}", lg_wo[m]), ("A", A_ATT)] +
                           ([("V", v_wo[m - 1])] if m >= 1 else []))
                    if (th == 0 and dc == 0) else ())
        for th in range(2):
            wv_ = [("T", t_wo[m])] if th == 0 else []
            if th == 0 and m >= 1:
                wv_.append(("WOS", so_wo[m - 1]))
            v_wo[m] = b.ve(lambda e, th=th: e.tensor_copy(
                attf[:, th * 512:(th + 1) * 512], pbk[4 + th][:, :]), waits=wv_)
        a_bo = b.ve(lambda e, m=m: e.tensor_scalar(
            attf, attf, boc_sb[:, m:m + 1], None, ALU.add, ALU.bypass))
        so_wo[m] = b._add("sync",
                          (lambda e, m=m: e.dma_start(
                              out=attout_d[m * 128:(m + 1) * 128, :], in_=attf)),
                          [("V", a_bo)], "WOS", 16)
    SO_WO = so_wo[NC - 1]

    b.bar()

    DBG_QFM = 0
    if debug:
        b.dma("sync", dbg["d_attout"], attout_d, "DBG", waits=[("WOS", SO_WO)])
        b.dma("sync", dbg["d_ksel"], ksel_d, "DBG", waits=[("KS", SO_KS)])
        b.dma("sync", dbg["d_vsf"], vsf_d, "DBG", waits=[("KS", SO_KS)])
        DBG_QFM = b.dma("sync", dbg["d_qfm"], qfm[:], "DBG", waits=[("A", A_QP)])

    ld_ac = [0, 0]
    sg_x2 = [0] * NTO
    sg_n2 = [0] * NTO
    a_n2 = [0] * NTO
    last_at = [0, 0]
    vadd_bk = [0, 0]
    for kt in range(NTO):
        i = kt % 2
        w = [("WOS", SO_WO)]
        if kt >= 2:
            w.append(("T", last_at[i]))
        ld_ac[i] = b.dma("sync", attcols[i].rearrange("p (m q) -> p m q", q=128),
                         attout_d[:, kt * 128:(kt + 1) * 128].rearrange(
                             "(m p) q -> p m q", p=128), f"AC{i}", waits=w)
        ldx = b.dma("sync", x2t[i], x_in[kt * 128:(kt + 1) * 128, :], f"XR{i}",
                    waits=[(f"X2{i}", sg_x2[kt - 2])] if kt >= 2 else
                          [("WOS", SO_WO)])
        vadd = 0
        for m in range(NC):
            bkx = 2 + m % 2
            t_at = b.t(lambda e, i=i, m=m, bkx=bkx: e.transpose(
                pbk[bkx][:, 0:128], attcols[i][:, m * 128:(m + 1) * 128], identf_sb[:]),
                waits=[(f"AC{i}", ld_ac[i]),
                       ("V", vadd_bk[m % 2] if vadd_bk[m % 2] else b.v["V"])])
            last_at[i] = t_at
            vadd = b.ve(lambda e, i=i, m=m, bkx=bkx: e.tensor_tensor(
                x2t[i][:, m * 128:(m + 1) * 128], x2t[i][:, m * 128:(m + 1) * 128],
                pbk[bkx][:, 0:128], ALU.add),
                waits=[("T", t_at), (f"XR{i}", ldx)])
            vadd_bk[m % 2] = vadd
        for q in range(4):
            b.ve(lambda e, i=i, q=q: e.bn_stats(stat6[i][:, q * 6:(q + 1) * 6],
                                                x2t[i][:, q * 512:(q + 1) * 512]))
        b.ve(lambda e, i=i: e.bn_aggr(mvt[i][:], stat6[i][:]))
        vm = b.ve(lambda e, i=i: e.tensor_copy(mean_sb[:, 0:1], mvt[i][:, 0:1]))
        av2 = b.a(lambda e, i=i: e.activation(std_sb[i][:], mvt[i][:, 1:2], AF.Sqrt,
                                              bias=csts[:, 0:1], scale=1.0),
                  waits=[("V", vm), ("A", b.v["A"])])
        vr2 = b.ve(lambda e, i=i: e.reciprocal(rstd_sb[:, 0:1], std_sb[i][:]),
                   waits=[("A", av2)])
        vn2 = b.ve(lambda e, i=i: e.scalar_tensor_tensor(
            negms_sb[:, 0:1], mvt[i][:, 0:1], -1.0, rstd_sb[:, 0:1],
            ALU.mult, ALU.mult))
        wn = []
        if kt >= 2:
            wn.append((f"N2{i}", sg_n2[kt - 2]))
        if debug and kt < 2:
            wn.append(("DBG", DBG_QFM))
        a_n2[kt] = b.ve(lambda e, i=i: e.tensor_scalar(
            xn2t[i], x2t[i], rstd_sb[:, 0:1], negms_sb[:, 0:1], ALU.mult, ALU.add),
            waits=wn)
        sg_n2[kt] = b._add("sync",
                           (lambda e, i=i, kt=kt: e.dma_start(
                               out=xn2_d[kt * 128:(kt + 1) * 128, :], in_=xn2t[i])),
                           [("V", a_n2[kt])], f"N2{i}", 16)
        sg_x2[kt] = b._add("scalar",
                           (lambda e, i=i, kt=kt: e.dma_start(
                               out=x2_d[kt * 128:(kt + 1) * 128, :], in_=x2t[i])),
                           [("V", a_n2[kt])], f"X2{i}", 16)
    N2TOT = (b.v["N20"], b.v["N21"])
    X2TOT = (b.v["X20"], b.v["X21"])

    b.bar()

    if debug:
        b.dma("gpsimd", dbg["d_x2"], x2_d, "DBG2",
              waits=[("X20", X2TOT[0]), ("X21", X2TOT[1])])
        b.dma("gpsimd", dbg["d_xn2"], xn2_d, "DBG2",
              waits=[("N20", N2TOT[0]), ("N21", N2TOT[1])])

    la_n2 = [0] * NC
    for dc in range(NC):
        la_n2[dc] = b.dmat("scalar", vb(R4 + dc * 2048, 128, 1024),
                           xn2_d[:, dc * 128:(dc + 1) * 128], "XN2F",
                           waits=[("N20", N2TOT[0]), ("N21", N2TOT[1])])
    LA_N2 = b.v["XN2F"]
    t_xs = 0
    for th in range(2):
        for dc in range(NC):
            t_xs = b.t(lambda e, th=th, dc=dc: e.matmul(
                pbk[th][0:1, :], ones_sb[:],
                vb(R4 + dc * 2048 + th * 1024, 128, 512),
                start=(dc == 0), stop=(dc == NC - 1)),
                waits=[("XN2F", LA_N2)] if (th == 0 and dc == 0) else ())
    a_xs = 0
    for th in range(2):
        a_xs = b.a(lambda e, th=th: e.activation(
            xsum_f[:, th * 512:(th + 1) * 512], pbk[th][0:1, :], AF.Copy,
            bias=0.0, scale=1.0), waits=[("T", t_xs)] if th == 0 else ())
    vhi = b.ve(lambda e: e.tensor_copy(xsum_hi[:], xsum_f), waits=[("A", a_xs)])
    vt_ = b.ve(lambda e: e.tensor_tensor(xsum_t, xsum_f, xsum_hi[:], ALU.subtract))
    vlo = b.ve(lambda e: e.tensor_copy(xsum_lo[:], xsum_t))
    V_XS = vlo

    groups = [list(range(g * 8, min(NJC, g * 8 + 8))) for g in range(6)]
    lg_g = 0
    lg_u = 0
    last_gd = 0
    g_d = [0] * 9
    a_sil = 0
    v_hp = [0] * 8
    last_vhp = 0
    t_dn = 0
    v_fl = 0
    t_gu = [0] * NJC
    for gi, grp in enumerate(groups):
        for jj, jc in enumerate(grp):
            jb = jc % 2
            wg_w = [("T", t_gu[jc - 2])] if jc >= 2 else []
            lg_g = b.dma("gpsimd", wgbb[jb], wg_b[jc], f"FG{jb}", waits=wg_w)
            lg_u = b.dma("gpsimd", wubb[jb], wu_b[jc], f"FU{jb}", waits=wg_w)
            ld_w = [("G", last_gd)] if last_gd else []
            ldd = b.dma("sync", wdi, wd_b[jc], "FD", waits=ld_w)
            g_d[jj] = b.g(lambda e, jj=jj, jc=jc: e.tensor_scalar(
                wdbb[jj], wdi, -128.0, su_sb[:, jc:jc + 1], ALU.add, ALU.mult),
                waits=[("FD", ldd)] + ([("T", t_dn)] if gi > 0 else []))
            last_gd = g_d[jj]
            for th in range(2):
                wmm = [(f"FG{jb}", lg_g), ("XN2F", LA_N2), ("V", V_XS)]
                if a_sil:
                    wmm.append(("A", a_sil))
                for dc in range(NC):
                    t_gu[jc] = b.t(lambda e, th=th, dc=dc, jb=jb: e.matmul(
                        pbk[2 + th][:, :], wgbb[jb][:, dc * 128:(dc + 1) * 128],
                        vb(R4 + dc * 2048 + th * 1024, 128, 512),
                        start=(dc == 0), stop=False),
                        waits=wmm if dc == 0 else ())
                t_gu[jc] = b.t(lambda e, th=th: e.matmul(
                    pbk[2 + th][:, :], zpr_sb[:], xsum_hi[:, th * 512:(th + 1) * 512],
                    start=False, stop=False))
                t_gu[jc] = b.t(lambda e, th=th: e.matmul(
                    pbk[2 + th][:, :], zpr_sb[:], xsum_lo[:, th * 512:(th + 1) * 512],
                    start=False, stop=True))
            for th in range(2):
                wsil = []
                if th == 0:
                    wsil = [("T", t_gu[jc])]
                    if last_vhp:
                        wsil.append(("V", last_vhp))
                a_sil = b.a(lambda e, th=th, jc=jc: e.activation(
                    gsil[:, th * 512:(th + 1) * 512], pbk[2 + th][:, :], AF.Silu,
                    bias=0.0, scale=sg_sb[:, jc:jc + 1]), waits=wsil)
            for th in range(2):
                wmm = ([(f"FU{jb}", lg_u)] + ([("V", last_vhp)] if last_vhp else [])) if th == 0 else ()
                for dc in range(NC):
                    t_gu[jc] = b.t(lambda e, th=th, dc=dc, jb=jb: e.matmul(
                        pbk[4 + th][:, :], wubb[jb][:, dc * 128:(dc + 1) * 128],
                        vb(R4 + dc * 2048 + th * 1024, 128, 512),
                        start=(dc == 0), stop=False),
                        waits=wmm if dc == 0 else ())
                t_gu[jc] = b.t(lambda e, th=th: e.matmul(
                    pbk[4 + th][:, :], zpr_sb[:], xsum_hi[:, th * 512:(th + 1) * 512],
                    start=False, stop=False))
                t_gu[jc] = b.t(lambda e, th=th: e.matmul(
                    pbk[4 + th][:, :], zpr_sb[:], xsum_lo[:, th * 512:(th + 1) * 512],
                    start=False, stop=True))
            for th in range(2):
                v_hp[jj] = b.ve(lambda e, jj=jj, th=th: e.tensor_tensor(
                    hpr[jj][:, th * 512:(th + 1) * 512], pbk[4 + th][:, :],
                    gsil[:, th * 512:(th + 1) * 512], ALU.mult),
                    waits=[("T", t_gu[jc]), ("A", a_sil)] if th == 0 else ())
            last_vhp = v_hp[jj]
        for m in range(NC):
            for jj in range(len(grp)):
                wmm = []
                if jj == 0:
                    wmm = [("V", v_hp[len(grp) - 1]), ("G", g_d[len(grp) - 1])]
                    if v_fl:
                        wmm.append(("V", v_fl))
                for th in range(2):
                    t_dn = b.t(lambda e, m=m, th=th, jj=jj,
                               ng=len(grp): e.matmul(
                        pbk[6 + th][:, :], wdbb[jj][:, m * 128:(m + 1) * 128],
                        hpr[jj][:, th * 512:(th + 1) * 512],
                        start=(jj == 0), stop=(jj == ng - 1)),
                        waits=wmm if th == 0 else ())
                    wmm = ()
            for th in range(2):
                if gi == 0:
                    v_fl = b.ve(lambda e, m=m, th=th: e.tensor_scalar(
                        dacc[:, m * 1024 + th * 512: m * 1024 + (th + 1) * 512],
                        pbk[6 + th][:, :], sd_sb[:, m:m + 1], None,
                        ALU.mult, ALU.bypass), waits=[("T", t_dn)] if th == 0 else ())
                else:
                    v_fl = b.ve(lambda e, m=m, th=th: e.scalar_tensor_tensor(
                        dacc[:, m * 1024 + th * 512: m * 1024 + (th + 1) * 512],
                        pbk[6 + th][:, :], sd_sb[:, m:m + 1],
                        dacc[:, m * 1024 + th * 512: m * 1024 + (th + 1) * 512],
                        ALU.mult, ALU.add), waits=[("T", t_dn)] if th == 0 else ())
    V_FFN = b.v["V"]
    if debug:
        b.dma("sync", dbg["d_dacc"], dacc, "DBG", waits=[("V", V_FFN)])

    b.bar()

    so_out = [0] * NTO
    ldx2 = [0, 0]
    vadd_bk = [0, 0]
    for kt in range(NTO):
        i = kt % 2
        w = [("V", V_FFN), ("X20", X2TOT[0]), ("X21", X2TOT[1])]
        if kt >= 2:
            w.append((f"O{i}", so_out[kt - 2]))
        ldx2[i] = b.dma("sync", x2fin[i], x2_d[kt * 128:(kt + 1) * 128, :],
                        f"XF{i}", waits=w)
        vadd = 0
        for m in range(NC):
            bkx = 2 + m % 2
            wt = [("V", vadd_bk[m % 2])] if vadd_bk[m % 2] else []
            if kt == 0 and m < 2:
                wt.append(("V", V_FFN))
            t_ds = b.t(lambda e, m=m, kt=kt, bkx=bkx: e.transpose(
                pbk[bkx][:, 0:128],
                dacc[:, m * 1024 + kt * 128: m * 1024 + (kt + 1) * 128],
                identf_sb[:]), waits=wt)
            t_ds_bk = t_ds
            vadd = b.ve(lambda e, i=i, m=m, bkx=bkx: e.tensor_tensor(
                x2fin[i][:, m * 128:(m + 1) * 128],
                x2fin[i][:, m * 128:(m + 1) * 128],
                pbk[bkx][:, 0:128], ALU.add), waits=[("T", t_ds), (f"XF{i}", ldx2[i])])
            vadd_bk[m % 2] = vadd
        so_out[kt] = b._add("sync",
                            (lambda e, i=i, kt=kt: e.dma_start(
                                out=out_d[kt * 128:(kt + 1) * 128, :], in_=x2fin[i])),
                            [("V", vadd)], f"O{i}", 16)
    fin = [("O0", b.v["O0"]), ("O1", b.v["O1"])]
    if debug:
        fin.append(("DBG", b.v["DBG"]))
        fin.append(("DBG2", b.v["DBG2"]))
    b.ops["sync"].append((None, tuple(fin), None, 0))

    with nc.Block() as block:
        b.emit(block)
    ctx.close()
    return nc, dbg


def prep_inputs(x, Wq, bq, Wk, bk, Wv, bv, Wo, bo, wq_gate, s_gate, wq_up, s_up,
                wq_down, s_down, core):
    xr = np.concatenate([x[core * TPC:], x[:core * TPC]], axis=0)
    xr = np.ascontiguousarray(xr)

    def tile4(W):
        return np.ascontiguousarray(
            W.reshape(NC, 128, NC, 128).transpose(0, 2, 1, 3))

    def rowch(W):
        return np.ascontiguousarray(W.reshape(NC, 128, DM))

    def colsplit(v):
        return np.ascontiguousarray(v.reshape(NC, 128).T)

    wgb = np.ascontiguousarray(
        wq_gate.reshape(NJC, 128, NC, 128).transpose(0, 3, 2, 1)).reshape(NJC, 128, DM)
    wub = np.ascontiguousarray(
        wq_up.reshape(NJC, 128, NC, 128).transpose(0, 3, 2, 1)).reshape(NJC, 128, DM)
    wdb = np.ascontiguousarray(
        wq_down.reshape(DM, NJC, 128).transpose(1, 2, 0)).reshape(NJC, 128, DM)

    e16 = np.zeros((16, 128), np.float32)
    for jj in range(16):
        e16[jj, jj * 8:(jj + 1) * 8] = 1.0
    b8 = np.zeros((128, 128), np.float32)
    l8 = np.zeros((128, 128), np.float32)
    for p in range(128):
        h0 = p // 8
        b8[h0 * 8:(h0 + 1) * 8, p] = 1.0
        l8[h0 * 8: p, p] = 1.0
    sbase = (np.arange(128) % 8 * 1024).astype(np.float32).reshape(128, 1)

    return dict(
        x=xr,
        wq_t=tile4(Wq), wk_t=rowch(Wk), wv_t=rowch(Wv), wo_t=tile4(Wo),
        bq_c=colsplit(bq), bk_c=colsplit(bk), bv_c=colsplit(bv), bo_c=colsplit(bo),
        bk_r=np.ascontiguousarray(bk.reshape(1, DM)),
        bq_r=np.ascontiguousarray(bq.reshape(1, DM)),
        wg_b=wgb, wu_b=wub, wd_b=wdb,
        sg_c=np.ascontiguousarray(s_gate.reshape(NJC, 128).T),
        su_c=np.ascontiguousarray(s_up.reshape(NJC, 128).T),
        sd_c=np.ascontiguousarray(s_down.reshape(NC, 128).T),
        ones_col=np.ones((128, 1), ml_dtypes.bfloat16),
        ident_in=np.eye(128, dtype=ml_dtypes.bfloat16),
        identf_in=np.eye(128, dtype=np.float32),
        e16_in=e16, b8_in=b8, l8_in=l8, sbase_in=sbase,
        zpr_in=np.full((1, 128), -128.0, ml_dtypes.bfloat16),
        onesr_in=np.ones((1, 128), ml_dtypes.bfloat16),
        ident16_in=np.eye(16, dtype=ml_dtypes.bfloat16),
        zr512_in=np.zeros((1, 512), ml_dtypes.bfloat16),
    )


def kernel(L0_attn_hidden_states, L0_attn_Wq, L0_attn_bq, L0_attn_Wk, L0_attn_bk,
           L0_attn_Wv, L0_attn_bv, L0_attn_Wo, L0_attn_bo, L0_ffn_wq_gate,
           L0_ffn_s_gate, L0_ffn_wq_up, L0_ffn_s_up, L0_ffn_wq_down, L0_ffn_s_down):
    from concourse.bass_utils import run_bass_kernel_spmd
    x = np.ascontiguousarray(np.asarray(L0_attn_hidden_states, np.float32).reshape(S, DM))
    args = (x, np.asarray(L0_attn_Wq, np.float32), np.asarray(L0_attn_bq, np.float32),
            np.asarray(L0_attn_Wk, np.float32), np.asarray(L0_attn_bk, np.float32),
            np.asarray(L0_attn_Wv, np.float32), np.asarray(L0_attn_bv, np.float32),
            np.asarray(L0_attn_Wo, np.float32), np.asarray(L0_attn_bo, np.float32),
            np.asarray(L0_ffn_wq_gate, np.int32), np.asarray(L0_ffn_s_gate, np.float32),
            np.asarray(L0_ffn_wq_up, np.int32), np.asarray(L0_ffn_s_up, np.float32),
            np.asarray(L0_ffn_wq_down, np.int32), np.asarray(L0_ffn_s_down, np.float32))
    nc, _ = build(debug=0)
    in_maps = [prep_inputs(*args, core=c) for c in range(NCORES)]
    res = run_bass_kernel_spmd(nc, in_maps, core_ids=list(range(NCORES)))
    out = np.concatenate([np.asarray(res.results[c]["out"]) for c in range(NCORES)],
                         axis=0)
    return out.reshape(1, S, DM)


if __name__ == "__main__":
    nc, _ = build(debug=1)
    n = sum(len(v) for v in nc.m.functions[0].basicblocks[0].instructions) if False else 0
    print("build ok")


# version 10 (final)
# speedup vs baseline: 1.0227x; 1.0090x over previous
import numpy as np
import ml_dtypes
import concourse.bass as bass
import concourse.mybir as mybir
from concourse.bass_core_rust import IndirectOffsetOnAxis
from contextlib import ExitStack

F32 = mybir.dt.float32
BF16 = mybir.dt.bfloat16
I32 = mybir.dt.int32
U32 = mybir.dt.uint32
AF = mybir.ActivationFunctionType
ALU = mybir.AluOpType
AX = mybir.AxisListType

S = 8192
DM = 2048
H = 16
TPC = 1024
NT = S // 128
NTO = TPC // 128
NC = DM // 128
J = 64
KP = 8 * J
HID = 5632
NJC = HID // 128
SCALE = float(1.0 / np.sqrt(128.0))
EPS = 1e-5
NEG = -1e30
NCORES = 8

ENGS = ("sync", "tensor", "vector", "scalar", "gpsimd")

R1 = 0
R2 = 65536
R3 = 110592
R4 = 143360
AREND = 176128
ARCOLS = AREND // 4


class Bld:
    def __init__(self, sems):
        self.sems = sems
        self.ops = {e: [] for e in ENGS}
        self.v = {k: 0 for k in sems}

    def _add(self, eng, fn, waits, inc, by):
        self.ops[eng].append((fn, tuple(waits), inc, by))
        if inc:
            self.v[inc] += by
            return self.v[inc]

    def t(self, fn, waits=()):
        return self._add("tensor", fn, waits, "T", 1)

    def ve(self, fn, waits=()):
        w = list(waits)
        if self.v["V"] > 0:
            w.append(("V", self.v["V"]))
        return self._add("vector", fn, w, "V", 1)

    def a(self, fn, waits=()):
        w = list(waits)
        if self.v["A"] > 0:
            w.append(("A", self.v["A"]))
        return self._add("scalar", fn, w, "A", 1)

    def g(self, fn, waits=()):
        w = list(waits)
        if self.v["G"] > 0:
            w.append(("G", self.v["G"]))
        return self._add("gpsimd", fn, w, "G", 1)

    def dma(self, q, out, in_, sem, waits=()):
        def fn(e, out=out, in_=in_):
            return e.dma_start(out=out, in_=in_)
        return self._add(q, fn, waits, sem, 16)

    def dmat(self, q, out, in_, sem, waits=()):
        def fn(e, out=out, in_=in_):
            return e.dma_start(out=out, in_=in_, transpose=True)
        return self._add(q, fn, waits, sem, 16)

    def gather(self, out, table, idx_ap, sem, waits=()):
        def fn(e, out=out, table=table, idx_ap=idx_ap):
            return e.indirect_dma_start(out=out, out_offset=None, in_=table,
                                        in_offset=IndirectOffsetOnAxis(idx_ap, 0))
        return self._add("gpsimd", fn, waits, sem, 16)

    def bar(self):
        snap = [(k, v) for k, v in self.v.items() if v > 0]
        for e in ENGS:
            self.ops[e].append((None, tuple(snap), None, 0))

    def emit(self, block):
        sems = self.sems
        for eng_name in ENGS:
            ops = self.ops[eng_name]

            def mk(ops=ops):
                def body(e):
                    for fn, waits, inc, by in ops:
                        for (sn, val) in waits:
                            if val and val > 0:
                                e.wait_ge(sems[sn], val)
                        if fn is None:
                            continue
                        inst = fn(e)
                        if inc:
                            inst.then_inc(sems[inc], by)
                return body
            getattr(block, eng_name)(mk())


def build(debug=0):
    nc = bass.Bass("TRN2", target_bir_lowering=False)
    ctx = ExitStack()
    E = ctx.enter_context
    ap = lambda t: t.ap()

    x_in = ap(nc.dram_tensor("x", [S, DM], F32, kind="ExternalInput"))
    wq_t = ap(nc.dram_tensor("wq_t", [NC, NC, 128, 128], F32, kind="ExternalInput"))
    wk_t = ap(nc.dram_tensor("wk_t", [NC, 128, DM], F32, kind="ExternalInput"))
    wv_t = ap(nc.dram_tensor("wv_t", [NC, 128, DM], F32, kind="ExternalInput"))
    wo_t = ap(nc.dram_tensor("wo_t", [NC, NC, 128, 128], F32, kind="ExternalInput"))
    bq_c = ap(nc.dram_tensor("bq_c", [128, NC], F32, kind="ExternalInput"))
    bk_c = ap(nc.dram_tensor("bk_c", [128, NC], F32, kind="ExternalInput"))
    bv_c = ap(nc.dram_tensor("bv_c", [128, NC], F32, kind="ExternalInput"))
    bo_c = ap(nc.dram_tensor("bo_c", [128, NC], F32, kind="ExternalInput"))
    bk_r = ap(nc.dram_tensor("bk_r", [1, DM], F32, kind="ExternalInput"))
    bq_r = ap(nc.dram_tensor("bq_r", [1, DM], F32, kind="ExternalInput"))
    wg_b = ap(nc.dram_tensor("wg_b", [NJC, 128, DM], I32, kind="ExternalInput"))
    wu_b = ap(nc.dram_tensor("wu_b", [NJC, 128, DM], I32, kind="ExternalInput"))
    wd_b = ap(nc.dram_tensor("wd_b", [NJC, 128, DM], I32, kind="ExternalInput"))
    sg_c = ap(nc.dram_tensor("sg_c", [128, NJC], F32, kind="ExternalInput"))
    su_c = ap(nc.dram_tensor("su_c", [128, NJC], F32, kind="ExternalInput"))
    sd_c = ap(nc.dram_tensor("sd_c", [128, NC], F32, kind="ExternalInput"))
    ones_col = ap(nc.dram_tensor("ones_col", [128, 1], BF16, kind="ExternalInput"))
    ident_in = ap(nc.dram_tensor("ident_in", [128, 128], BF16, kind="ExternalInput"))
    identf_in = ap(nc.dram_tensor("identf_in", [128, 128], F32, kind="ExternalInput"))
    e16_in = ap(nc.dram_tensor("e16_in", [16, 128], F32, kind="ExternalInput"))
    b8_in = ap(nc.dram_tensor("b8_in", [128, 128], F32, kind="ExternalInput"))
    l8_in = ap(nc.dram_tensor("l8_in", [128, 128], F32, kind="ExternalInput"))
    sbase_in = ap(nc.dram_tensor("sbase_in", [128, 1], F32, kind="ExternalInput"))
    zpr_in = ap(nc.dram_tensor("zpr_in", [1, 128], BF16, kind="ExternalInput"))
    onesr_in = ap(nc.dram_tensor("onesr_in", [1, 128], BF16, kind="ExternalInput"))
    ident16_in = ap(nc.dram_tensor("ident16_in", [16, 16], BF16, kind="ExternalInput"))
    zr512_in = ap(nc.dram_tensor("zr512_in", [1, 512], BF16, kind="ExternalInput"))

    out_d = ap(nc.dram_tensor("out", [TPC, DM], F32, kind="ExternalOutput"))

    xnsp = ap(nc.dram_tensor("xnsp", [S, DM], BF16, kind="Internal"))
    wkb_d = ap(nc.dram_tensor("wkb_d", [DM, DM], BF16, kind="Internal"))
    wvb_d = ap(nc.dram_tensor("wvb_d", [DM, DM], BF16, kind="Internal"))
    gdram = ap(nc.dram_tensor("gdram", [S, DM], BF16, kind="Internal"))
    wtt_d = ap(nc.dram_tensor("wtt_d", [16, DM], BF16, kind="Internal"))
    ch_d = ap(nc.dram_tensor("ch_d", [16, 1], F32, kind="Internal"))
    impb_d = ap(nc.dram_tensor("impb_d", [16, S], F32, kind="Internal"))
    candv_d = ap(nc.dram_tensor("candv_d", [16, 8 * J], F32, kind="Internal"))
    mbias_d = ap(nc.dram_tensor("mbias_d", [H, KP], BF16, kind="Internal"))
    wqb_d = ap(nc.dram_tensor("wqb_d", [DM, DM], BF16, kind="Internal"))
    ksel_d = ap(nc.dram_tensor("ksel_d", [H, 4, 128, 128], BF16, kind="Internal"))
    vsf_d = ap(nc.dram_tensor("vsf_d", [H, 4, 128, 128], BF16, kind="Internal"))
    attout_d = ap(nc.dram_tensor("attout_d", [DM, TPC], F32, kind="Internal"))
    x2_d = ap(nc.dram_tensor("x2_d", [TPC, DM], F32, kind="Internal"))
    xn2_d = ap(nc.dram_tensor("xn2_d", [TPC, DM], BF16, kind="Internal"))

    dbg = {}
    if debug:
        def dd(nm, shape, dt=F32):
            dbg[nm] = ap(nc.dram_tensor(nm, shape, dt, kind="ExternalOutput"))
        dd("d_mean", [128, NT]); dd("d_rstd", [128, NT]); dd("d_xbar", [1, DM])
        dd("d_qbar", [1, DM]); dd("d_wtt", [16, DM], BF16); dd("d_ch", [1, 16])
        dd("d_imp", [16, S]); dd("d_tau", [128, 1]); dd("d_valid", [128, J])
        dd("d_s1v", [128, J]); dd("d_s1g", [128, J], U32)
        dd("d_dacc", [128, NC * TPC])
        dd("d_attout", [DM, TPC]); dd("d_x2", [TPC, DM])
        dd("d_xn2", [TPC, DM], BF16); dd("d_ksel", [H, 4, 128, 128], BF16)
        dd("d_vsf", [H, 4, 128, 128], BF16); dd("d_qfm", [128, NC * TPC], BF16)

    sems = {}
    semnames = (["T", "V", "A", "G", "CST", "CST2", "CH", "WT0", "WT1", "CAND", "MB",
                 "KS", "XOF", "XN2F", "WOS", "IML", "WTLD", "DBG2", "QR0", "QR1",
                 "D0", "D1", "D2", "D3", "SP0", "SP1", "SP2", "SP3",
                 "W0", "W1", "WS0", "WS1",
                 "TK0", "TK1", "TK2", "TK3", "IM0", "IM1", "KV0", "KV1",
                 "G0", "G1", "G2", "G3", "S0", "S1", "S2", "S3",
                 "Q0", "Q1", "KH0", "KH1", "VH0", "VH1", "MH0", "MH1",
                 "WO0", "WO1", "AC0", "AC1", "XR0", "XR1", "DBG",
                 "N20", "N21", "X20", "X21", "FG", "FU", "FD",
                 "XF0", "XF1", "O0", "O1", "FG0", "FG1", "FU0", "FU1", "PV0", "PV1"])
    for sn in semnames:
        sems[sn] = E(nc.semaphore(sn))
    b = Bld(sems)

    AR = E(nc.sbuf_tensor("AR", [128, ARCOLS], F32))

    def vf(off, p, n):
        assert off % 4 == 0
        return AR[0:p, off // 4: off // 4 + n]

    def vb(off, p, n):
        assert off % 4 == 0 and n % 2 == 0
        return AR[0:p, off // 4: off // 4 + n // 2].bitcast(BF16)

    def vi(off, p, n):
        return AR[0:p, off // 4: off // 4 + n].bitcast(I32)

    sb = lambda name, shape, dt: E(nc.sbuf_tensor(name, shape, dt))
    stat6 = [sb(f"stat6_{i}", [128, 24], F32) for i in range(4)]
    mvt = [sb(f"mv{i}", [128, 2], F32) for i in range(4)]
    scr1 = vb(R1 + 57344, 128, 2048)
    mean_sb = sb("mean_sb", [128, NT], F32)
    rstd_sb = sb("rstd_sb", [128, NT], F32)
    negms_sb = sb("negms_sb", [128, NT], F32)
    std_sb = [sb(f"std_sb{i}", [128, 1], F32) for i in range(4)]
    ones_sb = sb("ones_sb", [128, 1], BF16)
    ident_sb = sb("ident_sb", [128, 128], BF16)
    identf_sb = sb("identf_sb", [128, 128], F32)
    e16_sb = sb("e16_sb", [16, 128], F32)
    b8_sb = sb("b8_sb", [128, 128], F32)
    l8_sb = sb("l8_sb", [128, 128], F32)
    sbase_sb = sb("sbase_sb", [128, 1], F32)
    zpr_sb = sb("zpr_sb", [1, 128], BF16)
    onesr_sb = sb("onesr_sb", [1, 128], BF16)
    ident16_sb = sb("ident16_sb", [16, 16], BF16)
    zr_sb = sb("zr_sb", [1, 512], BF16)
    bkc_sb = sb("bkc_sb", [128, NC], F32)
    bvc_sb = sb("bvc_sb", [128, NC], F32)
    bqc_sb = sb("bqc_sb", [128, NC], F32)
    boc_sb = sb("boc_sb", [128, NC], F32)
    sg_sb = sb("sg_sb", [128, NJC], F32)
    su_sb = sb("su_sb", [128, NJC], F32)
    sd_sb = sb("sd_sb", [128, NC], F32)
    xbcol = sb("xbcol", [128, NC], BF16)
    qcol = sb("qcol", [128, NC], BF16)
    wtil = sb("wtil", [128, NC * 16], BF16)
    wtt16 = sb("wtt16", [16, DM], BF16)
    chrow = sb("chrow", [1, 16], F32)
    ch16 = sb("ch16", [16, 1], F32)
    impg_s = [sb(f"impg_s{i}", [16, 512], F32) for i in range(2)]
    imp_hs = sb("imp_hs", [128, 1024], F32)
    s1v = sb("s1v", [128, J], F32)
    s1i = sb("s1i", [128, J], U32)
    s1g = sb("s1g", [128, J], U32)
    m8t = sb("m8t", [128, 8], F32)
    cand = sb("cand", [16, 8 * J], F32)
    m8b = sb("m8b", [16, 8], F32)
    tau16 = sb("tau16", [16, 1], F32)
    tau128 = sb("tau128", [128, 1], F32)
    strict = sb("strict", [128, J], F32)
    tiem = sb("tiem", [128, J], F32)
    scnt = sb("scnt", [128, 1], F32)
    tcnt = sb("tcnt", [128, 1], F32)
    c1h = sb("c1h", [128, 1], F32)
    quota = sb("quota", [128, 1], F32)
    tieoff = sb("tieoff", [128, 1], F32)
    tiescan = sb("tiescan", [128, J], F32)
    zcol = sb("zcol", [128, J], F32)
    tierank = sb("tierank", [128, J], F32)
    tieok = sb("tieok", [128, J], F32)
    valid = sb("valid", [128, J], F32)
    mb128 = sb("mb128", [128, J], BF16)
    smax = sb("smax", [128, 1], F32)
    nsm = sb("nsm", [128, 1], F32)
    ssum = sb("ssum", [128, 1], F32)
    srec = sb("srec", [128, 1], F32)
    xsum_hi = sb("xsum_hi", [1, TPC], BF16)
    csts = sb("csts", [128, 3], F32)
    xsum_lo = sb("xsum_lo", [1, TPC], BF16)

    xt = [vf(R1 + i * 8192, 128, 2048) for i in range(4)]
    xnt = [vb(R4 + i * 4096, 128, 2048) for i in range(4)]
    wrow = [vb(R2 + i * 4096, 128, 2048) for i in range(2)]
    wkt_row = [vb(R2 + 8192 + i * 4096, 128, 2048) for i in range(4)]
    qrow = [vb(R2 + 24576 + i * 4096, 128, 2048) for i in range(2)]
    xbar_sb = vf(R1 + 0, 1, 2048)
    xbar16 = vb(R1 + 8192, 1, 2048)
    qbar_sb = vf(R1 + 16384, 1, 2048)
    qbar16 = vb(R1 + 24576, 1, 2048)
    prodr = vf(R1 + 32768, 1, 2048)
    bkr_row = vf(R1 + 40960, 1, 2048)
    bqr_row = vf(R1 + 49152, 1, 2048)
    bq16row = vb(R1 + 28672, 1, 2048)
    ones512 = vb(R2 + 32768, 1, 512)
    wttrow = [vb(R1 + 57344, 1, 2048), vb(R1 + 61440, 1, 2048)]
    fmx = [vb(R3 + i * 16384, 128, 8192) for i in range(2)]
    wkv = [vb(R2 + i * 4096, 128, 2048) for i in range(2)]
    gt = [vb(R2 + 8192 + i * 4096, 128, 2048) for i in range(4)]
    fmc = [vb(R2 + 24576 + i * 4096, 128, 2048) for i in range(2)]
    kstg = vb(R2 + 32768, 128, 2048)
    xnofm = vb(R3 + 0, 128, 16384)
    wqcol = [vb(R2 + i * 4096, 128, 2048) for i in range(2)]
    qfm = vb(R4 + 0, 128, 16384)
    outfm = vb(R3 + 0, 128, 16384)
    kselb = [vb(R2 + 8192 + i * 1024, 128, 512) for i in range(2)]
    vkmb = [vb(R2 + 16384 + i * 1024, 128, 512) for i in range(2)]
    mbrow = [vb(R2 + 24576 + i * 1024, 1, 512) for i in range(2)]
    wsm = vb(R2 + 28672, 128, 512)
    wTt = vb(R2 + 30720, 128, 512)
    ssub = vf(R2 + 32768, 128, 512)
    sexp = vb(R2 + 34816, 128, 512)
    wocol = [vb(R2 + i * 4096, 128, 2048) for i in range(2)]
    attf = vf(R2 + 8192, 128, 1024)
    attcols = [vf(R2 + 12288 + i * 8192, 128, 2048) for i in range(2)]
    x2t = [vf(R1 + i * 8192, 128, 2048) for i in range(2)]
    xn2t = [vb(R4 + i * 4096, 128, 2048) for i in range(2)]
    dacc = vf(R1 + 0, 128, 16384)
    xn2fm = vb(R4 + 0, 128, 16384)
    wdbb = [vb(R2 + i * 4096, 128, 2048) for i in range(8)]
    wgbb = [vb(R2 + (8 + i) * 4096, 128, 2048) for i in range(2)]
    wubb = [vb(R2 + 10 * 4096, 128, 2048), vb(R3 + 28672, 128, 2048)]
    hpr = [vb(R3 + i * 2048, 128, 1024) for i in range(8)]
    gsil = vb(R3 + 16384, 128, 1024)
    dscs = vb(R3 + 18432, 128, 1024)
    wdi = vi(R3 + 20480, 128, 2048)
    xsum_f = vf(R3 + 0, 1, 1024)
    xsum_t = vf(R3 + 4096, 1, 1024)
    x2fin = [vf(R4 + i * 8192, 128, 2048) for i in range(2)]

    pbk = [E(nc.psum_tensor(f"pbk{i}", [128, 512], F32)) for i in range(8)]

    def pbf(i):
        return pbk[i][:]

    def pbb(i, c0=0, n=512):
        assert c0 % 2 == 0 and n % 2 == 0
        return pbk[i][:, c0 // 2: (c0 + n) // 2].bitcast(BF16)

    for dst, src in [(ones_sb[:], ones_col), (ident_sb[:], ident_in),
                     (identf_sb[:], identf_in), (e16_sb[:], e16_in),
                     (b8_sb[:], b8_in), (l8_sb[:], l8_in),
                     (sbase_sb[:], sbase_in), (zpr_sb[:], zpr_in),
                     (onesr_sb[:], onesr_in), (ident16_sb[:], ident16_in),
                     (zr_sb[:], zr512_in),
                     (bkc_sb[:], bk_c), (bvc_sb[:], bv_c),
                     (bqc_sb[:], bq_c), (boc_sb[:], bo_c),
                     (sg_sb[:], sg_c), (su_sb[:], su_c), (sd_sb[:], sd_c)]:
        b.dma("sync", dst, src, "CST")
    LDC = b.v["CST"]
    vz = b.ve(lambda e: e.memset(zcol[:], 0.0))
    for q in range(4):
        b.ve(lambda e, q=q: e.tensor_copy(ones512[0:1, q * 128:(q + 1) * 128],
                                          onesr_sb[:]), waits=[("CST", LDC)] if q == 0 else ())
    vz = b.ve(lambda e: e.memset(csts[:, 0:1], EPS))
    vz = b.ve(lambda e: e.memset(csts[:, 1:2], 1.0 / S))
    VCST = b.ve(lambda e: e.memset(csts[:, 2:3], SCALE))

    sgk = [0] * (3 * NC)
    for which, (srcs, wdst) in enumerate(
            [([wq_t[dc].rearrange("m p q -> p m q") for dc in range(NC)], wqb_d),
             ([wk_t[dc] for dc in range(NC)], wkb_d),
             ([wv_t[dc] for dc in range(NC)], wvb_d)]):
        for dc in range(NC):
            idx = which * NC + dc
            i = idx % 2
            w = [] if idx < 2 else [(f"WS{i}", sgk[idx - 2])]
            lgv = b.dma("gpsimd", wrow[i], srcs[dc], f"W{i}", waits=w)
            sgk[idx] = b._add("gpsimd",
                              (lambda e, i=i, dc=dc, wdst=wdst: e.dma_start(
                                  out=wdst[dc * 128:(dc + 1) * 128, :], in_=wrow[i])),
                              [(f"W{i}", lgv)], f"WS{i}", 16)
    WQS = (sgk[NC - 2], sgk[NC - 1])
    WKS = (sgk[2 * NC - 2], sgk[2 * NC - 1])
    WSTOT = (b.v["WS0"], b.v["WS1"])

    a_xn = [0] * NT
    sg_sp = [0] * NT
    t_xb = [0] * NT

    def emit_spill(k):
        i = k % 4
        sg_sp[k] = b._add("scalar" if k % 2 == 0 else "sync",
                          (lambda e, i=i, k=k: e.dma_start(
                              out=xnsp[k * 128:(k + 1) * 128, :], in_=xnt[i])),
                          [("V", a_xn[k])], f"SP{i}", 16)

    for k in range(NT):
        i = k % 4
        w = [("V", a_xn[k - 4])] if k >= 4 else []
        ldv = b.dma("sync" if k % 2 == 0 else "scalar",
                    xt[i], x_in[k * 128:(k + 1) * 128, :], f"D{i}", waits=w)
        if k >= 4:
            emit_spill(k - 4)
        if k % 3 != 2:
            asum = b.a(lambda e, i=i: e.activation(scr1, xt[i], AF.Copy,
                                                   bias=0.0, scale=1.0,
                                                   accum_out=mvt[i][:, 0:1]),
                       waits=[(f"D{i}", ldv)])
            asq = b.a(lambda e, i=i: e.activation(scr1, xt[i], AF.Square,
                                                  bias=0.0, scale=1.0,
                                                  accum_out=mvt[i][:, 1:2]))
            vm = b.ve(lambda e, i=i, k=k: e.tensor_scalar(
                mean_sb[:, k:k + 1], mvt[i][:, 0:1], 1.0 / DM, None,
                ALU.mult, ALU.bypass), waits=[("A", asum)])
            vv_ = b.ve(lambda e, i=i, k=k: e.scalar_tensor_tensor(
                std_sb[i][:], mean_sb[:, k:k + 1], -1.0, mean_sb[:, k:k + 1],
                ALU.mult, ALU.mult))
            vv_ = b.ve(lambda e, i=i: e.scalar_tensor_tensor(
                std_sb[i][:], mvt[i][:, 1:2], 1.0 / DM, std_sb[i][:],
                ALU.mult, ALU.add), waits=[("A", asq)])
            av = b.a(lambda e, i=i: e.activation(std_sb[i][:], std_sb[i][:], AF.Sqrt,
                                                 bias=csts[:, 0:1], scale=1.0),
                     waits=[("V", vv_), ("V", VCST)])
        else:
            for q in range(4):
                b.ve(lambda e, i=i, q=q: e.bn_stats(
                    stat6[i][:, q * 6:(q + 1) * 6], xt[i][:, q * 512:(q + 1) * 512]),
                    waits=[(f"D{i}", ldv)] if q == 0 else ())
            b.ve(lambda e, i=i: e.bn_aggr(mvt[i][:], stat6[i][:]))
            vm = b.ve(lambda e, i=i, k=k: e.tensor_copy(mean_sb[:, k:k + 1],
                                                        mvt[i][:, 0:1]))
            av = b.a(lambda e, i=i: e.activation(std_sb[i][:], mvt[i][:, 1:2], AF.Sqrt,
                                                 bias=csts[:, 0:1], scale=1.0),
                     waits=[("V", vm), ("V", VCST)])
        b.ve(lambda e, i=i, k=k: e.reciprocal(rstd_sb[:, k:k + 1], std_sb[i][:]),
             waits=[("A", av)])
        vn = b.ve(lambda e, i=i, k=k: e.scalar_tensor_tensor(
            negms_sb[:, k:k + 1], mean_sb[:, k:k + 1], -1.0, rstd_sb[:, k:k + 1],
            ALU.mult, ALU.mult))
        wn = []
        if k >= 4:
            wn += [(f"SP{i}", sg_sp[k - 4]), ("T", t_xb[k - 4])]
        axn = b.ve(lambda e, i=i, k=k: e.tensor_scalar(
            xnt[i], xt[i], rstd_sb[:, k:k + 1], negms_sb[:, k:k + 1],
            ALU.mult, ALU.add), waits=wn)
        a_xn[k] = axn
        for q in range(4):
            t_xb[k] = b.t(lambda e, i=i, q=q, k=k: e.matmul(
                pbk[q][0:1, :], ones_sb[:], xnt[i][:, q * 512:(q + 1) * 512],
                start=(k == 0), stop=(k == NT - 1)),
                waits=([("V", axn)] + ([("CST", LDC)] if k == 0 else [])) if q == 0 else ())
    for k in range(NT - 4, NT):
        emit_spill(k)
    T_P1 = b.v["T"]

    av = 0
    for q in range(4):
        av = b.ve(lambda e, q=q: e.tensor_scalar(
            xbar_sb[:, q * 512:(q + 1) * 512], pbk[q][0:1, :], 1.0 / S, None,
            ALU.mult, ALU.bypass), waits=[("T", T_P1)] if q == 0 else ())
    A_XB = av
    vx16 = b.ve(lambda e: e.tensor_copy(xbar16, xbar_sb))
    t_tr = 0
    for c in range(NC):
        t_tr = b.t(lambda e, c=c: e.matmul(pbk[4][:, c:c + 1],
                                           xbar16[:, c * 128:(c + 1) * 128],
                                           onesr_sb[:, 0:1], start=True, stop=True),
                   waits=[("V", vx16)] if c == 0 else ())
    vxb = b.ve(lambda e: e.tensor_copy(xbcol[:], pbk[4][:, 0:NC]),
               waits=[("T", t_tr)])

    if debug:
        b.dma("sync", dbg["d_mean"], mean_sb[:], "DBG", waits=[("V", b.v["V"])])
        b.dma("sync", dbg["d_rstd"], rstd_sb[:], "DBG", waits=[("V", b.v["V"])])
        b.dma("sync", dbg["d_xbar"], xbar_sb, "DBG", waits=[("V", av)])

    ldbq = b.dma("sync", bqr_row, bq_r, "CST2")
    ldbk = b.dma("sync", bkr_row, bk_r, "CST2")
    VBQ16 = b.ve(lambda e: e.tensor_copy(bq16row, bqr_row), waits=[("CST2", 32)])
    lg_q = [0] * NC
    t_q = [0] * NC
    for dc in range(NC):
        i = dc % 2
        w = [("T", t_q[dc - 2])] if dc >= 2 else [("WS0", WQS[0]), ("WS1", WQS[1])]
        lg_q[dc] = b.dma("sync", qrow[i], wqb_d[dc * 128:(dc + 1) * 128, :],
                         f"QR{i}", waits=w)
        for q in range(4):
            t_q[dc] = b.t(lambda e, i=i, q=q, dc=dc: e.matmul(
                pbk[q][0:1, :], xbcol[:, dc:dc + 1], qrow[i][:, q * 512:(q + 1) * 512],
                start=(dc == 0), stop=(dc == NC - 1)),
                waits=[(f"QR{i}", lg_q[dc]), ("V", vxb)] if q == 0 else ())
    aq = 0
    for q in range(4):
        aq = b.a(lambda e, q=q: e.activation(
            qbar_sb[:, q * 512:(q + 1) * 512], pbk[q][0:1, :], AF.Copy,
            bias=0.0, scale=1.0), waits=[("T", t_q[NC - 1])] if q == 0 else ())
    vqb = b.ve(lambda e: e.tensor_tensor(qbar_sb, qbar_sb, bqr_row, ALU.add),
               waits=[("A", aq), ("CST2", 32)])
    vq16 = b.ve(lambda e: e.tensor_copy(qbar16, qbar_sb))
    t_tr = 0
    for c in range(NC):
        t_tr = b.t(lambda e, c=c: e.matmul(pbk[4][:, 32 + c:33 + c],
                                           qbar16[:, c * 128:(c + 1) * 128],
                                           onesr_sb[:, 0:1], start=True, stop=True),
                   waits=[("V", vq16), ("V", vxb)] if c == 0 else ())
    vqc = b.ve(lambda e: e.tensor_copy(qcol[:], pbk[4][:, 32:32 + NC]),
               waits=[("T", t_tr)])
    vpr = b.ve(lambda e: e.tensor_tensor(prodr, qbar_sb, bkr_row, ALU.mult))
    vch = b.ve(lambda e: e.tensor_reduce(
        chrow[:].rearrange("a (h q) -> a h q", q=1),
        prodr.rearrange("a (h q) -> a h q", q=128), AX.X, ALU.add))
    chst = b.dma("sync", ch_d, chrow[:], "CH", waits=[("V", vch)])
    chld = b.dma("sync", ch16[:], ch_d, "CH", waits=[("CH", chst)])

    if debug:
        b.dma("sync", dbg["d_qbar"], qbar_sb, "DBG", waits=[("V", vqb)])
        b.dma("sync", dbg["d_ch"], chrow[:], "DBG", waits=[("V", vch)])


    a_wt = [0] * NC
    so_wt = [0] * NC
    t_wt = [0] * NC
    for jc in range(NC):
        i = jc % 4
        w = [("WS0", WKS[0]), ("WS1", WKS[1])]
        if jc >= 4:
            w.append(("T", t_wt[jc - 4]))
        lav = b.dmat("scalar",
                     wkt_row[i], wkb_d[:, jc * 128:(jc + 1) * 128],
                     f"TK{i}", waits=w)
        for q in range(4):
            wmm = [(f"TK{i}", lav), ("V", vqc)] if q == 0 else ()
            if jc > 1 and q == 0:
                wmm = list(wmm) + [(f"WT{jc % 2}", so_wt[jc - 2])]
            t_wt[jc] = b.t(lambda e, i=i, q=q, jc=jc: e.matmul(
                pbk[q][0:1, :], qcol[:, jc:jc + 1], wkt_row[i][:, q * 512:(q + 1) * 512],
                start=True, stop=True), waits=wmm)
        for q in range(4):
            a_wt[jc] = b.a(lambda e, q=q, jc=jc: e.activation(
                wttrow[jc % 2][:, q * 512:(q + 1) * 512], pbk[q][0:1, :],
                AF.Copy, bias=0.0, scale=1.0),
                waits=[("T", t_wt[jc])] if q == 0 else ())
        so_wt[jc] = b._add("sync",
                           (lambda e, jc=jc: e.dma_start(
                               out=wtt_d[jc:jc + 1, :], in_=wttrow[jc % 2])),
                           [("A", a_wt[jc])], f"WT{jc % 2}", 16)
    wtld = b.dma("sync", wtt16[:], wtt_d, "WTLD",
                 waits=[("WT0", so_wt[NC - 2]), ("WT1", so_wt[NC - 1])])
    t_tr = 0
    for dc in range(NC):
        t_tr = b.t(lambda e, dc=dc: e.matmul(
            pbk[5][:, dc * 16:(dc + 1) * 16],
            wtt16[:, dc * 128:(dc + 1) * 128], ident16_sb[:],
            start=True, stop=True),
            waits=[("WTLD", wtld)] if dc == 0 else ())
    vwt = b.ve(lambda e: e.tensor_copy(wtil[:], pbk[5][:, 0:NC * 16]),
               waits=[("T", t_tr)])
    if debug:
        b.dma("sync", dbg["d_wtt"], wtt16[:], "DBG", waits=[("WTLD", wtld)])

    b.bar()

    t_imp = [0] * NC
    SPTOT = tuple(b.v[f"SP{j}"] for j in range(4))
    for dc in range(NC):
        i = dc % 2
        w = [(f"SP{j}", SPTOT[j]) for j in range(4)]
        if dc >= 2:
            w.append(("T", t_imp[dc - 2]))
        else:
            w.append(("T", t_wt[NC - 1]))
        lav = b.dmat("scalar",
                     fmx[i], xnsp[:, dc * 128:(dc + 1) * 128],
                     f"TK{i}", waits=w)
        for mg in range(16):
            bank, pos = mg % 8, 32 * (mg // 8)
            t_imp[dc] = b.t(lambda e, i=i, dc=dc, mg=mg, bank=bank, pos=pos: e.matmul(
                pbk[bank][pos:pos + 16, :], wtil[:, dc * 16:(dc + 1) * 16],
                fmx[i][:, mg * 512:(mg + 1) * 512],
                start=(dc == 0), stop=(dc == NC - 1),
                tile_position=(0, pos), skip_group_check=True),
                waits=[(f"TK{i}", lav)] if mg == 0 else ())
    so_imp = [0, 0]
    a_imp = 0
    for mg in range(16):
        bank, pos = mg % 8, 32 * (mg // 8)
        i = mg % 2
        w = [("T", t_imp[NC - 1]), ("CH", chld)] if mg == 0 else []
        if mg >= 2:
            w.append((f"IM{mg % 2}", so_imp[mg % 2]))
        a_imp = b.ve(lambda e, mg=mg, bank=bank, pos=pos, i=i: e.tensor_scalar(
            impg_s[i][:], pbk[bank][pos:pos + 16, :], ch16[:], None,
            ALU.add, ALU.bypass), waits=w)
        so_imp[mg % 2] = b._add("sync",
                                (lambda e, mg=mg, i=i: e.dma_start(
                                    out=impb_d[:, mg * 512:(mg + 1) * 512],
                                    in_=impg_s[i][:])),
                                [("V", a_imp)], f"IM{mg % 2}", 16)
    A_IMP = a_imp
    IMTOT = (b.v["IM0"], b.v["IM1"])
    if debug:
        b.dma("sync", dbg["d_imp"], impb_d, "DBG",
              waits=[("IM0", IMTOT[0]), ("IM1", IMTOT[1])])
    impld = b.dma("sync", imp_hs[:],
                  impb_d.rearrange("h (s f) -> h s f", s=8), "IML",
                  waits=[("IM0", IMTOT[0]), ("IM1", IMTOT[1])])

    PRED_VS1G = b.v["V"] + 32
    so_g = [0] * J
    for j in range(J):
        i = j % 4
        w = [("V", PRED_VS1G)] + [(f"SP{j}", SPTOT[j]) for j in range(4)]
        if j >= 4:
            w.append((f"S{i}", so_g[j - 4]))
        lgv = b.gather(gt[i], xnsp, s1g[:, j:j + 1], f"G{i}", waits=w)
        so_g[j] = b._add("sync",
                         (lambda e, j=j, i=i: e.dma_start(
                             out=gdram[j * 128:(j + 1) * 128, :], in_=gt[i])),
                         [(f"G{i}", lgv)], f"S{i}", 16)
    STOT = [b.v[f"S{i}"] for i in range(4)]
    W_GD = [(f"S{i}", STOT[i]) for i in range(4)]

    la_xo = [0] * NC
    for dc in range(NC):
        la_xo[dc] = b.dmat("scalar", vb(R3 + dc * 2048, 128, 1024),
                           xnsp[0:TPC, dc * 128:(dc + 1) * 128], "XOF",
                           waits=[("T", t_imp[NC - 1])] +
                                 [(f"SP{j}", SPTOT[j]) for j in range(4)]
                           if dc == 0 else ())
    LA_XO = b.v["XOF"]
    lg_wq = [0] * NC
    t_qp = [0] * NC
    a_qp = [0] * NC
    for m in range(NC):
        i = m % 2
        w = [("T", t_qp[m - 2])] if m >= 2 else [("T", t_imp[NC - 1])]
        lg_wq[m] = b.dma("gpsimd", wqcol[i].rearrange("p (c q) -> p c q", q=128),
                         wqb_d[:, m * 128:(m + 1) * 128].rearrange(
                             "(c p) q -> p c q", p=128), f"Q{i}", waits=w)
        for th in range(2):
            for dc in range(NC):
                t_qp[m] = b.t(lambda e, i=i, m=m, th=th, dc=dc: e.matmul(
                    pbk[4 + th][:, :], wqcol[i][:, dc * 128:(dc + 1) * 128],
                    vb(R3 + dc * 2048 + th * 1024, 128, 512),
                    start=(dc == 0), stop=False),
                    waits=([(f"Q{i}", lg_wq[m]), ("XOF", LA_XO)] +
                           ([("A", a_qp[m - 1])] if m >= 1 else [("V", A_IMP)]))
                    if (th == 0 and dc == 0) else ())
            t_qp[m] = b.t(lambda e, m=m, th=th: e.matmul(
                pbk[4 + th][:, :], bq16row[0:1, m * 128:(m + 1) * 128],
                ones512[0:1, :], start=False, stop=True),
                waits=[("V", VBQ16)] if (m == 0 and th == 0) else ())
        for th in range(2):
            a_qp[m] = b.a(lambda e, m=m, th=th: e.activation(
                qfm[:, m * 1024 + th * 512: m * 1024 + (th + 1) * 512],
                pbk[4 + th][:, :], AF.Copy, bias=0.0, scale=1.0),
                waits=[("T", t_qp[m])] if th == 0 else ())
    A_QP = a_qp[NC - 1]

    vv = 0
    for r in range(8):
        vv = b.ve(lambda e: e.max(m8t[:], imp_hs[:]),
                  waits=[("IML", impld)] if r == 0 else ())
        vv = b.ve(lambda e, r=r: e.max_index(s1i[:, 8 * r:8 * r + 8], m8t[:], imp_hs[:]))
        vv = b.ve(lambda e, r=r: e.tensor_copy(s1v[:, 8 * r:8 * r + 8], m8t[:]))
        if r < 7:
            vv = b.ve(lambda e: e.match_replace(imp_hs[:], m8t[:], imp_hs[:], NEG))
    vs1g = b.ve(lambda e: e.tensor_scalar(s1g[:], s1i[:], sbase_sb[:], None,
                                          ALU.add, ALU.bypass))
    assert vs1g == PRED_VS1G, (vs1g, PRED_VS1G)
    cvst = b.dma("scalar", candv_d.rearrange("h (s j) -> h s j", s=8), s1v[:],
                 "CAND", waits=[("V", b.v["V"])])
    cvld = b.dma("scalar", cand[:], candv_d, "CAND", waits=[("CAND", cvst)])
    for r in range(32):
        vv = b.ve(lambda e: e.max(m8b[:], cand[:]),
                  waits=[("CAND", cvld)] if r == 0 else ())
        if r < 31:
            vv = b.ve(lambda e: e.match_replace(cand[:], m8b[:], cand[:], NEG))
    vtau = b.ve(lambda e: e.tensor_copy(tau16[:], m8b[:, 7:8]))
    ttau = b.t(lambda e: e.matmul(pbk[0][:, 0:1], e16_sb[:], tau16[:],
                                  start=True, stop=True),
               waits=[("V", vtau)])
    vtb = b.ve(lambda e: e.tensor_copy(tau128[:], pbk[0][:, 0:1]), waits=[("T", ttau)])
    vst = b.ve(lambda e: e.tensor_scalar(strict[:], s1v[:], tau128[:], None,
                                         ALU.is_gt, ALU.add, accum_out=scnt[:]))
    vti = b.ve(lambda e: e.tensor_scalar(tiem[:], s1v[:], tau128[:], None,
                                         ALU.is_equal, ALU.add, accum_out=tcnt[:]))
    tc1 = b.t(lambda e: e.matmul(pbk[1][:, 0:1], b8_sb[:], scnt[:],
                                 start=True, stop=True), waits=[("V", vst)])
    tto = b.t(lambda e: e.matmul(pbk[2][:, 0:1], l8_sb[:], tcnt[:],
                                 start=True, stop=True), waits=[("V", vti)])
    vc1 = b.ve(lambda e: e.tensor_copy(c1h[:], pbk[1][:, 0:1]), waits=[("T", tc1)])
    vto = b.ve(lambda e: e.tensor_copy(tieoff[:], pbk[2][:, 0:1]), waits=[("T", tto)])
    vqo = b.ve(lambda e: e.tensor_scalar(quota[:], c1h[:], -1.0, 256.0,
                                         ALU.mult, ALU.add))
    vsc = b.ve(lambda e: e.tensor_tensor_scan(tiescan[:], tiem[:], zcol[:], 0.0,
                                              ALU.add, ALU.add))
    vtr = b.ve(lambda e: e.tensor_tensor(tierank[:], tiescan[:], tiem[:], ALU.subtract))
    vtr = b.ve(lambda e: e.tensor_scalar(tierank[:], tierank[:], tieoff[:], None,
                                         ALU.add, ALU.bypass))
    vtk = b.ve(lambda e: e.tensor_scalar(tieok[:], tierank[:], quota[:], None,
                                         ALU.is_lt, ALU.bypass))
    vva = b.ve(lambda e: e.tensor_tensor(valid[:], tiem[:], tieok[:], ALU.mult))
    vva = b.ve(lambda e: e.tensor_tensor(valid[:], valid[:], strict[:], ALU.add))
    vmb = b.ve(lambda e: e.tensor_scalar(mb128[:], valid[:], -1.0, 30000.0,
                                         ALU.add, ALU.mult))
    mbst = 0
    for qq in range(4):
        mbst = b.dma("scalar",
                     mbias_d[:, qq * 128:(qq + 1) * 128].rearrange(
                         "h (s j) -> h s j", s=8),
                     mb128[:, qq * 16:(qq + 1) * 16],
                     "MB", waits=[("V", vmb)] if qq == 0 else ())
    if debug:
        b.dma("sync", dbg["d_tau"], tau128[:], "DBG", waits=[("V", vtb)])
        b.dma("sync", dbg["d_valid"], valid[:], "DBG", waits=[("V", vva)])
        b.dma("sync", dbg["d_s1v"], s1v[:], "DBG", waits=[("V", b.v["V"])])
        b.dma("sync", dbg["d_s1g"], s1g[:], "DBG", waits=[("V", vs1g)])


    def hview(i, h):
        v3 = fmc[i].rearrange("p (j q) -> p j q", q=128)
        v3 = v3[:, :, 8 * h:8 * h + 8]
        return v3.rearrange("p j s -> p s j")

    t_ks = 0
    so_ks = 0
    a_ev = 0
    tks_par = [0, 0]
    wkv2 = [vb(R2 + 36864 + i * 4096, 128, 2048) for i in range(2)]
    for qq in range(4):
        for bk_ in range(8):
            wz = []
            if bk_ == 0:
                wz = [("CST", LDC)]
                if qq > 0:
                    wz.append(("V", a_ev))
                else:
                    wz.append(("T", t_imp[NC - 1]))
                    wz.append(("V", vto))
            if bk_ == 4 and qq == 0:
                wz = [("A", A_QP)]
            t_ks = b.t(lambda e, bk_=bk_: e.matmul(
                pbk[bk_][:, :], zr_sb[:, 0:128], zr_sb[:],
                start=True, stop=True), waits=wz)
        la_f = [0] * NC
        ld_w = [0] * NC
        ld_w2 = [0] * NC
        for dc in range(NC):
            i = dc % 2
            w = list(W_GD)
            if qq > 0 or dc >= 2:
                w.append(("T", tks_par[i]))
            else:
                w.append(("T", t_qp[NC - 1]))
            la_f[dc] = b.dmat("scalar",
                              fmc[i],
                              gdram[qq * 2048:(qq + 1) * 2048,
                                    dc * 128:(dc + 1) * 128],
                              f"TK{i}", waits=w)
            ld_w[dc] = b.dma("sync", wkv[i],
                             wkb_d[dc * 128:(dc + 1) * 128, :],
                             f"KV{i}", waits=[w[-1]])
            ld_w2[dc] = b.dma("gpsimd", wkv2[i],
                              wvb_d[dc * 128:(dc + 1) * 128, :],
                              f"PV{i}", waits=[w[-1]])
            for h in range(H):
                wmm = [(f"TK{i}", la_f[dc]), (f"KV{i}", ld_w[dc]),
                       (f"PV{i}", ld_w2[dc])] if h == 0 else ()
                if h == 0 and dc == 0 and qq > 0:
                    wmm = list(wmm) + [("KS", so_ks)]
                t_ks = b.t(lambda e, i=i, dc=dc, h=h: e.matmul(
                    pbk[h // 4][:, (h % 4) * 128:(h % 4 + 1) * 128],
                    wkv[i][:, h * 128:(h + 1) * 128], hview(i, h),
                    start=False, stop=False,
                    skip_group_check=True), waits=wmm)
                t_ks = b.t(lambda e, i=i, dc=dc, h=h: e.matmul(
                    pbk[4 + h // 4][:, (h % 4) * 128:(h % 4 + 1) * 128],
                    wkv2[i][:, h * 128:(h + 1) * 128], hview(i, h),
                    start=False, stop=False,
                    skip_group_check=True))
            tks_par[i] = t_ks
        for which, (dstd, bias_sb, boff) in enumerate(
                [(ksel_d, bkc_sb, 0), (vsf_d, bvc_sb, 4)]):
            for h in range(H):
                wev = []
                if h == 0:
                    if which == 0:
                        wev.append(("T", t_ks))
                    if so_ks:
                        wev.append(("KS", so_ks))
                a_ev = b.ve(lambda e, h=h, bias_sb=bias_sb, boff=boff: e.tensor_scalar(
                    kstg[:, h * 128:(h + 1) * 128],
                    pbk[boff + h // 4][:, (h % 4) * 128:(h % 4 + 1) * 128],
                    bias_sb[:, h:h + 1], None, ALU.add, ALU.bypass), waits=wev)
            so_ks = b._add("sync",
                           (lambda e, qq=qq, dstd=dstd: e.dma_start(
                               out=dstd[:, qq].rearrange("h p q -> p h q"),
                               in_=kstg.rearrange("p (h q) -> p h q", q=128))),
                           [("V", a_ev)], "KS", 16)
    SO_KS = so_ks

    b.bar()


    last_sc_read = [0, 0]
    last_aex = [0]
    last_av_read = [0, 0]
    last_scT = [0, 0]
    last_avT = [0, 0]
    ld_k = [0, 0]
    ld_v = [0, 0]
    ld_m = [0, 0]
    v_wT = 0
    MBTOT = b.v["MB"]

    def em_loads(h):
        hb = h % 2
        wk_ = [("KS", SO_KS)]
        if last_scT[hb]:
            wk_.append(("T", last_scT[hb]))
        ld_k[hb] = b.dma("sync", kselb[hb].rearrange("p (c q) -> p c q", q=128),
                         ksel_d[h].rearrange("qq p q -> p qq q"),
                         f"KH{hb}", waits=wk_)
        wv_ = [("KS", SO_KS)]
        if last_avT[hb]:
            wv_.append(("T", last_avT[hb]))
        ld_v[hb] = b.dmat("scalar", vkmb[hb],
                          vsf_d[h].rearrange("qq p q -> (qq p) q"),
                          f"VH{hb}", waits=wv_)
        ld_m[hb] = b.dma("gpsimd", mbrow[hb], mbias_d[h:h + 1, :],
                         f"MH{hb}", waits=[("MB", MBTOT)] + (
                             [("T", last_scT[hb])] if last_scT[hb] else []))

    def em_scores(it):
        h, tt = it // NTO, it % NTO
        hb = h % 2
        sbank = 4 + it % 2
        wsc = [("A", A_QP), (f"KH{hb}", ld_k[hb]), (f"MH{hb}", ld_m[hb])]
        if last_sc_read[it % 2]:
            wsc.append(("A", last_sc_read[it % 2]))
        tsc = b.t(lambda e, h=h, tt=tt, sbank=sbank, hb=hb: e.matmul(
            pbk[sbank][:, :], qfm[:, h * 1024 + tt * 128: h * 1024 + (tt + 1) * 128],
            kselb[hb][:, :], start=True, stop=False), waits=wsc)
        tsc = b.t(lambda e, hb=hb, sbank=sbank: e.matmul(
            pbk[sbank][:, :], onesr_sb[:], mbrow[hb], start=False, stop=True))
        last_scT[hb] = tsc
        return tsc

    def em_rest(it, tsc):
        nonlocal_v = None
        h, tt = it // NTO, it % NTO
        hb = h % 2
        sbank = 4 + it % 2
        abank = 6 + it % 2
        vmx = b.ve(lambda e, sbank=sbank: e.reduce_max(smax[:], pbk[sbank][:, :], AX.X),
                   waits=[("T", tsc)])
        vsb = b.ve(lambda e: e.tensor_scalar(
            nsm[:], smax[:], -SCALE, None, ALU.mult, ALU.bypass),
            waits=[("A", last_aex[0])] if last_aex[0] else ())
        aex = b.a(lambda e, sbank=sbank: e.activation(
            sexp, pbk[sbank][:, :], AF.Exp, bias=nsm[:],
            scale=csts[:, 2:3], accum_out=ssum[:]), waits=[("V", vsb)])
        last_sc_read[it % 2] = aex
        last_aex[0] = aex
        vrc = b.ve(lambda e: e.reciprocal(srec[:], ssum[:]), waits=[("A", aex)])
        vws = b.ve(lambda e: e.tensor_scalar(wsm, sexp, srec[:], None,
                                             ALU.mult, ALU.bypass))
        global_v_wT = v_wT
        t_wt4 = 0
        for kc in range(4):
            wv_ = [("V", vws)] if kc == 0 else ()
            if kc == 0 and global_v_wT:
                wv_ = list(wv_) + [("V", global_v_wT)]
            t_wt4 = b.t(lambda e, kc=kc: e.transpose(
                pbb(3, kc * 128, 128), wsm[:, kc * 128:(kc + 1) * 128],
                ident_sb[:]), waits=wv_)
        nv_wT = b.ve(lambda e: e.tensor_copy(wTt, pbb(3, 0, 512)),
                     waits=[("T", t_wt4)])
        wav = [("V", nv_wT), (f"VH{hb}", ld_v[hb])]
        if last_av_read[it % 2]:
            wav.append(("A", last_av_read[it % 2]))
        tav = 0
        for kc in range(4):
            tav = b.t(lambda e, hb=hb, kc=kc, abank=abank: e.matmul(
                pbk[abank][:, 0:128], vkmb[hb][:, kc * 128:(kc + 1) * 128],
                wTt[:, kc * 128:(kc + 1) * 128],
                start=(kc == 0), stop=(kc == 3)),
                waits=wav if kc == 0 else ())
        last_avT[hb] = tav
        aov = b.a(lambda e, h=h, tt=tt, abank=abank: e.activation(
            outfm[:, h * 1024 + tt * 128: h * 1024 + (tt + 1) * 128],
            pbk[abank][:, 0:128], AF.Copy, bias=0.0, scale=1.0),
            waits=[("T", tav)])
        last_av_read[it % 2] = aov
        return nv_wT

    NIT = H * NTO
    pend = None
    for it in range(NIT + 1):
        if it < NIT:
            if it % NTO == 0:
                em_loads(it // NTO)
            tsc = em_scores(it)
        if pend is not None:
            v_wT = em_rest(it - 1, pend)
        pend = tsc if it < NIT else None
        A_ATT = b.v["A"]

    b.bar()

    lg_wo = [0] * NC
    t_wo = [0] * NC
    v_wo = [0] * NC
    so_wo = [0] * NC
    for m in range(NC):
        i = m % 2
        w = [("T", t_wo[m - 2])] if m >= 2 else []
        lg_wo[m] = b.dma("gpsimd", wocol[i].rearrange("p (c q) -> p c q", q=128),
                         wo_t[:, m].rearrange("c p q -> p c q"), f"WO{i}", waits=w)
        for th in range(2):
            for dc in range(NC):
                t_wo[m] = b.t(lambda e, i=i, th=th, dc=dc: e.matmul(
                    pbk[4 + th][:, :], wocol[i][:, dc * 128:(dc + 1) * 128],
                    outfm[:, dc * 1024 + th * 512: dc * 1024 + (th + 1) * 512],
                    start=(dc == 0), stop=(dc == NC - 1)),
                    waits=([(f"WO{i}", lg_wo[m]), ("A", A_ATT)] +
                           ([("V", v_wo[m - 1])] if m >= 1 else []))
                    if (th == 0 and dc == 0) else ())
        for th in range(2):
            wv_ = [("T", t_wo[m])] if th == 0 else []
            if th == 0 and m >= 1:
                wv_.append(("WOS", so_wo[m - 1]))
            v_wo[m] = b.ve(lambda e, th=th: e.tensor_copy(
                attf[:, th * 512:(th + 1) * 512], pbk[4 + th][:, :]), waits=wv_)
        a_bo = b.ve(lambda e, m=m: e.tensor_scalar(
            attf, attf, boc_sb[:, m:m + 1], None, ALU.add, ALU.bypass))
        so_wo[m] = b._add("sync",
                          (lambda e, m=m: e.dma_start(
                              out=attout_d[m * 128:(m + 1) * 128, :], in_=attf)),
                          [("V", a_bo)], "WOS", 16)
    SO_WO = so_wo[NC - 1]

    b.bar()

    DBG_QFM = 0
    if debug:
        b.dma("sync", dbg["d_attout"], attout_d, "DBG", waits=[("WOS", SO_WO)])
        b.dma("sync", dbg["d_ksel"], ksel_d, "DBG", waits=[("KS", SO_KS)])
        b.dma("sync", dbg["d_vsf"], vsf_d, "DBG", waits=[("KS", SO_KS)])
        DBG_QFM = b.dma("sync", dbg["d_qfm"], qfm[:], "DBG", waits=[("A", A_QP)])

    ld_ac = [0, 0]
    sg_x2 = [0] * NTO
    sg_n2 = [0] * NTO
    a_n2 = [0] * NTO
    last_at = [0, 0]
    vadd_bk = [0, 0]
    for kt in range(NTO):
        i = kt % 2
        w = [("WOS", SO_WO)]
        if kt >= 2:
            w.append(("T", last_at[i]))
        ld_ac[i] = b.dma("sync", attcols[i].rearrange("p (m q) -> p m q", q=128),
                         attout_d[:, kt * 128:(kt + 1) * 128].rearrange(
                             "(m p) q -> p m q", p=128), f"AC{i}", waits=w)
        ldx = b.dma("sync", x2t[i], x_in[kt * 128:(kt + 1) * 128, :], f"XR{i}",
                    waits=[(f"X2{i}", sg_x2[kt - 2])] if kt >= 2 else
                          [("WOS", SO_WO)])
        vadd = 0
        for m in range(NC):
            bkx = 2 + m % 2
            t_at = b.t(lambda e, i=i, m=m, bkx=bkx: e.transpose(
                pbk[bkx][:, 0:128], attcols[i][:, m * 128:(m + 1) * 128], identf_sb[:]),
                waits=[(f"AC{i}", ld_ac[i]),
                       ("V", vadd_bk[m % 2] if vadd_bk[m % 2] else b.v["V"])])
            last_at[i] = t_at
            vadd = b.ve(lambda e, i=i, m=m, bkx=bkx: e.tensor_tensor(
                x2t[i][:, m * 128:(m + 1) * 128], x2t[i][:, m * 128:(m + 1) * 128],
                pbk[bkx][:, 0:128], ALU.add),
                waits=[("T", t_at), (f"XR{i}", ldx)])
            vadd_bk[m % 2] = vadd
        for q in range(4):
            b.ve(lambda e, i=i, q=q: e.bn_stats(stat6[i][:, q * 6:(q + 1) * 6],
                                                x2t[i][:, q * 512:(q + 1) * 512]))
        b.ve(lambda e, i=i: e.bn_aggr(mvt[i][:], stat6[i][:]))
        vm = b.ve(lambda e, i=i: e.tensor_copy(mean_sb[:, 0:1], mvt[i][:, 0:1]))
        av2 = b.a(lambda e, i=i: e.activation(std_sb[i][:], mvt[i][:, 1:2], AF.Sqrt,
                                              bias=csts[:, 0:1], scale=1.0),
                  waits=[("V", vm), ("A", b.v["A"])])
        vr2 = b.ve(lambda e, i=i: e.reciprocal(rstd_sb[:, 0:1], std_sb[i][:]),
                   waits=[("A", av2)])
        vn2 = b.ve(lambda e, i=i: e.scalar_tensor_tensor(
            negms_sb[:, 0:1], mvt[i][:, 0:1], -1.0, rstd_sb[:, 0:1],
            ALU.mult, ALU.mult))
        wn = []
        if kt >= 2:
            wn.append((f"N2{i}", sg_n2[kt - 2]))
        if debug and kt < 2:
            wn.append(("DBG", DBG_QFM))
        a_n2[kt] = b.ve(lambda e, i=i: e.tensor_scalar(
            xn2t[i], x2t[i], rstd_sb[:, 0:1], negms_sb[:, 0:1], ALU.mult, ALU.add),
            waits=wn)
        sg_n2[kt] = b._add("sync",
                           (lambda e, i=i, kt=kt: e.dma_start(
                               out=xn2_d[kt * 128:(kt + 1) * 128, :], in_=xn2t[i])),
                           [("V", a_n2[kt])], f"N2{i}", 16)
        sg_x2[kt] = b._add("scalar",
                           (lambda e, i=i, kt=kt: e.dma_start(
                               out=x2_d[kt * 128:(kt + 1) * 128, :], in_=x2t[i])),
                           [("V", a_n2[kt])], f"X2{i}", 16)
    N2TOT = (b.v["N20"], b.v["N21"])
    X2TOT = (b.v["X20"], b.v["X21"])

    b.bar()

    if debug:
        b.dma("gpsimd", dbg["d_x2"], x2_d, "DBG2",
              waits=[("X20", X2TOT[0]), ("X21", X2TOT[1])])
        b.dma("gpsimd", dbg["d_xn2"], xn2_d, "DBG2",
              waits=[("N20", N2TOT[0]), ("N21", N2TOT[1])])

    la_n2 = [0] * NC
    for dc in range(NC):
        la_n2[dc] = b.dmat("scalar", vb(R4 + dc * 2048, 128, 1024),
                           xn2_d[:, dc * 128:(dc + 1) * 128], "XN2F",
                           waits=[("N20", N2TOT[0]), ("N21", N2TOT[1])])
    LA_N2 = b.v["XN2F"]
    t_xs = 0
    for th in range(2):
        for dc in range(NC):
            t_xs = b.t(lambda e, th=th, dc=dc: e.matmul(
                pbk[th][0:1, :], ones_sb[:],
                vb(R4 + dc * 2048 + th * 1024, 128, 512),
                start=(dc == 0), stop=(dc == NC - 1)),
                waits=[("XN2F", LA_N2)] if (th == 0 and dc == 0) else ())
    a_xs = 0
    for th in range(2):
        a_xs = b.a(lambda e, th=th: e.activation(
            xsum_f[:, th * 512:(th + 1) * 512], pbk[th][0:1, :], AF.Copy,
            bias=0.0, scale=1.0), waits=[("T", t_xs)] if th == 0 else ())
    vhi = b.ve(lambda e: e.tensor_copy(xsum_hi[:], xsum_f), waits=[("A", a_xs)])
    vt_ = b.ve(lambda e: e.tensor_tensor(xsum_t, xsum_f, xsum_hi[:], ALU.subtract))
    vlo = b.ve(lambda e: e.tensor_copy(xsum_lo[:], xsum_t))
    V_XS = vlo

    groups = [list(range(g * 8, min(NJC, g * 8 + 8))) for g in range(6)]
    lg_g = 0
    lg_u = 0
    last_gd = 0
    g_d = [0] * 9
    a_sil = 0
    v_hp = [0] * 8
    last_vhp = 0
    t_dn = 0
    v_fl = 0
    t_gu = [0] * NJC
    for gi, grp in enumerate(groups):
        for jj, jc in enumerate(grp):
            jb = jc % 2
            wg_w = [("T", t_gu[jc - 2])] if jc >= 2 else []
            lg_g = b.dma("gpsimd", wgbb[jb], wg_b[jc], f"FG{jb}", waits=wg_w)
            lg_u = b.dma("gpsimd", wubb[jb], wu_b[jc], f"FU{jb}", waits=wg_w)
            ld_w = [("G", last_gd)] if last_gd else []
            ldd = b.dma("sync", wdi, wd_b[jc], "FD", waits=ld_w)
            g_d[jj] = b.g(lambda e, jj=jj, jc=jc: e.tensor_scalar(
                wdbb[jj], wdi, -128.0, su_sb[:, jc:jc + 1], ALU.add, ALU.mult),
                waits=[("FD", ldd)] + ([("T", t_dn)] if gi > 0 else []))
            last_gd = g_d[jj]
            for th in range(2):
                wmm = [(f"FG{jb}", lg_g), ("XN2F", LA_N2), ("V", V_XS)]
                if a_sil:
                    wmm.append(("A", a_sil))
                for dc in range(NC):
                    t_gu[jc] = b.t(lambda e, th=th, dc=dc, jb=jb: e.matmul(
                        pbk[2 + th][:, :], wgbb[jb][:, dc * 128:(dc + 1) * 128],
                        vb(R4 + dc * 2048 + th * 1024, 128, 512),
                        start=(dc == 0), stop=False),
                        waits=wmm if dc == 0 else ())
                t_gu[jc] = b.t(lambda e, th=th: e.matmul(
                    pbk[2 + th][:, :], zpr_sb[:], xsum_hi[:, th * 512:(th + 1) * 512],
                    start=False, stop=False))
                t_gu[jc] = b.t(lambda e, th=th: e.matmul(
                    pbk[2 + th][:, :], zpr_sb[:], xsum_lo[:, th * 512:(th + 1) * 512],
                    start=False, stop=True))
            for th in range(2):
                wsil = []
                if th == 0:
                    wsil = [("T", t_gu[jc])]
                    if last_vhp:
                        wsil.append(("V", last_vhp))
                a_sil = b.a(lambda e, th=th, jc=jc: e.activation(
                    gsil[:, th * 512:(th + 1) * 512], pbk[2 + th][:, :], AF.Silu,
                    bias=0.0, scale=sg_sb[:, jc:jc + 1]), waits=wsil)
            for th in range(2):
                wmm = ([(f"FU{jb}", lg_u)] + ([("V", last_vhp)] if last_vhp else [])) if th == 0 else ()
                for dc in range(NC):
                    t_gu[jc] = b.t(lambda e, th=th, dc=dc, jb=jb: e.matmul(
                        pbk[4 + th][:, :], wubb[jb][:, dc * 128:(dc + 1) * 128],
                        vb(R4 + dc * 2048 + th * 1024, 128, 512),
                        start=(dc == 0), stop=False),
                        waits=wmm if dc == 0 else ())
                t_gu[jc] = b.t(lambda e, th=th: e.matmul(
                    pbk[4 + th][:, :], zpr_sb[:], xsum_hi[:, th * 512:(th + 1) * 512],
                    start=False, stop=False))
                t_gu[jc] = b.t(lambda e, th=th: e.matmul(
                    pbk[4 + th][:, :], zpr_sb[:], xsum_lo[:, th * 512:(th + 1) * 512],
                    start=False, stop=True))
            for th in range(2):
                v_hp[jj] = b.ve(lambda e, jj=jj, th=th: e.tensor_tensor(
                    hpr[jj][:, th * 512:(th + 1) * 512], pbk[4 + th][:, :],
                    gsil[:, th * 512:(th + 1) * 512], ALU.mult),
                    waits=[("T", t_gu[jc]), ("A", a_sil)] if th == 0 else ())
            last_vhp = v_hp[jj]
        for m in range(NC):
            for jj in range(len(grp)):
                wmm = []
                if jj == 0:
                    wmm = [("V", v_hp[len(grp) - 1]), ("G", g_d[len(grp) - 1])]
                    if v_fl:
                        wmm.append(("V", v_fl))
                for th in range(2):
                    t_dn = b.t(lambda e, m=m, th=th, jj=jj,
                               ng=len(grp): e.matmul(
                        pbk[6 + th][:, :], wdbb[jj][:, m * 128:(m + 1) * 128],
                        hpr[jj][:, th * 512:(th + 1) * 512],
                        start=(jj == 0), stop=(jj == ng - 1)),
                        waits=wmm if th == 0 else ())
                    wmm = ()
            for th in range(2):
                if gi == 0:
                    v_fl = b.ve(lambda e, m=m, th=th: e.tensor_scalar(
                        dacc[:, m * 1024 + th * 512: m * 1024 + (th + 1) * 512],
                        pbk[6 + th][:, :], sd_sb[:, m:m + 1], None,
                        ALU.mult, ALU.bypass), waits=[("T", t_dn)] if th == 0 else ())
                else:
                    v_fl = b.ve(lambda e, m=m, th=th: e.scalar_tensor_tensor(
                        dacc[:, m * 1024 + th * 512: m * 1024 + (th + 1) * 512],
                        pbk[6 + th][:, :], sd_sb[:, m:m + 1],
                        dacc[:, m * 1024 + th * 512: m * 1024 + (th + 1) * 512],
                        ALU.mult, ALU.add), waits=[("T", t_dn)] if th == 0 else ())
    V_FFN = b.v["V"]
    if debug:
        b.dma("sync", dbg["d_dacc"], dacc, "DBG", waits=[("V", V_FFN)])

    b.bar()

    so_out = [0] * NTO
    ldx2 = [0, 0]
    vadd_bk = [0, 0]
    for kt in range(NTO):
        i = kt % 2
        w = [("V", V_FFN), ("X20", X2TOT[0]), ("X21", X2TOT[1])]
        if kt >= 2:
            w.append((f"O{i}", so_out[kt - 2]))
        ldx2[i] = b.dma("sync", x2fin[i], x2_d[kt * 128:(kt + 1) * 128, :],
                        f"XF{i}", waits=w)
        vadd = 0
        for m in range(NC):
            bkx = 2 + m % 2
            wt = [("V", vadd_bk[m % 2])] if vadd_bk[m % 2] else []
            if kt == 0 and m < 2:
                wt.append(("V", V_FFN))
            t_ds = b.t(lambda e, m=m, kt=kt, bkx=bkx: e.transpose(
                pbk[bkx][:, 0:128],
                dacc[:, m * 1024 + kt * 128: m * 1024 + (kt + 1) * 128],
                identf_sb[:]), waits=wt)
            t_ds_bk = t_ds
            vadd = b.ve(lambda e, i=i, m=m, bkx=bkx: e.tensor_tensor(
                x2fin[i][:, m * 128:(m + 1) * 128],
                x2fin[i][:, m * 128:(m + 1) * 128],
                pbk[bkx][:, 0:128], ALU.add), waits=[("T", t_ds), (f"XF{i}", ldx2[i])])
            vadd_bk[m % 2] = vadd
        so_out[kt] = b._add("sync",
                            (lambda e, i=i, kt=kt: e.dma_start(
                                out=out_d[kt * 128:(kt + 1) * 128, :], in_=x2fin[i])),
                            [("V", vadd)], f"O{i}", 16)
    fin = [("O0", b.v["O0"]), ("O1", b.v["O1"])]
    if debug:
        fin.append(("DBG", b.v["DBG"]))
        fin.append(("DBG2", b.v["DBG2"]))
    b.ops["sync"].append((None, tuple(fin), None, 0))

    with nc.Block() as block:
        b.emit(block)
    ctx.close()
    return nc, dbg


def prep_inputs(x, Wq, bq, Wk, bk, Wv, bv, Wo, bo, wq_gate, s_gate, wq_up, s_up,
                wq_down, s_down, core):
    xr = np.concatenate([x[core * TPC:], x[:core * TPC]], axis=0)
    xr = np.ascontiguousarray(xr)

    def tile4(W):
        return np.ascontiguousarray(
            W.reshape(NC, 128, NC, 128).transpose(0, 2, 1, 3))

    def rowch(W):
        return np.ascontiguousarray(W.reshape(NC, 128, DM))

    def colsplit(v):
        return np.ascontiguousarray(v.reshape(NC, 128).T)

    wgb = np.ascontiguousarray(
        wq_gate.reshape(NJC, 128, NC, 128).transpose(0, 3, 2, 1)).reshape(NJC, 128, DM)
    wub = np.ascontiguousarray(
        wq_up.reshape(NJC, 128, NC, 128).transpose(0, 3, 2, 1)).reshape(NJC, 128, DM)
    wdb = np.ascontiguousarray(
        wq_down.reshape(DM, NJC, 128).transpose(1, 2, 0)).reshape(NJC, 128, DM)

    e16 = np.zeros((16, 128), np.float32)
    for jj in range(16):
        e16[jj, jj * 8:(jj + 1) * 8] = 1.0
    b8 = np.zeros((128, 128), np.float32)
    l8 = np.zeros((128, 128), np.float32)
    for p in range(128):
        h0 = p // 8
        b8[h0 * 8:(h0 + 1) * 8, p] = 1.0
        l8[h0 * 8: p, p] = 1.0
    sbase = (np.arange(128) % 8 * 1024).astype(np.float32).reshape(128, 1)

    return dict(
        x=xr,
        wq_t=tile4(Wq), wk_t=rowch(Wk), wv_t=rowch(Wv), wo_t=tile4(Wo),
        bq_c=colsplit(bq), bk_c=colsplit(bk), bv_c=colsplit(bv), bo_c=colsplit(bo),
        bk_r=np.ascontiguousarray(bk.reshape(1, DM)),
        bq_r=np.ascontiguousarray(bq.reshape(1, DM)),
        wg_b=wgb, wu_b=wub, wd_b=wdb,
        sg_c=np.ascontiguousarray(s_gate.reshape(NJC, 128).T),
        su_c=np.ascontiguousarray(s_up.reshape(NJC, 128).T),
        sd_c=np.ascontiguousarray(s_down.reshape(NC, 128).T),
        ones_col=np.ones((128, 1), ml_dtypes.bfloat16),
        ident_in=np.eye(128, dtype=ml_dtypes.bfloat16),
        identf_in=np.eye(128, dtype=np.float32),
        e16_in=e16, b8_in=b8, l8_in=l8, sbase_in=sbase,
        zpr_in=np.full((1, 128), -128.0, ml_dtypes.bfloat16),
        onesr_in=np.ones((1, 128), ml_dtypes.bfloat16),
        ident16_in=np.eye(16, dtype=ml_dtypes.bfloat16),
        zr512_in=np.zeros((1, 512), ml_dtypes.bfloat16),
    )


def kernel(L0_attn_hidden_states, L0_attn_Wq, L0_attn_bq, L0_attn_Wk, L0_attn_bk,
           L0_attn_Wv, L0_attn_bv, L0_attn_Wo, L0_attn_bo, L0_ffn_wq_gate,
           L0_ffn_s_gate, L0_ffn_wq_up, L0_ffn_s_up, L0_ffn_wq_down, L0_ffn_s_down):
    from concourse.bass_utils import run_bass_kernel_spmd
    x = np.ascontiguousarray(np.asarray(L0_attn_hidden_states, np.float32).reshape(S, DM))
    args = (x, np.asarray(L0_attn_Wq, np.float32), np.asarray(L0_attn_bq, np.float32),
            np.asarray(L0_attn_Wk, np.float32), np.asarray(L0_attn_bk, np.float32),
            np.asarray(L0_attn_Wv, np.float32), np.asarray(L0_attn_bv, np.float32),
            np.asarray(L0_attn_Wo, np.float32), np.asarray(L0_attn_bo, np.float32),
            np.asarray(L0_ffn_wq_gate, np.int32), np.asarray(L0_ffn_s_gate, np.float32),
            np.asarray(L0_ffn_wq_up, np.int32), np.asarray(L0_ffn_s_up, np.float32),
            np.asarray(L0_ffn_wq_down, np.int32), np.asarray(L0_ffn_s_down, np.float32))
    nc, _ = build(debug=0)
    in_maps = [prep_inputs(*args, core=c) for c in range(NCORES)]
    res = run_bass_kernel_spmd(nc, in_maps, core_ids=list(range(NCORES)))
    out = np.concatenate([np.asarray(res.results[c]["out"]) for c in range(NCORES)],
                         axis=0)
    return out.reshape(1, S, DM)


if __name__ == "__main__":
    nc, _ = build(debug=1)
    n = sum(len(v) for v in nc.m.functions[0].basicblocks[0].instructions) if False else 0
    print("build ok")
